# version 26
# baseline (speedup 1.0000x reference)
import numpy as np
import ml_dtypes
import concourse.bass as bass
import concourse.mybir as mybir
import concourse.bass_utils as _bass_utils
from concourse.bass_utils import run_bass_kernel_spmd

_orig_run_command = _bass_utils.run_command


def _run_command_ldwopt(argv, **kwargs):
    argv = ["--enable-ldw-opt=true" if a == "--enable-ldw-opt=false" else a
            for a in argv]
    return _orig_run_command(argv, **kwargs)


_bass_utils.run_command = _run_command_ldwopt

F32 = mybir.dt.float32
BF16 = mybir.dt.bfloat16
AF = mybir.ActivationFunctionType
ALU = mybir.AluOpType
AX = mybir.AxisListType

RND = 12582912.0
D = 1024
DC = 8
I = 2816
IC = 22
EPS = 1e-5

ENGINES = ("sync", "scalar", "vector", "tensor", "gpsimd")

OFF_IDENT = 2048
OFF_BQ = 2304
OFF_BK = 2336
OFF_BVROW = 2368
OFF_BOROW = 4416
OFF_ONESB = 6464
OFF_ONESF = 6720
OFF_CLIP = 7232
OFF_LNST = 7264
OFF_ROW128 = 7328
OFF_RSG = 7840
OFF_RSU = 7936
OFF_CSG = 8032
OFF_CSU = 8128
OFF_SLOTS = 8224
OFF_XT = 10752
OFF_OUTT = 19456
OFF_XNBF = 27648
OFF_TEVAC = 31744
OFF_WO = 35840
OFF_W = 52224
FO_WG = 0
FO_WU = 45056
FO_WD = 90112
FO_XQ = 135168
FO_HEV = 151552
FO_SILU = 153600
FO_HQ = 157696
FO_HQF = 168960
FO_P0STG = 151552
FO_P0WI = 162816
AO_WQ = 0
AO_WK = 16384
AO_WV = 32768
AO_XN2 = 49152
AO_KEV = 65536
AO_VEV = 67584
AO_WST = 71680
FL_QTB = 0
FL_KTB = 8192
FL_VTB = 24576
FL_U = 40960
FL_P = 106496
FL_T1 = 139264
FL_T2 = 143360
FL_R = 147456
FL_PT = 151552
FL_AO = 155648
FL_AOT = 157696


class Builder:
    def __init__(self, nc):
        self.nc = nc
        self.q = {e: [] for e in ENGINES}
        self.sems = {}
        self.cnt = {}
        self.hist = {}
        self._slots = {}
        self._slot_next = OFF_SLOTS

    def sem(self, name):
        if name not in self.sems:
            self.sems[name] = self.nc.alloc_semaphore(name)
            self.cnt[name] = 0
            self.hist[name] = []
        return self.sems[name]

    def c(self, name):
        self.sem(name)
        return self.cnt[name]

    def n(self, name):
        self.sem(name)
        return len(self.hist[name])

    def h(self, name, k):
        self.sem(name)
        if k <= 0:
            return 0
        hist = self.hist[name]
        assert k <= len(hist), (name, k, len(hist))
        return hist[k - 1]

    SERIAL = {"vector": "ser_v", "scalar": "ser_s", "gpsimd": "ser_g"}

    def emit(self, eng, fn, waits=(), inc=None, amt=1):
        waits = tuple(w for w in waits if w is not None)
        for s, _ in waits:
            self.sem(s)
        if inc is not None and amt > 0:
            self.sem(inc)
            self.cnt[inc] += amt
            self.hist[inc].append(self.cnt[inc])
        else:
            inc = None
        ser = self.SERIAL.get(eng)
        prev_tok = None
        my_inc = inc
        if ser is not None:
            prev_tok = getattr(self, "_tok", {}).get(eng)
            if not hasattr(self, "_tok"):
                self._tok = {}
            if inc is not None:
                self._tok[eng] = (inc, self.cnt[inc])
            else:
                self.sem(ser)
                self.cnt[ser] += 1
                self.hist[ser].append(self.cnt[ser])
                my_inc = ser
                amt = 1
                self._tok[eng] = (ser, self.cnt[ser])
        sems = self.sems

        def run(engine, fn=fn, waits=waits, inc=my_inc, amt=amt,
                prev_tok=prev_tok):
            if prev_tok is not None and prev_tok[1] > 0:
                engine.wait_ge(sems[prev_tok[0]], prev_tok[1])
            for s, v in waits:
                if v > 0:
                    engine.wait_ge(sems[s], v)
            ins = fn(engine)
            if inc is not None:
                ins.then_inc(sems[inc], amt)

        self.q[eng].append(run)

    def wait_only(self, eng, waits):
        waits = tuple(w for w in waits if w is not None and w[1] > 0)
        if not waits:
            return
        for s, _ in waits:
            self.sem(s)
        sems = self.sems

        def run(engine, waits=waits):
            for s, v in waits:
                engine.wait_ge(sems[s], v)

        self.q[eng].append(run)

    def dma(self, eng, out, in_, inc, waits=(), **kw):
        self.emit(eng, lambda e, o=out, i=in_, kw=kw: e.dma_start(out=o, in_=i, **kw),
                  waits=waits, inc=inc, amt=16)

    def dma2(self, eng, out, in_, fam, idx, waits=(), **kw):
        self.dma(eng, out, in_, f"{fam}{idx % 2}", waits=waits, **kw)

    def dtot(self, fam, idx):
        name = f"{fam}{idx % 2}"
        return (name, self.c(name))

    def dboth(self, fam):
        return [(f"{fam}0", self.c(f"{fam}0")), (f"{fam}1", self.c(f"{fam}1"))]

    def slot(self, name, shape=(128, 1)):
        if name not in self._slots:
            nbytes = int(np.prod(shape[1:])) * 4
            nbytes = (nbytes + 31) // 32 * 32
            off = self._slot_next
            assert off + nbytes <= OFF_XT, (name, off)
            self._slot_next = off + nbytes
            self._slots[name] = self.nc.alloc_sbuf_tensor_at(
                "slot_" + name, list(shape), F32, offset=off)
        return self._slots[name]


def flat(handle, n, off=0):
    return bass.AP(handle, off, [[1, 1], [1, n]])


def build_program(T=4096):
    nc = bass.Bass("TRN2", target_bir_lowering=False, num_devices=8,
                   dynamic_dma_scratch_size=2048)
    nt = T // 128
    ng = T // 512
    S = 2 * T
    nkc = S // 512
    nsb = T // 256

    xin = nc.dram_tensor("xin", [T, D], F32, kind="ExternalInput")
    wqTD = nc.dram_tensor("wqT", [D, D], F32, kind="ExternalInput")
    wkTD = nc.dram_tensor("wkT", [D, D], F32, kind="ExternalInput")
    wvTD = nc.dram_tensor("wvT", [D, D], F32, kind="ExternalInput")
    woTD = nc.dram_tensor("woT", [D, D], F32, kind="ExternalInput")
    bqpD = nc.dram_tensor("bqp", [128, 8], F32, kind="ExternalInput")
    bkpD = nc.dram_tensor("bkp", [128, 8], F32, kind="ExternalInput")
    bvrD = nc.dram_tensor("bvr", [1, D], F32, kind="ExternalInput")
    borD = nc.dram_tensor("bor", [1, D], F32, kind="ExternalInput")
    wgD = nc.dram_tensor("wg", [I, D], F32, kind="ExternalInput")
    wuD = nc.dram_tensor("wu", [I, D], F32, kind="ExternalInput")
    wdD = nc.dram_tensor("wd", [D, I], F32, kind="ExternalInput")
    clipD = nc.dram_tensor("clip", [1, 2], F32, kind="ExternalInput")
    identD = nc.dram_tensor("ident", [128, 128], BF16, kind="ExternalInput")
    xout = nc.dram_tensor("xout", [T, D], F32, kind="ExternalOutput")

    xnTD = nc.dram_tensor("xnT", [D, T], BF16)
    hDD = nc.dram_tensor("hd", [I, T], BF16)
    x1D = nc.dram_tensor("x1", [T, D], F32)
    x2D = nc.dram_tensor("x2", [T, D], F32)
    qtD = nc.dram_tensor("qt", [D, T], BF16)
    ktoD = nc.dram_tensor("kto", [D, T], BF16)
    ktgD = nc.dram_tensor("ktg", [2 * D, T], BF16)
    voD = nc.dram_tensor("vo", [T, D], BF16)
    vgD = nc.dram_tensor("vg", [2 * T, D], BF16)
    wgS = nc.dram_tensor("wgs", [D, I], BF16)
    wuS = nc.dram_tensor("wus", [D, I], BF16)
    wdS = nc.dram_tensor("wds", [I, D], BF16)
    scrD = nc.dram_tensor("scr", [1, 128], F32)
    ccB = [nc.dram_tensor(f"cc{j}", [1, 8], F32) for j in range(4)]

    A = nc.alloc_sbuf_tensor_at
    ident = A("ident", [128, 128], BF16, offset=OFF_IDENT)
    identF = A("identf", [128, 128], F32, offset=18944)
    bq_pp = A("bq_pp", [128, 8], F32, offset=OFF_BQ)
    bk_pp = A("bk_pp", [128, 8], F32, offset=OFF_BK)
    bv_row = A("bv_row", [1, D], BF16, offset=OFF_BVROW)
    bo_row = A("bo_row", [1, D], BF16, offset=OFF_BOROW)
    ones_b = A("ones_b", [1, 128], BF16, offset=OFF_ONESB)
    ones_f = A("ones_f", [1, 128], F32, offset=OFF_ONESF)
    clip_sb = A("clip_sb", [1, 2], F32, offset=OFF_CLIP)
    lnst = A("lnst", [128, 12], F32, offset=OFF_LNST)
    row128 = A("row128", [1, 128], F32, offset=OFF_ROW128)
    rs_g = A("rs_g", [128, IC], F32, offset=OFF_RSG)
    rs_u = A("rs_u", [128, IC], F32, offset=OFF_RSU)
    csg = A("csg", [128, IC], F32, offset=OFF_CSG)
    csu = A("csu", [128, IC], F32, offset=OFF_CSU)
    xt = [A(f"xt{k}", [128, D], F32, offset=OFF_XT + k * 4096) for k in range(2)]
    outt = [A(f"outt{k}", [128, D], F32, offset=OFF_OUTT + k * 4096) for k in range(2)]
    xnbf = [A(f"xnbf{k}", [128, D], BF16, offset=OFF_XNBF + k * 2048) for k in range(2)]
    tevac = [A(f"tevac{k}", [128, 8, 128], BF16, offset=OFF_TEVAC + k * 2048)
             for k in range(2)]
    wo_sb = A("wo_sb", [128, 8, D], BF16, offset=OFF_WO)
    wgT = A("wgT", [128, 8, I], BF16, offset=OFF_W + FO_WG)
    wuT = A("wuT", [128, 8, I], BF16, offset=OFF_W + FO_WU)
    wdT = A("wdT", [128, IC, D], BF16, offset=OFF_W + FO_WD)
    xq = [A(f"xq{k}", [128, 8, 512], BF16, offset=OFF_W + FO_XQ + k * 8192)
          for k in range(2)]
    hev = [A(f"hev{k}", [128, 512], BF16, offset=OFF_W + FO_HEV + k * 1024)
           for k in range(2)]
    silu = [A(f"silu{k}", [128, 512], F32, offset=OFF_W + FO_SILU + k * 2048)
            for k in range(2)]
    hq = [A(f"hq{k}", [128, IC, 128], BF16, offset=OFF_W + FO_HQ + k * 5632)
          for k in range(2)]
    hqfs = [A("hqf0", [128, 1408], F32, offset=OFF_W + FO_HQF),
            A("hqf1", [128, 1408], F32, offset=OFF_W + FO_XQ)]
    qscs = [A("qsc0", [128, 512], F32, offset=OFF_W + FO_HQF),
            A("qsc1", [128, 512], F32, offset=OFF_W + FO_HQF + 2048)]
    p0stg = A("p0stg", [128, I], F32, offset=OFF_W + FO_P0STG)
    awq = A("awq", [128, 8, D], BF16, offset=OFF_W + AO_WQ)
    awk = A("awk", [128, 8, D], BF16, offset=OFF_W + AO_WK)
    awv = A("awv", [128, 8, D], BF16, offset=OFF_W + AO_WV)
    xn2 = [A(f"xn2{k}", [128, 8, 512], BF16, offset=OFF_W + AO_XN2 + k * 8192)
           for k in range(2)]
    kev = [A(f"kev{k}", [128, 512], BF16, offset=OFF_W + AO_KEV + k * 1024)
           for k in range(2)]
    vev = [A(f"vev{k}", [128, D], BF16, offset=OFF_W + AO_VEV + k * 2048)
           for k in range(2)]
    wst = [A(f"wst{k}", [128, D], F32, offset=OFF_W + AO_WST + k * 4096)
           for k in range(2)]
    qtb = [A(f"qtb{k}", [128, 8, 256], BF16, offset=OFF_W + FL_QTB + k * 4096)
           for k in range(2)]
    qtb3 = A("qtb2x", [128, 8, 256], BF16, offset=OFF_W + FL_AOT + 2048)
    ktb = [A(f"ktb{k}", [128, 8, 512], BF16, offset=OFF_W + FL_KTB + k * 8192)
           for k in range(2)]
    vtb = [A(f"vtb{k}", [128, 4, D], BF16, offset=OFF_W + FL_VTB + k * 8192)
           for k in range(2)]
    Ubuf = [A(f"ubuf{k}", [128, S], F32, offset=OFF_W + FL_U + k * 32768)
            for k in range(2)]
    Pbuf = [A(f"pbuf{k}", [128, S], BF16, offset=OFF_W + FL_P + k * 16384)
            for k in range(2)]
    t1s = [A(f"t1s{k}", [128, 512], F32,
             offset=(OFF_W + FL_T1 + k * 2048 if k < 2 else OFF_W + 167936))
           for k in range(3)]
    t2s = [A(f"t2s{k}", [128, 512], F32,
             offset=(OFF_W + FL_T2 + k * 2048 if k < 2 else OFF_W + 169984))
           for k in range(3)]
    rsc = [A(f"rsc{k}", [128, 512], F32,
             offset=(OFF_W + FL_R + k * 2048 if k < 2 else OFF_W + 172032))
           for k in range(3)]
    ptsb = [A(f"ptsb{k}", [128, 8, 128], BF16, offset=OFF_W + FL_PT + k * 2048)
            for k in range(2)]
    ao_sb2 = [A(f"ao_sb{k}", [128, D], BF16,
                offset=OFF_W + FL_AOT + 6144 + k * 2048) for k in range(2)]
    aoT_sb2 = [A(f"aoT_sb{k}", [128, 8, 128], BF16,
                 offset=OFF_W + FL_AO + k * 2048) for k in range(2)]
    lrp = [A(f"lrp{k}", [128, 8], F32, offset=OFF_W + 174080 + k * 32)
           for k in range(2)]

    p0 = nc.alloc_psum_tensor("p0", [128, 512], F32)
    p1 = nc.alloc_psum_tensor("p1", [128, 512], F32)
    pacc0 = nc.alloc_psum_tensor("pacc0", [128, D], F32)
    pacc1 = nc.alloc_psum_tensor("pacc1", [128, D], F32)
    pt0 = nc.alloc_psum_tensor("pt0", [128, 8, 128], BF16)
    pt1 = nc.alloc_psum_tensor("pt1", [128, 8, 128], BF16)
    pts = [pt0, pt1]
    paccs = [pacc0, pacc1]

    B = Builder(nc)
    sl = B.slot

    def xpart_request(acc_slot, ccb, pre_waits, mm_extra, tag):
        B.emit("tensor", lambda e, acc_slot=acc_slot: e.matmul(
            p0[0:1, 0:128], acc_slot[:, 0:1], identF.ap(),
            start=True, stop=True),
            waits=list(pre_waits) + list(mm_extra) + [("p0_ones", 6)],
            inc="cc_bc")
        B.emit("vector", lambda e: e.tensor_reduce(
            sl("ccrow", (1, 8))[0:1, 0:1], p0[0:1, 0:128], AX.X, ALU.max),
            waits=[("cc_bc", B.c("cc_bc"))], inc="cc_rm")
        B.dma("sync", ccb.ap()[0:1, :], sl("ccrow", (1, 8))[0:1, :], "cc_d",
              waits=[("cc_rm", B.c("cc_rm"))])
        B.emit("gpsimd", lambda e, ccb=ccb: e.collective_compute(
            "AllReduce", ALU.max, replica_groups=[[0, 1, 2, 3, 4, 5, 6, 7]],
            ins=[flat(ccb, 8)], outs=[flat(ccb, 8)]),
            waits=[("cc_d", B.c("cc_d"))], inc="cc_cc")
        B.dma("sync", sl(tag + "_cr")[0:1, 0:1], ccb.ap()[0:1, 0:1], "cc_d",
              waits=[("cc_cc", B.c("cc_cc"))])

    def xpart_complete(out_slot, tag):
        B.emit("tensor", lambda e, tag=tag: e.matmul(
            p0[:, 0:1], ones_f[0:1, :], sl(tag + "_cr")[0:1, 0:1],
            start=True, stop=True),
            waits=[("cc_d", B.c("cc_d"))], inc="cc_bc")
        B.emit("vector", lambda e, out_slot=out_slot: e.tensor_copy(
            out_slot[:, 0:1], p0[:, 0:1]),
            waits=[("cc_bc", B.c("cc_bc"))], inc="cc_done")
        return "cc_done"

    def xpart_max_allreduce(acc_slot, ccb, out_slot, pre_waits, mm_extra, tag):
        xpart_request(acc_slot, ccb, pre_waits, mm_extra, tag)
        return xpart_complete(out_slot, tag)

    deferred_stores = []

    def emit_setup():
        B.dma("sync", ident.ap(), identD.ap(), "p0_misc")
        B.dma("sync", bq_pp.ap(), bqpD.ap(), "p0_misc")
        B.dma("sync", bk_pp.ap(), bkpD.ap(), "p0_misc")
        B.dma("sync", clip_sb.ap(), clipD.ap(), "p0_misc")
        B.emit("vector", lambda e: e.memset(ones_b.ap(), 1.0), inc="p0_ones")
        B.emit("vector", lambda e: e.memset(ones_f.ap(), 1.0), inc="p0_ones")
        B.emit("vector", lambda e: e.memset(sl("epsb")[:, 0:1], EPS), inc="p0_ones")
        B.emit("vector", lambda e: e.memset(sl("ccrow", (1, 8))[0:1, :], 0.0),
               inc="p0_ones")
        B.emit("vector", lambda e: e.memset(sl("zb")[:, 0:1], 0.0),
               inc="p0_ones")
        B.dma("sync", wst[0][0:1, :], bvrD.ap(), "p0_bv")
        B.emit("vector", lambda e: e.tensor_copy(bv_row.ap(), wst[0][0:1, :]),
               waits=[("p0_bv", 16)], inc="p0_bvc")
        B.dma("sync", wst[1][0:1, :], borD.ap(), "p0_bv",
              waits=[("p0_bvc", 1)])
        B.emit("vector", lambda e: e.tensor_copy(bo_row.ap(), wst[1][0:1, :]),
               waits=[("p0_bv", 32)], inc="p0_bvc")
        B.emit("scalar", lambda e: e.activation(
            out=identF.ap(), in_=ident.ap(), func=AF.Identity,
            scale=1.0, bias=sl("zb")[:, 0:1]),
            waits=[("p0_misc", 64), ("p0_ones", 5)], inc="p0_ones")
        B.emit("scalar", lambda e: e.activation(
            out=sl("sgtmp", (1, 2))[0:1, :], in_=clip_sb[0:1, :], func=AF.Tanh,
            scale=0.5, bias=sl("zb")[0:1, 0:1]),
            waits=[("p0_misc", 64), ("p0_ones", 5)], inc="p0_sg")
        B.emit("vector", lambda e: e.tensor_scalar(
            sl("sgrow", (1, 2))[0:1, :], sl("sgtmp", (1, 2))[0:1, :],
            0.5, 0.5, ALU.mult, ALU.add), waits=[("p0_sg", 1)], inc="p0_sg")
        B.emit("tensor", lambda e: e.matmul(
            p1[:, 0:2], ones_f[0:1, :], sl("sgrow", (1, 2))[0:1, :],
            start=True, stop=True),
            waits=[("p0_sg", 2), ("p0_ones", 2)], inc="p0_sg")
        B.emit("vector", lambda e: e.tensor_copy(
            sl("sigc", (128, 2))[:, :], p1[:, 0:2]),
            waits=[("p0_sg", 3)], inc="p0_sg")
        for j in range(DC):
            par = j % 2
            B.dma2("sync", wst[par].ap(),
                   bass.AP(woTD, j * 128 * D, [[D, 128], [1, D]]), "aw_ld", j,
                   waits=[("aw_c", B.h("aw_c", B.n("aw_c") - 1)),
                          ("p0_bvc", 2)])
            B.emit("vector", lambda e, j=j, par=par: e.tensor_copy(
                wo_sb[:, j, :], wst[par].ap()),
                waits=[B.dtot("aw_ld", j)], inc="aw_c")

        def quant_w(src, rs_tile, dstT, stash, nrt_, fold_back, ncol):
            nchk = ncol // 128
            gpt = (nchk + 7) // 8
            nbuf = 2 if ncol == D else 1
            raws = [A(f"qraw{k}_" + stash.name, [128, ncol], F32,
                      offset=OFF_W + FO_XQ + k * 4096) for k in range(nbuf)]
            stgs = [A(f"qstg{k}_" + stash.name, [128, ncol], F32,
                      offset=OFF_W + FO_P0STG + k * 4096) for k in range(nbuf)]
            wints = [A(f"qwint{k}_" + stash.name, [128, ncol], BF16,
                       offset=OFF_W + FO_P0WI + k * 2048) for k in range(nbuf)]
            n_sc = B.n("q_sc")
            n_cl = B.n("q_cl")
            n_tr = B.n("q_tr")
            n_ev = B.n("q_ev")

            def stage_a(t):
                raw = raws[t % nbuf]
                stg = stgs[t % nbuf]
                p = t % 2
                B.dma2("sync", raw.ap(),
                       bass.AP(src, t * 128 * ncol, [[ncol, 128], [1, ncol]]),
                       "q_ld", t % nbuf,
                       waits=[("q_sc", B.h("q_sc", n_sc + t + 1 - nbuf))])
                B.emit("vector", lambda e, raw=raw, p=p: e.tensor_reduce(
                    sl(f"q_rmax{p}")[:, 0:1], raw.ap(), AX.X, ALU.max,
                    apply_absolute_value=True),
                    waits=[B.dtot("q_ld", t % nbuf)], inc="q_rm")
                B.emit("vector", lambda e, p=p: e.reciprocal(
                    sl(f"q_rinv{p}")[:, 0:1], sl(f"q_rmax{p}")[:, 0:1]),
                    waits=[("q_rm", B.c("q_rm"))], inc="q_rm")
                B.emit("vector", lambda e, p=p: e.tensor_scalar(
                    sl(f"q_r127{p}")[:, 0:1], sl(f"q_rinv{p}")[:, 0:1], 127.0,
                    None, ALU.mult), waits=[("q_rm", B.c("q_rm"))], inc="q_rm")
                if rs_tile is not None:
                    B.emit("vector", lambda e, t=t, rs_tile=rs_tile, p=p:
                           e.tensor_scalar(
                               rs_tile[:, t:t + 1], sl(f"q_rmax{p}")[:, 0:1],
                               1.0 / 127.0, None, ALU.mult), inc="q_rm")
                B.emit("scalar", lambda e, raw=raw, stg=stg, ncol=ncol, p=p:
                       e.activation(
                           out=stg[:, 0:ncol], in_=raw.ap(), func=AF.Identity,
                           scale=sl(f"q_r127{p}")[:, 0:1],
                           bias=sl("zb")[:, 0:1]),
                       waits=[("q_rm", B.c("q_rm")),
                              ("q_cl", B.h("q_cl", n_cl + t + 1 - nbuf))],
                       inc="q_sc")

            def stage_b(t):
                stg = stgs[t % nbuf]
                wint = wints[t % nbuf]
                p = t % 2
                wint_wait = ("q_tr", B.h("q_tr", n_tr + (t + 1 - nbuf) * gpt))
                if fold_back:
                    B.emit("vector", lambda e, stg=stg, ncol=ncol: e.tensor_scalar(
                        stg[:, 0:ncol], stg[:, 0:ncol], RND, RND, ALU.add,
                        ALU.subtract),
                        waits=[("q_sc", B.h("q_sc", n_sc + t + 1))], inc="q_rd")
                    B.emit("vector", lambda e, wint=wint, stg=stg, ncol=ncol, p=p:
                           e.tensor_scalar(
                               wint.ap(), stg[:, 0:ncol],
                               sl(f"q_rmax{p}")[:, 0:1],
                               1.0 / 127.0, ALU.mult, ALU.mult),
                           waits=[("q_rd", B.c("q_rd")), wint_wait], inc="q_cl")
                else:
                    B.emit("vector", lambda e, wint=wint, stg=stg, ncol=ncol:
                           e.tensor_scalar(
                               wint.ap(), stg[:, 0:ncol], RND, RND, ALU.add,
                               ALU.subtract),
                           waits=[("q_sc", B.h("q_sc", n_sc + t + 1)),
                                  wint_wait], inc="q_cl")
                for gi in range(gpt):
                    j0 = gi * 8
                    jn = min(8, nchk - j0)
                    gg = t * gpt + gi
                    par = gg % 2
                    for jj in range(jn):
                        B.emit("tensor",
                               lambda e, par=par, jj=jj, j0=j0, wint=wint:
                               e.transpose(
                                   pts[par][:, jj, :],
                                   wint[:, (j0 + jj) * 128:(j0 + jj + 1) * 128],
                                   ident.ap()),
                               waits=([("q_cl", B.c("q_cl")),
                                       ("q_ev", B.h("q_ev", n_ev + gg - 1))]
                                      if jj == 0 else ()),
                               inc=("q_tr" if jj == jn - 1 else None))
                    B.emit("vector",
                           lambda e, par=par, j0=j0, jn=jn, t=t, dstT=dstT:
                           e.tensor_copy(
                               dstT[:, j0:j0 + jn, t * 128:(t + 1) * 128],
                               pts[par][:, 0:jn, :]),
                           waits=[("q_tr", B.c("q_tr"))], inc="q_ev")

            if nbuf == 2:
                stage_a(0)
                for t in range(nrt_):
                    if t + 1 < nrt_:
                        stage_a(t + 1)
                    stage_b(t)
            else:
                for t in range(nrt_):
                    stage_a(t)
                    stage_b(t)

            def stores(nchk=nchk, stash=stash, dstT=dstT, nrt_=nrt_,
                       ev=B.c("q_ev")):
                for j in range(nchk):
                    B.dma("sync", bass.AP(stash, j * 128 * nrt_ * 128,
                                          [[nrt_ * 128, 128],
                                           [1, nrt_ * 128]]),
                          dstT[:, j, :], "q_st", waits=[("q_ev", ev)])
            deferred_stores.append(stores)

        quant_w(wgD, rs_g, wgT, wgS, IC, False, D)
        quant_w(wuD, rs_u, wuT, wuS, IC, False, D)
        return B.c("q_ev"), quant_w

    class LNPass:

        def __init__(self, track_slot, act_norm=False):
            self.n_rc = B.n("l_rc"); self.n_nm = B.n("l_nm")
            self.n_tr = B.n("l_tr"); self.n_ev = B.n("l_ev")
            self.n_sq = B.n("l_sq")
            self.track = track_slot
            self.act_norm = act_norm
            if track_slot:
                B.emit("vector", lambda e, t=track_slot: e.memset(
                    sl(t)[:, 0:1], 0.0), inc="l_mx")

        def stats(self, i, src_ap, gates):
            p = i % 2
            B.emit("vector", lambda e, src_ap=src_ap, p=p: e.bn_stats(
                sl(f"lnst{p}", (128, 12))[:, 0:6], src_ap[:, 0:512]),
                waits=list(gates) + [("l_sq", B.h("l_sq", self.n_sq + i - 1))])
            B.emit("vector", lambda e, src_ap=src_ap, p=p: e.bn_stats(
                sl(f"lnst{p}", (128, 12))[:, 6:12], src_ap[:, 512:]))
            B.emit("vector", lambda e, p=p: e.bn_aggr(
                sl(f"lnmv{p}", (128, 2))[:, :], sl(f"lnst{p}", (128, 12)).ap()),
                inc="l_st")
            B.emit("scalar", lambda e, p=p: e.activation(
                out=sl(f"rstd{p}")[:, 0:1], in_=sl(f"lnmv{p}", (128, 2))[:, 1:2],
                func=AF.Sqrt, bias=sl("epsb")[:, 0:1]),
                waits=[("l_st", B.c("l_st"))], inc="l_sq")

        def norm(self, i, src_ap):
            par = i % 2
            p = i % 2
            B.emit("vector", lambda e, p=p: e.reciprocal(
                sl(f"rstd2{p}")[:, 0:1], sl(f"rstd{p}")[:, 0:1]),
                waits=[("l_sq", B.h("l_sq", self.n_sq + i + 1))], inc="l_rc")
            if self.act_norm:
                B.emit("vector", lambda e, p=p: e.tensor_scalar(
                    sl(f"lnnb{p}")[:, 0:1], sl(f"lnmv{p}", (128, 2))[:, 0:1],
                    sl(f"rstd2{p}")[:, 0:1], -1.0, ALU.mult, ALU.mult),
                    waits=[("l_rc", B.c("l_rc"))], inc="l_rc")
                B.emit("scalar", lambda e, par=par, src_ap=src_ap, p=p:
                       e.activation(
                           out=xnbf[par].ap(), in_=src_ap, func=AF.Identity,
                           scale=sl(f"rstd2{p}")[:, 0:1],
                           bias=sl(f"lnnb{p}")[:, 0:1]),
                       waits=[("l_rc", B.c("l_rc")),
                              ("l_tr", B.h("l_tr", self.n_tr + i - 1))],
                       inc="l_nm")
            else:
                B.emit("vector", lambda e, par=par, src_ap=src_ap, p=p:
                       e.tensor_scalar(
                           xnbf[par].ap(), src_ap,
                           sl(f"lnmv{p}", (128, 2))[:, 0:1],
                           sl(f"rstd2{p}")[:, 0:1], ALU.subtract, ALU.mult),
                       waits=[("l_rc", B.c("l_rc")),
                              ("l_tr", B.h("l_tr", self.n_tr + i - 1))],
                       inc="l_nm")
            if self.track:
                B.emit("vector", lambda e, par=par: e.tensor_reduce(
                    sl("tmax")[:, 0:1], xnbf[par].ap(), AX.X, ALU.max,
                    apply_absolute_value=True),
                    waits=([("l_nm", B.c("l_nm"))] if self.act_norm else ()))
                B.emit("vector", lambda e, t=self.track: e.tensor_max(
                    sl(t)[:, 0:1], sl(t)[:, 0:1], sl("tmax")[:, 0:1]),
                    inc="l_mx")
            for j in range(DC):
                B.emit("tensor", lambda e, par=par, j=j: e.transpose(
                    pts[par][:, j, :], xnbf[par][:, j * 128:(j + 1) * 128],
                    ident.ap()),
                    waits=([("l_nm", B.c("l_nm")),
                            ("l_ev", B.h("l_ev", self.n_ev + i - 1))]
                           if j == 0 else ()),
                    inc=("l_tr" if j == DC - 1 else None))
            B.emit("scalar", lambda e, par=par: e.copy(
                tevac[par].ap(), pts[par].ap()),
                waits=[("l_tr", B.c("l_tr")), B.dtot("l_xn", i)],
                inc="l_ev")
            B.dma2("sync",
                   bass.AP(xnTD, i * 128, [[T, 128], [128 * T, 8], [1, 128]]),
                   tevac[par].ap(), "l_xn", i, waits=[("l_ev", B.c("l_ev"))])

        def tile(self, i, src_ap, gates):
            self.stats(i, src_ap, gates)
            self.norm(i, src_ap)

        def norm_done(self, i):
            return ("l_nm", B.h("l_nm", self.n_nm + i + 1))

    def emit_ln_dram(src, track_slot, entry_waits=()):
        ln = LNPass(track_slot, act_norm=True)
        n_nm0 = ln.n_nm

        def load(i, w0=()):
            par = i % 2
            w = [("l_nm", B.h("l_nm", n_nm0 + i - 1))] + list(w0)
            B.dma2("sync", xt[par].ap(),
                   bass.AP(src, i * 128 * D, [[D, 128], [1, D]]), "l_ld", i,
                   waits=w)
            ln.stats(i, xt[par].ap(), [B.dtot("l_ld", i)])

        load(0, entry_waits)
        for i in range(nt):
            if i + 1 < nt:
                load(i + 1)
            ln.norm(i, xt[i % 2].ap())

    def emit_scales_x(Mslot):
        B.emit("vector", lambda e, M=Mslot: e.tensor_mul(
            sl("w_sx")[:, 0:1], M[:, 0:1], sl("sigc", (128, 2))[:, 0:1]), inc="sc")
        B.emit("vector", lambda e: e.reciprocal(
            sl("w_si")[:, 0:1], sl("w_sx")[:, 0:1]),
            waits=[("sc", B.c("sc"))], inc="sc")
        B.emit("vector", lambda e: e.tensor_scalar(
            sl("inv_sx")[:, 0:1], sl("w_si")[:, 0:1], 127.0, None, ALU.mult),
            waits=[("sc", B.c("sc"))], inc="sc")
        B.emit("vector", lambda e: e.tensor_scalar(
            sl("w_ss")[:, 0:1], sl("w_sx")[:, 0:1], 1.0 / 127.0, None, ALU.mult),
            waits=[("sc", B.c("sc"))], inc="sc")
        B.emit("vector", lambda e: e.tensor_scalar(
            csg.ap(), rs_g.ap(), sl("w_ss")[:, 0:1], None, ALU.mult),
            waits=[("sc", B.c("sc"))], inc="sc")
        B.emit("vector", lambda e: e.tensor_scalar(
            csu.ap(), rs_u.ap(), sl("w_ss")[:, 0:1], None, ALU.mult),
            waits=[("sc", B.c("sc"))], inc="sc")

    def emit_scales_h(Mslot):
        B.emit("vector", lambda e, M=Mslot: e.tensor_mul(
            sl("w_sd")[:, 0:1], M[:, 0:1], sl("sigc", (128, 2))[:, 1:2]), inc="sc")
        B.emit("vector", lambda e: e.reciprocal(
            sl("w_di")[:, 0:1], sl("w_sd")[:, 0:1]),
            waits=[("sc", B.c("sc"))], inc="sc")
        B.emit("vector", lambda e: e.tensor_scalar(
            sl("inv_sd")[:, 0:1], sl("w_di")[:, 0:1], 127.0, None, ALU.mult),
            waits=[("sc", B.c("sc"))], inc="sc")
        B.emit("vector", lambda e: e.tensor_scalar(
            sl("half_sd")[:, 0:1], sl("w_sd")[:, 0:1], 0.5 / 127.0, None, ALU.mult),
            waits=[("sc", B.c("sc"))], inc="sc")

    def emit_ffn_b(hamax_slot):
        n_q3 = B.n("b_q3"); n_mg = B.n("b_mg")
        n_mu = B.n("b_mu"); n_si = B.n("b_si"); n_h = B.n("b_h")
        n_hv = B.n("b_hv")
        xn_waits = B.dboth("l_xn")
        sc_val = B.c("sc")
        def quant_chunk(g, j):
            par = g % 2
            m = g * DC + j
            qp = m % 2
            B.emit("scalar", lambda e, par=par, j=j, qp=qp: e.activation(
                out=qscs[qp].ap(), in_=xq[par][:, j, :], func=AF.Identity,
                scale=sl("inv_sx")[:, 0:1], bias=sl("zb")[:, 0:1]),
                waits=[B.dtot("b_ld", g), ("sc", sc_val),
                       ("b_q3", B.h("b_q3", n_q3 + m - 1))],
                inc="b_q1")
            B.emit("vector", lambda e, qp=qp: e.tensor_scalar(
                qscs[qp].ap(), qscs[qp].ap(), RND, RND, ALU.add, ALU.subtract),
                waits=[("b_q1", B.c("b_q1"))], inc="b_q2")
            B.emit("vector", lambda e, par=par, j=j, qp=qp: e.tensor_scalar(
                xq[par][:, j, :], qscs[qp].ap(), -128.0, 127.0, ALU.max,
                ALU.min), waits=[("b_q2", B.c("b_q2"))], inc="b_q3")

        qsc_done = B.c("q_sc")
        def load_group(g):
            par = g % 2
            w = [("b_mu", B.h("b_mu", n_mu + (g - 1) * IC))] + xn_waits
            if g == 0:
                w = w + [("q_sc", qsc_done)]
            for hc in range(DC):
                B.dma2("sync", xq[par][:, hc, :],
                       bass.AP(xnTD, hc * 128 * T + g * 512,
                               [[T, 128], [1, 512]]),
                       "b_ld", g, waits=(w if hc == 0 else ()))

        load_group(0)
        for j in range(DC):
            quant_chunk(0, j)
        for g in range(ng):
            par = g % 2
            q3_need = B.c("b_q3")
            if g + 1 < ng:
                load_group(g + 1)
            for ic in range(IC):
                k = g * IC + ic
                pi = ic % 2
                pg = p0.ap() if pi == 0 else pacc0[:, 0:512]
                pu = p1.ap() if pi == 0 else pacc0[:, 512:]
                wgq = [("b_q3", q3_need), ("b_si", B.h("b_si", n_si + k - 1))]
                wuq = [("b_h", B.h("b_h", n_h + k - 1))]
                for hc in range(DC):
                    B.emit("tensor", lambda e, par=par, hc=hc, ic=ic, pg=pg:
                           e.matmul(pg, wgT[:, hc, ic * 128:(ic + 1) * 128],
                                    xq[par][:, hc, :], start=(hc == 0),
                                    stop=(hc == DC - 1)),
                           waits=(wgq if hc == 0 else ()),
                           inc=("b_mg" if hc == DC - 1 else None))
                for hc in range(DC):
                    B.emit("tensor", lambda e, par=par, hc=hc, ic=ic, pu=pu:
                           e.matmul(pu, wuT[:, hc, ic * 128:(ic + 1) * 128],
                                    xq[par][:, hc, :], start=(hc == 0),
                                    stop=(hc == DC - 1)),
                           waits=(wuq if hc == 0 else ()),
                           inc=("b_mu" if hc == DC - 1 else None))
                B.emit("scalar", lambda e, pg=pg, pi=pi, ic=ic: e.activation(
                    out=silu[pi].ap(), in_=pg, func=AF.Silu,
                    scale=csg[:, ic:ic + 1], bias=sl("zb")[:, 0:1]),
                    waits=[("b_mg", B.c("b_mg")), ("b_h", B.h("b_h", n_h + k - 1))],
                    inc="b_si")
                B.emit("vector", lambda e, pu=pu, pi=pi, ic=ic:
                       e.scalar_tensor_tensor(
                           hev[pi].ap(), pu, csu[:, ic:ic + 1], silu[pi].ap(),
                           ALU.mult, ALU.mult),
                       waits=[("b_mu", B.c("b_mu")), ("b_si", B.c("b_si")),
                              B.dtot("b_hd", k),
                              ("b_hv", B.h("b_hv", n_hv + k - 1))], inc="b_h")
                B.emit("vector", lambda e, pi=pi, k=k: e.tensor_reduce(
                    sl("hacc", (128, 176))[:, k:k + 1], hev[pi].ap(), AX.X,
                    ALU.max, apply_absolute_value=True), inc="b_hv")
                B.dma2("sync", bass.AP(hDD, ic * 128 * T + g * 512,
                                       [[T, 128], [1, 512]]),
                       hev[pi].ap(), "b_hd", k, waits=[("b_h", B.c("b_h"))])
                if g + 1 < ng and 6 <= ic < 6 + DC:
                    quant_chunk(g + 1, ic - 6)
        B.emit("vector", lambda e, t=hamax_slot, n=ng * IC: e.tensor_reduce(
            sl(t)[:, 0:1], sl("hacc", (128, 176))[:, 0:n], AX.X, ALU.max),
            inc="b_hm")

    def emit_ffn_c(res_src, xdst, first_mm_waits=(), tile_hook=None,
                   outt_guard=None):
        n_q3 = B.n("c_q3"); n_mm = B.n("c_mm")
        n_ev = B.n("c_ev")
        sc_val = B.c("sc")

        def tile_front(tt):
            par = tt % 2
            w = [("c_mm", B.h("c_mm", n_mm + tt - 1))]
            for half in range(2):
                ic0 = 11 * half
                B.dma2("sync", hq[par][:, ic0:ic0 + 11, :],
                       bass.AP(hDD, ic0 * 128 * T + tt * 128,
                               [[T, 128], [128 * T, 11], [1, 128]]),
                       "c_ld", tt, waits=(w if half == 0 else ()))
            for half in range(2):
                ic0 = 11 * half
                m = 2 * tt + half
                qp = m % 2
                B.emit("scalar", lambda e, par=par, ic0=ic0, qp=qp: e.activation(
                    out=hqfs[qp].ap(), in_=hq[par][:, ic0:ic0 + 11, :],
                    func=AF.Identity, scale=sl("inv_sd")[:, 0:1],
                    bias=sl("zb")[:, 0:1]),
                    waits=[B.dtot("c_ld", tt), ("sc", sc_val),
                           ("c_q3", B.h("c_q3", n_q3 + m - 1))],
                    inc="c_q1")
                B.emit("vector", lambda e, qp=qp: e.tensor_scalar(
                    hqfs[qp].ap(), hqfs[qp].ap(), RND, RND, ALU.add,
                    ALU.subtract),
                    waits=[("c_q1", B.c("c_q1"))], inc="c_q2")
                B.emit("vector", lambda e, par=par, ic0=ic0, qp=qp: e.tensor_scalar(
                    hq[par][:, ic0:ic0 + 11, :], hqfs[qp].ap(), -128.0, 127.0,
                    ALU.max, ALU.min),
                    waits=[("c_q2", B.c("c_q2"))], inc="c_q3")

        tile_front(0)
        for tt in range(nt):
            par = tt % 2
            acc = paccs[par]
            wmm = [("c_q3", B.c("c_q3")), ("c_ev", B.h("c_ev", n_ev + tt - 1))]
            if tt == 0:
                wmm = wmm + list(first_mm_waits)
            for ic in range(IC):
                B.emit("tensor", lambda e, par=par, ic=ic, acc=acc: e.matmul(
                    acc[:, 0:512], hq[par][:, ic, :], wdT[:, ic, 0:512],
                    start=(ic == 0), stop=(ic == IC - 1)),
                    waits=(wmm if ic == 0 else ()))
                B.emit("tensor", lambda e, par=par, ic=ic, acc=acc: e.matmul(
                    acc[:, 512:], hq[par][:, ic, :], wdT[:, ic, 512:],
                    start=(ic == 0), stop=(ic == IC - 1)),
                    inc=("c_mm" if ic == IC - 1 else None))
            B.dma2("sync", xt[par].ap(),
                   bass.AP(res_src, tt * 128 * D, [[D, 128], [1, D]]), "c_xl",
                   tt, waits=[("c_ev", B.h("c_ev", n_ev + tt - 1))])
            if tt + 1 < nt:
                tile_front(tt + 1)
            wev = [("c_mm", B.c("c_mm")), B.dtot("c_xl", tt),
                   B.dtot("c_xo", tt)]
            if outt_guard is not None:
                wev.append(outt_guard(tt))
            B.emit("vector", lambda e, par=par, acc=acc: e.scalar_tensor_tensor(
                outt[par].ap(), acc.ap(), sl("half_sd")[:, 0:1], xt[par].ap(),
                ALU.mult, ALU.add),
                waits=wev, inc="c_ev")
            B.dma2("sync", bass.AP(xdst, tt * 128 * D, [[D, 128], [1, D]]),
                   outt[par].ap(), "c_xo", tt, waits=[("c_ev", B.c("c_ev"))])
            if tile_hook is not None:
                tile_hook(tt)

    def emit_attn_wload(entry_waits):
        first = True
        for wsrc, dst in ((wqTD, awq), (wkTD, awk), (wvTD, awv)):
            for j in range(DC):
                idx = B.n("aw_c")
                par = idx % 2
                w = [("aw_c", B.h("aw_c", idx - 1))]
                if first:
                    w = w + list(entry_waits)
                    first = False
                B.dma2("sync", wst[par].ap(),
                       bass.AP(wsrc, j * 128 * D, [[D, 128], [1, D]]), "aw_ld",
                       idx, waits=w)
                B.emit("vector", lambda e, dst=dst, j=j, par=par: e.tensor_copy(
                    dst[:, j, :], wst[par].ap()),
                    waits=[B.dtot("aw_ld", idx)], inc="aw_c")

    def emit_proj():
        xn_waits = B.dboth("l_xn")
        aw_val = B.c("aw_c")

        def one_pass(pre, wsb, bias_pp, dstD, s_mm, s_ev, s_dma, prev_mm=None,
                     prev_dma=None):
            n_mm = B.n(s_mm); n_ev = B.n(s_ev)

            def g_load(g):
                par = g % 2
                w = [(s_mm, B.h(s_mm, n_mm + (g - 1) * DC)),
                     ("aw_c", aw_val)] + xn_waits
                if g < 2 and prev_mm is not None:
                    w.append(prev_mm)
                for hc in range(DC):
                    B.dma2("sync", xn2[par][:, hc, :],
                           bass.AP(xnTD, hc * 128 * T + g * 512,
                                   [[T, 128], [1, 512]]),
                           "pj_ld", g, waits=(w if hc == 0 else ()))

            g_load(0)
            for g in range(ng):
                par = g % 2
                for m in range(DC):
                    gm = g * DC + m
                    pp = p0 if m % 2 == 0 else p1
                    wmm = [B.dtot("pj_ld", g),
                           (s_ev, B.h(s_ev, n_ev + gm - 1))]
                    for hc in range(DC):
                        B.emit("tensor",
                               lambda e, par=par, hc=hc, m=m, pp=pp, wsb=wsb:
                               e.matmul(pp.ap(),
                                        wsb[:, hc, m * 128:(m + 1) * 128],
                                        xn2[par][:, hc, :],
                                        start=(hc == 0), stop=(hc == DC - 1)),
                               waits=(wmm if hc == 0 else ()),
                               inc=(s_mm if hc == DC - 1 else None))
                    wev = [(s_mm, B.c(s_mm)), B.dtot(s_dma, gm)]
                    if gm < 2 and prev_dma is not None:
                        wev += prev_dma
                    B.emit("vector",
                           lambda e, m=m, pp=pp, bias_pp=bias_pp: e.tensor_scalar(
                               kev[m % 2].ap(), pp.ap(), bias_pp[:, m:m + 1],
                               None, ALU.add),
                           waits=wev, inc=s_ev)
                    B.dma2("sync", bass.AP(dstD, m * 128 * T + g * 512,
                                           [[T, 128], [1, 512]]),
                           kev[m % 2].ap(), s_dma, gm, waits=[(s_ev, B.c(s_ev))])
                    if m == 0 and g + 1 < ng:
                        g_load(g + 1)

        one_pass("pk", awk, bk_pp, ktoD, "pk_mm", "pk_ev", "pk_dma")
        B.emit("gpsimd", lambda e: e.collective_compute(
            "AllGather", ALU.bypass,
            replica_groups=[[0, 1], [2, 3], [4, 5], [6, 7]],
            ins=[flat(ktoD, D * T)], outs=[flat(ktgD, 2 * D * T)]),
            waits=B.dboth("pk_dma"), inc="ag_kt")

        n_mm = B.n("pv_mm"); n_ev = B.n("pv_ev")
        pk_mm_done = ("pk_mm", B.c("pk_mm"))

        def v_load(g):
            par = g % 2
            w = [("pv_mm", B.h("pv_mm", n_mm + (g - 1) * 4)),
                 ("aw_c", aw_val)] + xn_waits
            if g < 2:
                w.append(pk_mm_done)
            for hc in range(DC):
                B.dma2("sync", xn2[par][:, hc, :],
                       bass.AP(xnTD, hc * 128 * T + g * 512,
                               [[T, 128], [1, 512]]),
                       "pj_ld", g, waits=(w if hc == 0 else ()))

        v_load(0)
        for g in range(ng):
            par = g % 2
            for sub in range(4):
                gs = g * 4 + sub
                acc = paccs[sub % 2]
                wmm = [B.dtot("pj_ld", g),
                       ("pv_ev", B.h("pv_ev", n_ev + gs - 1))]
                for hc in range(DC):
                    B.emit("tensor", lambda e, par=par, hc=hc, sub=sub, acc=acc:
                           e.matmul(acc[:, 0:512],
                                    xn2[par][:, hc, sub * 128:(sub + 1) * 128],
                                    awv[:, hc, 0:512], start=(hc == 0), stop=False),
                           waits=(wmm if hc == 0 else ()))
                    B.emit("tensor", lambda e, par=par, hc=hc, sub=sub, acc=acc:
                           e.matmul(acc[:, 512:],
                                    xn2[par][:, hc, sub * 128:(sub + 1) * 128],
                                    awv[:, hc, 512:], start=(hc == 0), stop=False))
                B.emit("tensor", lambda e, acc=acc: e.matmul(
                    acc[:, 0:512], ones_b[0:1, :], bv_row[0:1, 0:512],
                    start=False, stop=True))
                B.emit("tensor", lambda e, acc=acc: e.matmul(
                    acc[:, 512:], ones_b[0:1, :], bv_row[0:1, 512:],
                    start=False, stop=True), inc="pv_mm")
                B.emit("vector", lambda e, sub=sub, acc=acc: e.tensor_copy(
                    vev[sub % 2].ap(), acc.ap()),
                    waits=[("pv_mm", B.c("pv_mm")), B.dtot("pv_dma", gs)],
                    inc="pv_ev")
                B.dma2("sync", bass.AP(voD, (g * 512 + sub * 128) * D,
                                       [[D, 128], [1, D]]),
                       vev[sub % 2].ap(), "pv_dma", gs,
                       waits=[("pv_ev", B.c("pv_ev"))])
                if sub == 0 and g + 1 < ng:
                    v_load(g + 1)
        B.emit("gpsimd", lambda e: e.collective_compute(
            "AllGather", ALU.bypass,
            replica_groups=[[0, 1], [2, 3], [4, 5], [6, 7]],
            ins=[flat(voD, T * D)], outs=[flat(vgD, 2 * T * D)]),
            waits=B.dboth("pv_dma"), inc="ag_v")

        one_pass("pq", awq, bq_pp, qtD, "pq_mm", "pq_ev", "pq_dma",
                 prev_mm=("pv_mm", B.c("pv_mm")), prev_dma=B.dboth("pk_dma"))

    def emit_flash(res_src, xdst, sb_hook=None, outt_guard=None):
        entry = B.dboth("pq_dma") + [("ag_kt", B.c("ag_kt")),
                                     ("ag_v", B.c("ag_v"))]
        for e_ in ENGINES:
            B.wait_only(e_, entry)
        n_sm = B.n("f_sm"); n_ac = B.n("f_ac"); n_u = B.n("f_u")
        n_ex = B.n("f_ex"); n_ptr = B.n("f_ptr"); n_pev = B.n("f_pev")
        n_pv = B.n("f_pv"); n_ao = B.n("f_ao"); n_aoe = B.n("f_aoe")
        n_aot = B.n("f_aot"); n_oe = B.n("f_oe"); n_om = B.n("f_om")
        qtbs = [qtb[0], qtb[1], qtb3]
        QB = min(4, nkc)
        nseg = nkc // QB
        hk = nkc // 2
        npc = max(1, hk // 2)
        pwk = hk // npc
        bst = {"rr": 0, "lu": {}, "sb_pacc": -1}

        def scores_dma(sb, kc):
            sbpar = sb % 3
            if kc == 0:
                for hc in range(DC):
                    B.dma2("sync", qtbs[sbpar][:, hc, :],
                           bass.AP(qtD, hc * 128 * T + sb * 256,
                                   [[T, 128], [1, 256]]),
                           "f_qt", sb,
                           waits=([("f_sm",
                                    B.h("f_sm", n_sm + (sb - 2) * nkc * 2))]
                                  if hc == 0 else ()))
            kpar = kc % 2
            gi = sb * nkc + kc
            rank = 0 if kc * 512 < T else 1
            base = rank * D * T + (kc * 512 - rank * T)
            B.dma2("sync", ktb[kpar][:, :, :],
                   bass.AP(ktgD, base, [[T, 128], [128 * T, 8], [1, 512]]),
                   "f_kt", gi,
                   waits=[("f_sm", B.h("f_sm", n_sm + (gi - 1) * 2))])

        def scores_kc(sb, kc, dma_done=False):
            sbpar = sb % 3
            if not dma_done:
                scores_dma(sb, kc)
            kpar = kc % 2
            gi = sb * nkc + kc
            for qt2 in (0, 1):
                ci = gi * 2 + qt2
                use4 = (sb > 0 and kc >= 5)
                if use4:
                    if kc == 5 and qt2 == 0:
                        bst["rr"] = 0
                    bi = bst["rr"] % 4
                    bst["rr"] += 1
                else:
                    bi = ci % 2
                psap = (p0.ap(), p1.ap(), pacc0[:, 0:512],
                        pacc0[:, 512:])[bi]
                tpar = ci % 3
                wmm = [B.dtot("f_kt", gi), B.dtot("f_qt", sb)]
                lu = bst["lu"].get(bi)
                if lu is not None:
                    wmm.append(("f_ac", B.h("f_ac", n_ac + lu + 1)))
                if bi >= 2 and bst["sb_pacc"] != sb:
                    bst["sb_pacc"] = sb
                    wmm.append(("f_oe",
                                B.h("f_oe", n_oe + 2 * (sb - 1) + 2)))
                bst["lu"][bi] = ci
                for hc in range(DC):
                    B.emit("tensor", lambda e, sbpar=sbpar, kpar=kpar, hc=hc,
                           qt2=qt2, psap=psap: e.matmul(
                               psap,
                               qtbs[sbpar][:, hc, qt2 * 128:(qt2 + 1) * 128],
                               ktb[kpar][:, hc, :],
                               start=(hc == 0), stop=(hc == DC - 1)),
                           waits=(wmm if hc == 0 else ()),
                           inc=("f_sm" if hc == DC - 1 else None))
                B.emit("scalar", lambda e, psap=psap, tpar=tpar: e.activation(
                    out=t1s[tpar].ap(), in_=psap, func=AF.Tanh, scale=0.5,
                    bias=sl("zb")[:, 0:1]),
                    waits=[("f_sm", B.c("f_sm")),
                           ("f_u", B.h("f_u", n_u + ci - 2))])
                B.emit("scalar", lambda e, psap=psap, tpar=tpar: e.activation(
                    out=t2s[tpar].ap(), in_=psap, func=AF.Tanh,
                    bias=sl("zb")[:, 0:1]))
                B.emit("scalar", lambda e, psap=psap, tpar=tpar: e.activation(
                    out=rsc[tpar].ap(), in_=psap, func=AF.Relu,
                    bias=sl("zb")[:, 0:1]), inc="f_ac")
                B.emit("vector", lambda e, tpar=tpar: e.scalar_tensor_tensor(
                    t1s[tpar].ap(), t2s[tpar].ap(), 1.5, t1s[tpar].ap(),
                    ALU.mult, ALU.add),
                    waits=[("f_ac", B.c("f_ac"))])
                B.emit("vector", lambda e, qt2=qt2, kc=kc, tpar=tpar:
                       e.tensor_add(
                           Ubuf[qt2][:, kc * 512:(kc + 1) * 512],
                           t1s[tpar].ap(), rsc[tpar].ap()),
                       waits=[("f_ex",
                               B.h("f_ex", n_ex + 4 * npc * (sb - 1)
                                   + (kc // hk) * 2 * npc
                                   + 2 * ((kc % hk) // pwk) + qt2 + 1))],
                       inc="f_u")
                if (kc + 1) % QB == 0:
                    seg = (kc + 1) // QB - 1
                    B.emit("vector", lambda e, qt2=qt2, kc=kc, seg=seg:
                           e.tensor_reduce(
                               sl(f"mp{qt2}", (128, 16))[:, seg:seg + 1],
                               Ubuf[qt2][:, (kc + 1 - QB) * 512:
                                         (kc + 1) * 512],
                               AX.X, ALU.max))

        def exp_part(sb):
            for qt2 in (0, 1):
                B.emit("vector", lambda e, qt2=qt2: e.tensor_reduce(
                    sl(f"mrow{qt2}")[:, 0:1],
                    sl(f"mp{qt2}", (128, 16))[:, 0:nseg], AX.X, ALU.max),
                    waits=[("f_u",
                            B.h("f_u", n_u + 2 * nkc * sb + 2 * nkc - 1 + qt2))])
                B.emit("vector", lambda e, qt2=qt2: e.tensor_scalar(
                    sl(f"negm{qt2}")[:, 0:1], sl(f"mrow{qt2}")[:, 0:1], -0.1,
                    None, ALU.mult), inc="f_mxr")
            first = True
            for half in (0, 1):
                for p in range(npc):
                    for qt2 in (0, 1):
                        c0 = half * (S // 2) + p * pwk * 512
                        c1 = c0 + pwk * 512
                        idx = half * npc + p
                        vlast = half * hk + (p + 1) * pwk - 1
                        w = [("f_ptr", B.h("f_ptr", n_ptr
                                           + 2 * nkc * (sb - 1)
                                           + 2 * vlast + qt2 + 1))]
                        if first:
                            w.append(("f_mxr", B.c("f_mxr")))
                            first = False
                        B.emit("scalar", lambda e, qt2=qt2, c0=c0, c1=c1,
                               idx=idx: e.activation(
                                   out=Pbuf[qt2][:, c0:c1],
                                   in_=Ubuf[qt2][:, c0:c1],
                                   func=AF.Exp, bias=sl(f"negm{qt2}")[:, 0:1],
                                   scale=0.1,
                                   accum_out=lrp[qt2][:, idx:idx + 1]),
                               waits=w, inc="f_ex")

        def vtb_dma(sb, vc):
            vpar = vc % 2
            B.dma2("sync", vtb[vpar][:, :, :],
                   bass.AP(vgD, vc * 512 * D,
                           [[D, 128], [128 * D, 4], [1, D]]),
                   "f_vt", sb * nkc + vc,
                   waits=[("f_pv",
                           B.h("f_pv", n_pv + 2 * nkc * sb + 2 * vc - 2))])

        def pv_part(sb, pre_dma=0):
            for vc in range(nkc):
                vpar = vc % 2
                if vc >= pre_dma:
                    vtb_dma(sb, vc)
                for qt2 in (0, 1):
                    wtr = [("f_ex",
                            B.h("f_ex", n_ex + 4 * npc * sb
                                + (vc // hk) * 2 * npc
                                + 2 * ((vc % hk) // pwk) + qt2 + 1)),
                           ("f_pev",
                            B.h("f_pev", n_pev + 2 * nkc * sb + 2 * vc + qt2 - 1))]
                    if vc == 0:
                        wtr.append(("f_aoe", B.h("f_aoe", n_aoe + 2 * (sb - 1)
                                                 + qt2 + 1)))
                    for i4 in range(4):
                        B.emit("tensor", lambda e, qt2=qt2, vc=vc, i4=i4:
                               e.transpose(
                                   pts[qt2][:, i4, :],
                                   Pbuf[qt2][:, (vc * 4 + i4) * 128:
                                             (vc * 4 + i4 + 1) * 128],
                                   ident.ap()),
                               waits=(wtr if i4 == 0 else ()),
                               inc=("f_ptr" if i4 == 3 else None))
                for qt2 in (0, 1):
                    B.emit("vector", lambda e, qt2=qt2: e.tensor_copy(
                        ptsb[qt2][:, 0:4, :], pts[qt2][:, 0:4, :]),
                        waits=[("f_ptr", B.h("f_ptr",
                                             n_ptr + 2 * nkc * sb + 2 * vc
                                             + qt2 + 1)),
                               ("f_pv",
                                B.h("f_pv", n_pv + 2 * nkc * sb + 2 * (vc - 1)
                                    + qt2 + 1))], inc="f_pev")
                for qt2 in (0, 1):
                    acc = paccs[qt2]
                    wpv = [("f_pev", B.h("f_pev",
                                         n_pev + 2 * nkc * sb + 2 * vc
                                         + qt2 + 1)),
                           B.dtot("f_vt", sb * nkc + vc)]
                    if vc == 0:
                        wpv.append(("f_oe", B.h("f_oe", n_oe + 2 * sb + qt2 - 1)))
                    for half in (0, 1):
                        for i4 in range(4):
                            B.emit("tensor", lambda e, qt2=qt2, vc=vc, i4=i4,
                                   vpar=vpar, acc=acc, half=half: e.matmul(
                                       acc[:, half * 512:(half + 1) * 512],
                                       ptsb[qt2][:, i4, :],
                                       vtb[vpar][:, i4,
                                                 half * 512:(half + 1) * 512],
                                       start=(vc == 0 and i4 == 0),
                                       stop=(vc == nkc - 1 and i4 == 3)),
                                   waits=(wpv if (half == 0 and i4 == 0)
                                          else ()),
                                   inc=("f_pv" if (half == 1 and i4 == 3)
                                        else None))

        def epi_e0(sb):
            for qt2 in (0, 1):
                B.emit("vector", lambda e, qt2=qt2: e.tensor_reduce(
                    sl(f"lrow{qt2}")[:, 0:1], lrp[qt2][:, 0:2 * npc],
                    AX.X, ALU.add), waits=[("f_ex", B.c("f_ex"))])
                B.emit("vector", lambda e, qt2=qt2: e.reciprocal(
                    sl(f"invl{qt2}")[:, 0:1], sl(f"lrow{qt2}")[:, 0:1]),
                    inc="f_iv")
            for qt2 in (0, 1):
                acc = paccs[qt2]
                B.emit("vector", lambda e, qt2=qt2, acc=acc: e.tensor_scalar(
                    ao_sb2[qt2].ap(), acc.ap(), sl(f"invl{qt2}")[:, 0:1], None,
                    ALU.mult),
                    waits=[("f_pv", B.h("f_pv", n_pv + 2 * nkc * sb
                                        + 2 * (nkc - 1) + qt2 + 1)),
                           ("f_iv", B.c("f_iv")),
                           ("f_aot", B.h("f_aot", n_aot + 2 * sb + qt2 - 1))],
                    inc="f_ao")
            for qt2 in (0, 1):
                row = sb * 256 + qt2 * 128
                B.dma2("sync", xt[qt2].ap(),
                       bass.AP(res_src, row * D, [[D, 128], [1, D]]), "f_xl",
                       qt2,
                       waits=[("f_oe", B.h("f_oe", n_oe + 2 * sb + qt2 - 1))])

        def epi_aot(sb, qt2):
            for j in range(DC):
                B.emit("tensor", lambda e, qt2=qt2, j=j: e.transpose(
                    pts[qt2][:, j, :], ao_sb2[qt2][:, j * 128:(j + 1) * 128],
                    ident.ap()),
                    waits=([("f_ao", B.h("f_ao", n_ao + 2 * sb + qt2 + 1))]
                           if j == 0 else ()),
                    inc=("f_aot" if j == DC - 1 else None))

        def epi_aoe(sb, qt2):
            B.emit("vector", lambda e, qt2=qt2: e.tensor_copy(
                aoT_sb2[qt2].ap(), pts[qt2].ap()),
                waits=[("f_aot", B.h("f_aot", n_aot + 2 * sb + qt2 + 1)),
                       ("f_om", B.h("f_om", n_om + 2 * (sb - 1) + qt2 + 1))],
                inc="f_aoe")

        def epi_omm(sb, qt2):
            acc = paccs[qt2]
            womm = [("f_aoe", B.h("f_aoe", n_aoe + 2 * sb + qt2 + 1)),
                    ("f_ao", B.h("f_ao", n_ao + 2 * sb + 2))]
            for hc in range(DC):
                B.emit("tensor", lambda e, hc=hc, acc=acc, qt2=qt2: e.matmul(
                    acc[:, 0:512], aoT_sb2[qt2][:, hc, :],
                    wo_sb[:, hc, 0:512],
                    start=(hc == 0), stop=False),
                    waits=(womm if hc == 0 else ()))
                B.emit("tensor", lambda e, hc=hc, acc=acc, qt2=qt2: e.matmul(
                    acc[:, 512:], aoT_sb2[qt2][:, hc, :],
                    wo_sb[:, hc, 512:],
                    start=(hc == 0), stop=False))
            B.emit("tensor", lambda e, acc=acc: e.matmul(
                acc[:, 0:512], ones_b[0:1, :], bo_row[0:1, 0:512],
                start=False, stop=True))
            B.emit("tensor", lambda e, acc=acc: e.matmul(
                acc[:, 512:], ones_b[0:1, :], bo_row[0:1, 512:],
                start=False, stop=True), inc="f_om")

        def epi_out(sb, qt2):
            acc = paccs[qt2]
            row = sb * 256 + qt2 * 128
            wev = [("f_om", B.h("f_om", n_om + 2 * sb + qt2 + 1)),
                   B.dtot("f_xl", qt2), B.dtot("f_xo", qt2)]
            if outt_guard is not None:
                g_ = outt_guard(sb, qt2)
                if g_ is not None:
                    wev.append(g_)
            B.emit("vector", lambda e, qt2=qt2, acc=acc: e.tensor_add(
                outt[qt2][:, 0:512], acc[:, 0:512], xt[qt2][:, 0:512]),
                waits=wev)
            B.emit("vector", lambda e, qt2=qt2, acc=acc: e.tensor_add(
                outt[qt2][:, 512:], acc[:, 512:], xt[qt2][:, 512:]),
                inc="f_oe")
            B.dma2("sync", bass.AP(xdst, row * D, [[D, 128], [1, D]]),
                   outt[qt2].ap(), "f_xo", qt2,
                   waits=[("f_oe", B.c("f_oe"))])

        HOIST_VC = 2 if nkc >= 8 else 0
        for kc in range(nkc):
            scores_kc(0, kc)
            if HOIST_VC and kc == 6:
                vtb_dma(0, 0)
            if HOIST_VC and kc == 7:
                vtb_dma(0, 1)
        for sb in range(nsb):
            exp_part(sb)
            if sb + 1 < nsb:
                scores_dma(sb + 1, 0)
                scores_dma(sb + 1, 1)
            pv_part(sb, pre_dma=HOIST_VC)
            stages = [lambda sb=sb: epi_e0(sb),
                      lambda sb=sb: epi_aot(sb, 0),
                      lambda sb=sb: (epi_aoe(sb, 0), epi_aot(sb, 1)),
                      lambda sb=sb: (epi_aoe(sb, 1), epi_omm(sb, 0)),
                      lambda sb=sb: (epi_omm(sb, 1), epi_out(sb, 0)),
                      lambda sb=sb: (epi_out(sb, 1),
                                     sb_hook(sb) if sb_hook else None)]
            if sb + 1 < nsb:
                stages[0]()
                done = 1
                for kc in range(nkc):
                    scores_kc(sb + 1, kc, dma_done=(kc < 2))
                    if HOIST_VC and kc == 6:
                        vtb_dma(sb + 1, 0)
                    if HOIST_VC and kc == 7:
                        vtb_dma(sb + 1, 1)
                    if done < len(stages):
                        stages[done]()
                        done += 1
                while done < len(stages):
                    stages[done]()
                    done += 1
            else:
                for f_ in stages:
                    f_()

    def emit_wreload(entry_waits):
        first = True
        for j in range(DC):
            w = (list(entry_waits) + [("q_st", B.c("q_st"))]) if first else ()
            first = False
            B.dma("sync", wgT[:, j, :],
                  bass.AP(wgS, j * 128 * I, [[I, 128], [1, I]]), "wr_ld", waits=w)
        for j in range(DC):
            B.dma("sync", wuT[:, j, :],
                  bass.AP(wuS, j * 128 * I, [[I, 128], [1, I]]), "wr_ld")
        for ic in range(IC):
            B.dma("sync", wdT[:, ic, :],
                  bass.AP(wdS, ic * 128 * D, [[D, 128], [1, D]]), "wr_ld")

    gu_ev, quant_w_fn = emit_setup()
    emit_ln_dram(xin, "amax1", entry_waits=[("p0_misc", 64)])
    xpart_request(sl("amax1"), ccB[0], [("l_mx", B.c("l_mx"))], [], "c1")
    quant_w_fn(wdD, None, wdT, wdS, DC, True, I)
    xpart_complete(sl("M1"), "c1")
    emit_scales_x(sl("M1"))
    for f_ in deferred_stores:
        f_()
    B.wait_only("tensor", [("q_ev", gu_ev), ("sc", B.c("sc"))])
    emit_ffn_b("hamax1")
    xpart_max_allreduce(sl("hamax1"), ccB[1], sl("M2"),
                        [("b_hm", B.c("b_hm"))] + B.dboth("b_hd"), [], "c2")
    emit_scales_h(sl("M2"))
    b1_h = B.c("b_h")
    ln2 = LNPass(None)
    emit_ffn_c(xin, x1D,
               first_mm_waits=[("b_h", b1_h), ("q_ev", B.c("q_ev"))],
               tile_hook=lambda tt: ln2.tile(
                   tt, outt[tt % 2].ap(), [("c_ev", B.c("c_ev"))]),
               outt_guard=lambda tt: ln2.norm_done(tt - 2) if tt >= 2 else None)
    emit_attn_wload([("c_mm", B.c("c_mm")), ("b_mu", B.c("b_mu")),
                     ("q_st", B.c("q_st"))] + B.dboth("c_xo"))
    emit_proj()
    ln3 = LNPass("amax3")

    def ln3_hook(sb):
        g = [("f_oe", B.c("f_oe"))]
        ln3.stats(2 * sb, outt[0].ap(), g)
        ln3.stats(2 * sb + 1, outt[1].ap(), g)
        ln3.norm(2 * sb, outt[0].ap())
        ln3.norm(2 * sb + 1, outt[1].ap())

    emit_flash(x1D, x2D, sb_hook=ln3_hook,
               outt_guard=lambda sb, qt2: (ln3.norm_done(2 * (sb - 1) + qt2)
                                           if sb >= 1 else None))
    emit_wreload(B.dboth("f_xo"))
    xpart_max_allreduce(sl("amax3"), ccB[2], sl("M1"),
                        [("l_mx", B.c("l_mx"))], [("f_oe", B.c("f_oe"))], "c3")
    emit_scales_x(sl("M1"))
    B.wait_only("tensor", [("wr_ld", B.c("wr_ld")), ("sc", B.c("sc"))])
    emit_ffn_b("hamax3")
    xpart_max_allreduce(sl("hamax3"), ccB[3], sl("M2"),
                        [("b_hm", B.c("b_hm"))] + B.dboth("b_hd"), [], "c4")
    emit_scales_h(sl("M2"))
    b3_h = B.c("b_h")
    emit_ffn_c(x2D, xout, first_mm_waits=[("b_h", b3_h)])
    B.wait_only("sync", B.dboth("c_xo"))
    B.wait_only("gpsimd", B.dboth("c_xo"))

    with nc.Block() as block:
        @block.sync
        def _(eng):
            for f in B.q["sync"]:
                f(eng)

        @block.scalar
        def _(eng):
            for f in B.q["scalar"]:
                f(eng)

        @block.vector
        def _(eng):
            for f in B.q["vector"]:
                f(eng)

        @block.tensor
        def _(eng):
            for f in B.q["tensor"]:
                f(eng)

        @block.gpsimd
        def _(eng):
            for f in B.q["gpsimd"]:
                f(eng)

    return nc


def make_in_maps(x_full, Wq, bq, Wk, bk, Wv, bv, Wo, bo, clip_x, clip_down,
                 w_gate, w_up, w_down, T=4096):
    f32 = np.float32
    ident = np.eye(128, dtype=ml_dtypes.bfloat16)
    com = {
        "wqT": np.ascontiguousarray(np.asarray(Wq, f32).T),
        "wkT": np.ascontiguousarray(np.asarray(Wk, f32).T),
        "wvT": np.ascontiguousarray(np.asarray(Wv, f32).T),
        "woT": np.ascontiguousarray(np.asarray(Wo, f32).T),
        "bqp": np.ascontiguousarray(np.asarray(bq, f32).reshape(8, 128).T),
        "bkp": np.ascontiguousarray(np.asarray(bk, f32).reshape(8, 128).T),
        "bvr": np.asarray(bv, f32).reshape(1, D),
        "bor": np.asarray(bo, f32).reshape(1, D),
        "wg": np.asarray(w_gate, f32),
        "wu": np.asarray(w_up, f32),
        "wd": np.asarray(w_down, f32),
        "clip": np.array([[np.asarray(clip_x).reshape(-1)[0],
                           np.asarray(clip_down).reshape(-1)[0]]], f32),
        "ident": ident,
    }
    in_maps = []
    for c in range(8):
        b, hh = c // 2, c % 2
        shard = np.ascontiguousarray(x_full[b, hh * T:(hh + 1) * T, :], dtype=f32)
        m = dict(com)
        m["xin"] = shard
        in_maps.append(m)
    return in_maps


def kernel(L0_attn_x, L0_attn_Wq, L0_attn_bq, L0_attn_Wk, L0_attn_bk,
           L0_attn_Wv, L0_attn_bv, L0_attn_Wo, L0_attn_bo,
           L0_ffn_clip_x, L0_ffn_clip_down, L0_ffn_w_gate, L0_ffn_w_up,
           L0_ffn_w_down):
    T = 4096
    x = np.asarray(L0_attn_x, np.float32)
    in_maps = make_in_maps(x, L0_attn_Wq, L0_attn_bq, L0_attn_Wk, L0_attn_bk,
                           L0_attn_Wv, L0_attn_bv, L0_attn_Wo, L0_attn_bo,
                           L0_ffn_clip_x, L0_ffn_clip_down, L0_ffn_w_gate,
                           L0_ffn_w_up, L0_ffn_w_down, T=T)
    nc = build_program(T)
    res = run_bass_kernel_spmd(nc, in_maps, core_ids=list(range(8)))
    out = np.empty((4, 2 * T, D), np.float32)
    for c in range(8):
        b, hh = c // 2, c % 2
        out[b, hh * T:(hh + 1) * T, :] = res.results[c]["xout"]
    return out



# version 27
# speedup vs baseline: 1.1317x; 1.1317x over previous
import numpy as np
import ml_dtypes
import concourse.bass as bass
import concourse.mybir as mybir
import concourse.bass_utils as _bass_utils
from concourse.bass_utils import run_bass_kernel_spmd

_orig_run_command = _bass_utils.run_command


def _run_command_ldwopt(argv, **kwargs):
    argv = ["--enable-ldw-opt=true" if a == "--enable-ldw-opt=false" else a
            for a in argv]
    return _orig_run_command(argv, **kwargs)


_bass_utils.run_command = _run_command_ldwopt

F32 = mybir.dt.float32
BF16 = mybir.dt.bfloat16
AF = mybir.ActivationFunctionType
ALU = mybir.AluOpType
AX = mybir.AxisListType

RND = 12582912.0
D = 1024
DC = 8
I = 2816
IC = 22
EPS = 1e-5

ENGINES = ("sync", "scalar", "vector", "tensor", "gpsimd")

OFF_IDENT = 2048
OFF_BQ = 2304
OFF_BK = 2336
OFF_BVROW = 2368
OFF_BOROW = 4416
OFF_ONESB = 6464
OFF_ONESF = 6720
OFF_CLIP = 7232
OFF_LNST = 7264
OFF_ROW128 = 7328
OFF_RSG = 7840
OFF_RSU = 7936
OFF_CSG = 8032
OFF_CSU = 8128
OFF_SLOTS = 8224
OFF_XT = 10752
OFF_OUTT = 19456
OFF_XNBF = 27648
OFF_TEVAC = 31744
OFF_WO = 35840
OFF_W = 52224
FO_WG = 0
FO_WU = 45056
FO_WD = 90112
FO_XQ = 135168
FO_HEV = 151552
FO_SILU = 153600
FO_HQ = 157696
FO_HQF = 168960
FO_P0STG = 151552
FO_P0WI = 162816
AO_WQ = 0
AO_WK = 16384
AO_WV = 32768
AO_XN2 = 49152
AO_KEV = 65536
AO_VEV = 67584
AO_WST = 71680
FL_QTB = 0
FL_KTB = 8192
FL_VTB = 24576
FL_U = 40960
FL_P = 106496
FL_T1 = 139264
FL_T2 = 143360
FL_R = 147456
FL_PT = 151552
FL_AO = 155648
FL_AOT = 157696


class Builder:
    def __init__(self, nc):
        self.nc = nc
        self.q = {e: [] for e in ENGINES}
        self.sems = {}
        self.cnt = {}
        self.hist = {}
        self._slots = {}
        self._slot_next = OFF_SLOTS

    def sem(self, name):
        if name not in self.sems:
            self.sems[name] = self.nc.alloc_semaphore(name)
            self.cnt[name] = 0
            self.hist[name] = []
        return self.sems[name]

    def c(self, name):
        self.sem(name)
        return self.cnt[name]

    def n(self, name):
        self.sem(name)
        return len(self.hist[name])

    def h(self, name, k):
        self.sem(name)
        if k <= 0:
            return 0
        hist = self.hist[name]
        assert k <= len(hist), (name, k, len(hist))
        return hist[k - 1]

    SERIAL = {"vector": "ser_v", "scalar": "ser_s", "gpsimd": "ser_g"}

    def emit(self, eng, fn, waits=(), inc=None, amt=1):
        waits = tuple(w for w in waits if w is not None)
        for s, _ in waits:
            self.sem(s)
        if inc is not None and amt > 0:
            self.sem(inc)
            self.cnt[inc] += amt
            self.hist[inc].append(self.cnt[inc])
        else:
            inc = None
        ser = self.SERIAL.get(eng)
        prev_tok = None
        my_inc = inc
        if ser is not None:
            prev_tok = getattr(self, "_tok", {}).get(eng)
            if not hasattr(self, "_tok"):
                self._tok = {}
            if inc is not None:
                self._tok[eng] = (inc, self.cnt[inc])
            else:
                self.sem(ser)
                self.cnt[ser] += 1
                self.hist[ser].append(self.cnt[ser])
                my_inc = ser
                amt = 1
                self._tok[eng] = (ser, self.cnt[ser])
        sems = self.sems

        def run(engine, fn=fn, waits=waits, inc=my_inc, amt=amt,
                prev_tok=prev_tok):
            if prev_tok is not None and prev_tok[1] > 0:
                engine.wait_ge(sems[prev_tok[0]], prev_tok[1])
            for s, v in waits:
                if v > 0:
                    engine.wait_ge(sems[s], v)
            ins = fn(engine)
            if inc is not None:
                ins.then_inc(sems[inc], amt)

        self.q[eng].append(run)

    def wait_only(self, eng, waits):
        waits = tuple(w for w in waits if w is not None and w[1] > 0)
        if not waits:
            return
        for s, _ in waits:
            self.sem(s)
        sems = self.sems

        def run(engine, waits=waits):
            for s, v in waits:
                engine.wait_ge(sems[s], v)

        self.q[eng].append(run)

    def dma(self, eng, out, in_, inc, waits=(), **kw):
        self.emit(eng, lambda e, o=out, i=in_, kw=kw: e.dma_start(out=o, in_=i, **kw),
                  waits=waits, inc=inc, amt=16)

    def dma2(self, eng, out, in_, fam, idx, waits=(), **kw):
        self.dma(eng, out, in_, f"{fam}{idx % 2}", waits=waits, **kw)

    def dtot(self, fam, idx):
        name = f"{fam}{idx % 2}"
        return (name, self.c(name))

    def dboth(self, fam):
        return [(f"{fam}0", self.c(f"{fam}0")), (f"{fam}1", self.c(f"{fam}1"))]

    def slot(self, name, shape=(128, 1)):
        if name not in self._slots:
            nbytes = int(np.prod(shape[1:])) * 4
            nbytes = (nbytes + 31) // 32 * 32
            off = self._slot_next
            assert off + nbytes <= OFF_XT, (name, off)
            self._slot_next = off + nbytes
            self._slots[name] = self.nc.alloc_sbuf_tensor_at(
                "slot_" + name, list(shape), F32, offset=off)
        return self._slots[name]


def flat(handle, n, off=0):
    return bass.AP(handle, off, [[1, 1], [1, n]])


def build_program(T=4096):
    nc = bass.Bass("TRN2", target_bir_lowering=False, num_devices=8,
                   dynamic_dma_scratch_size=2048)
    nt = T // 128
    ng = T // 512
    S = 2 * T
    nkc = S // 512
    nsb = T // 256

    xin = nc.dram_tensor("xin", [T, D], F32, kind="ExternalInput")
    wqTD = nc.dram_tensor("wqT", [D, D], F32, kind="ExternalInput")
    wkTD = nc.dram_tensor("wkT", [D, D], F32, kind="ExternalInput")
    wvTD = nc.dram_tensor("wvT", [D, D], F32, kind="ExternalInput")
    woTD = nc.dram_tensor("woT", [D, D], F32, kind="ExternalInput")
    bqpD = nc.dram_tensor("bqp", [128, 8], F32, kind="ExternalInput")
    bkpD = nc.dram_tensor("bkp", [128, 8], F32, kind="ExternalInput")
    bvrD = nc.dram_tensor("bvr", [1, D], F32, kind="ExternalInput")
    borD = nc.dram_tensor("bor", [1, D], F32, kind="ExternalInput")
    wgD = nc.dram_tensor("wg", [I, D], F32, kind="ExternalInput")
    wuD = nc.dram_tensor("wu", [I, D], F32, kind="ExternalInput")
    wdD = nc.dram_tensor("wd", [D, I], F32, kind="ExternalInput")
    clipD = nc.dram_tensor("clip", [1, 2], F32, kind="ExternalInput")
    identD = nc.dram_tensor("ident", [128, 128], BF16, kind="ExternalInput")
    xout = nc.dram_tensor("xout", [T, D], F32, kind="ExternalOutput")

    xnTD = nc.dram_tensor("xnT", [D, T], BF16)
    hDD = nc.dram_tensor("hd", [I, T], BF16)
    x1D = nc.dram_tensor("x1", [T, D], F32)
    x2D = nc.dram_tensor("x2", [T, D], F32)
    qtD = nc.dram_tensor("qt", [D, T], BF16)
    ktoD = nc.dram_tensor("kto", [D, T], BF16)
    ktgD = nc.dram_tensor("ktg", [2 * D, T], BF16)
    voD = nc.dram_tensor("vo", [T, D], BF16)
    vgD = nc.dram_tensor("vg", [2 * T, D], BF16)
    wgS = nc.dram_tensor("wgs", [D, I], BF16)
    wuS = nc.dram_tensor("wus", [D, I], BF16)
    wdS = nc.dram_tensor("wds", [I, D], BF16)
    scrD = nc.dram_tensor("scr", [1, 128], F32)
    ccB = [nc.dram_tensor(f"cc{j}", [1, 8], F32) for j in range(4)]

    A = nc.alloc_sbuf_tensor_at
    ident = A("ident", [128, 128], BF16, offset=OFF_IDENT)
    identF = A("identf", [128, 128], F32, offset=18944)
    bq_pp = A("bq_pp", [128, 8], F32, offset=OFF_BQ)
    bk_pp = A("bk_pp", [128, 8], F32, offset=OFF_BK)
    bv_row = A("bv_row", [1, D], BF16, offset=OFF_BVROW)
    bo_row = A("bo_row", [1, D], BF16, offset=OFF_BOROW)
    ones_b = A("ones_b", [1, 128], BF16, offset=OFF_ONESB)
    ones_f = A("ones_f", [1, 128], F32, offset=OFF_ONESF)
    clip_sb = A("clip_sb", [1, 2], F32, offset=OFF_CLIP)
    lnst = A("lnst", [128, 12], F32, offset=OFF_LNST)
    row128 = A("row128", [1, 128], F32, offset=OFF_ROW128)
    rs_g = A("rs_g", [128, IC], F32, offset=OFF_RSG)
    rs_u = A("rs_u", [128, IC], F32, offset=OFF_RSU)
    csg = A("csg", [128, IC], F32, offset=OFF_CSG)
    csu = A("csu", [128, IC], F32, offset=OFF_CSU)
    xt = [A(f"xt{k}", [128, D], F32, offset=OFF_XT + k * 4096) for k in range(2)]
    outt = [A(f"outt{k}", [128, D], F32, offset=OFF_OUTT + k * 4096) for k in range(2)]
    xnbf = [A(f"xnbf{k}", [128, D], BF16, offset=OFF_XNBF + k * 2048) for k in range(2)]
    tevac = [A(f"tevac{k}", [128, 8, 128], BF16, offset=OFF_TEVAC + k * 2048)
             for k in range(2)]
    wo_sb = A("wo_sb", [128, 8, D], BF16, offset=OFF_WO)
    wgT = A("wgT", [128, 8, I], BF16, offset=OFF_W + FO_WG)
    wuT = A("wuT", [128, 8, I], BF16, offset=OFF_W + FO_WU)
    wdT = A("wdT", [128, IC, D], BF16, offset=OFF_W + FO_WD)
    xq = [A(f"xq{k}", [128, 8, 512], BF16, offset=OFF_W + FO_XQ + k * 8192)
          for k in range(2)]
    hev = [A(f"hev{k}", [128, 512], BF16, offset=OFF_W + FO_HEV + k * 1024)
           for k in range(2)]
    silu = [A(f"silu{k}", [128, 512], F32, offset=OFF_W + FO_SILU + k * 2048)
            for k in range(2)]
    hq = [A(f"hq{k}", [128, IC, 128], BF16, offset=OFF_W + FO_HQ + k * 5632)
          for k in range(2)]
    hqfs = [A("hqf0", [128, 1408], F32, offset=OFF_W + FO_HQF),
            A("hqf1", [128, 1408], F32, offset=OFF_W + FO_XQ)]
    qscs = [A("qsc0", [128, 512], F32, offset=OFF_W + FO_HQF),
            A("qsc1", [128, 512], F32, offset=OFF_W + FO_HQF + 2048)]
    p0stg = A("p0stg", [128, I], F32, offset=OFF_W + FO_P0STG)
    awq = A("awq", [128, 8, D], BF16, offset=OFF_W + AO_WQ)
    awk = A("awk", [128, 8, D], BF16, offset=OFF_W + AO_WK)
    awv = A("awv", [128, 8, D], BF16, offset=OFF_W + AO_WV)
    xn2 = [A(f"xn2{k}", [128, 8, 512], BF16, offset=OFF_W + AO_XN2 + k * 8192)
           for k in range(2)]
    kev = [A(f"kev{k}", [128, 512], BF16, offset=OFF_W + AO_KEV + k * 1024)
           for k in range(2)]
    vev = [A(f"vev{k}", [128, D], BF16, offset=OFF_W + AO_VEV + k * 2048)
           for k in range(2)]
    wst = [A(f"wst{k}", [128, D], F32, offset=OFF_W + AO_WST + k * 4096)
           for k in range(2)]
    qtb = [A(f"qtb{k}", [128, 8, 256], BF16, offset=OFF_W + FL_QTB + k * 4096)
           for k in range(2)]
    qtb3 = A("qtb2x", [128, 8, 256], BF16, offset=OFF_W + FL_AOT + 2048)
    ktb = [A(f"ktb{k}", [128, 8, 512], BF16, offset=OFF_W + FL_KTB + k * 8192)
           for k in range(2)]
    vtb = [A(f"vtb{k}", [128, 4, D], BF16, offset=OFF_W + FL_VTB + k * 8192)
           for k in range(2)]
    Ubuf = [A(f"ubuf{k}", [128, S], F32, offset=OFF_W + FL_U + k * 32768)
            for k in range(2)]
    Pbuf = [A(f"pbuf{k}", [128, S], BF16, offset=OFF_W + FL_P + k * 16384)
            for k in range(2)]
    t1s = [A(f"t1s{k}", [128, 512], F32,
             offset=(OFF_W + FL_T1 + k * 2048 if k < 2 else OFF_W + 167936))
           for k in range(3)]
    t2s = [A(f"t2s{k}", [128, 512], F32,
             offset=(OFF_W + FL_T2 + k * 2048 if k < 2 else OFF_W + 169984))
           for k in range(3)]
    rsc = [A(f"rsc{k}", [128, 512], F32,
             offset=(OFF_W + FL_R + k * 2048 if k < 2 else OFF_W + 172032))
           for k in range(3)]
    ptsb = [A(f"ptsb{k}", [128, 8, 128], BF16, offset=OFF_W + FL_PT + k * 2048)
            for k in range(2)]
    ao_sb2 = [A(f"ao_sb{k}", [128, D], BF16,
                offset=OFF_W + FL_AOT + 6144 + k * 2048) for k in range(2)]
    aoT_sb2 = [A(f"aoT_sb{k}", [128, 8, 128], BF16,
                 offset=OFF_W + FL_AO + k * 2048) for k in range(2)]
    lrp = [A(f"lrp{k}", [128, 8], F32, offset=OFF_W + 174080 + k * 32)
           for k in range(2)]

    p0 = nc.alloc_psum_tensor("p0", [128, 512], F32)
    p1 = nc.alloc_psum_tensor("p1", [128, 512], F32)
    pacc0 = nc.alloc_psum_tensor("pacc0", [128, D], F32)
    pacc1 = nc.alloc_psum_tensor("pacc1", [128, D], F32)
    pt0 = nc.alloc_psum_tensor("pt0", [128, 8, 128], BF16)
    pt1 = nc.alloc_psum_tensor("pt1", [128, 8, 128], BF16)
    pts = [pt0, pt1]
    paccs = [pacc0, pacc1]

    B = Builder(nc)
    sl = B.slot

    def xpart_request(acc_slot, ccb, pre_waits, mm_extra, tag):
        B.emit("tensor", lambda e, acc_slot=acc_slot: e.matmul(
            p0[0:1, 0:128], acc_slot[:, 0:1], identF.ap(),
            start=True, stop=True),
            waits=list(pre_waits) + list(mm_extra) + [("p0_ones", 6)],
            inc="cc_bc")
        B.emit("vector", lambda e: e.tensor_reduce(
            sl("ccrow", (1, 8))[0:1, 0:1], p0[0:1, 0:128], AX.X, ALU.max),
            waits=[("cc_bc", B.c("cc_bc"))], inc="cc_rm")
        B.dma("sync", ccb.ap()[0:1, :], sl("ccrow", (1, 8))[0:1, :], "cc_d",
              waits=[("cc_rm", B.c("cc_rm"))])
        B.emit("gpsimd", lambda e, ccb=ccb: e.collective_compute(
            "AllReduce", ALU.max, replica_groups=[[0, 1, 2, 3, 4, 5, 6, 7]],
            ins=[flat(ccb, 8)], outs=[flat(ccb, 8)]),
            waits=[("cc_d", B.c("cc_d"))], inc="cc_cc")
        B.dma("sync", sl(tag + "_cr")[0:1, 0:1], ccb.ap()[0:1, 0:1], "cc_d",
              waits=[("cc_cc", B.c("cc_cc"))])

    def xpart_complete(out_slot, tag):
        B.emit("tensor", lambda e, tag=tag: e.matmul(
            p0[:, 0:1], ones_f[0:1, :], sl(tag + "_cr")[0:1, 0:1],
            start=True, stop=True),
            waits=[("cc_d", B.c("cc_d"))], inc="cc_bc")
        B.emit("vector", lambda e, out_slot=out_slot: e.tensor_copy(
            out_slot[:, 0:1], p0[:, 0:1]),
            waits=[("cc_bc", B.c("cc_bc"))], inc="cc_done")
        return "cc_done"

    def xpart_max_allreduce(acc_slot, ccb, out_slot, pre_waits, mm_extra, tag):
        xpart_request(acc_slot, ccb, pre_waits, mm_extra, tag)
        return xpart_complete(out_slot, tag)

    deferred_stores = []

    def emit_setup():
        B.dma("sync", ident.ap(), identD.ap(), "p0_misc")
        B.dma("sync", bq_pp.ap(), bqpD.ap(), "p0_misc")
        B.dma("sync", bk_pp.ap(), bkpD.ap(), "p0_misc")
        B.dma("sync", clip_sb.ap(), clipD.ap(), "p0_misc")
        B.emit("vector", lambda e: e.memset(ones_b.ap(), 1.0), inc="p0_ones")
        B.emit("vector", lambda e: e.memset(ones_f.ap(), 1.0), inc="p0_ones")
        B.emit("vector", lambda e: e.memset(sl("epsb")[:, 0:1], EPS), inc="p0_ones")
        B.emit("vector", lambda e: e.memset(sl("ccrow", (1, 8))[0:1, :], 0.0),
               inc="p0_ones")
        B.emit("vector", lambda e: e.memset(sl("zb")[:, 0:1], 0.0),
               inc="p0_ones")
        B.dma("sync", wst[0][0:1, :], bvrD.ap(), "p0_bv")
        B.emit("vector", lambda e: e.tensor_copy(bv_row.ap(), wst[0][0:1, :]),
               waits=[("p0_bv", 16)], inc="p0_bvc")
        B.dma("sync", wst[1][0:1, :], borD.ap(), "p0_bv",
              waits=[("p0_bvc", 1)])
        B.emit("vector", lambda e: e.tensor_copy(bo_row.ap(), wst[1][0:1, :]),
               waits=[("p0_bv", 32)], inc="p0_bvc")
        B.emit("scalar", lambda e: e.activation(
            out=identF.ap(), in_=ident.ap(), func=AF.Identity,
            scale=1.0, bias=sl("zb")[:, 0:1]),
            waits=[("p0_misc", 64), ("p0_ones", 5)], inc="p0_ones")
        B.emit("scalar", lambda e: e.activation(
            out=sl("sgtmp", (1, 2))[0:1, :], in_=clip_sb[0:1, :], func=AF.Tanh,
            scale=0.5, bias=sl("zb")[0:1, 0:1]),
            waits=[("p0_misc", 64), ("p0_ones", 5)], inc="p0_sg")
        B.emit("vector", lambda e: e.tensor_scalar(
            sl("sgrow", (1, 2))[0:1, :], sl("sgtmp", (1, 2))[0:1, :],
            0.5, 0.5, ALU.mult, ALU.add), waits=[("p0_sg", 1)], inc="p0_sg")
        B.emit("tensor", lambda e: e.matmul(
            p1[:, 0:2], ones_f[0:1, :], sl("sgrow", (1, 2))[0:1, :],
            start=True, stop=True),
            waits=[("p0_sg", 2), ("p0_ones", 2)], inc="p0_sg")
        B.emit("vector", lambda e: e.tensor_copy(
            sl("sigc", (128, 2))[:, :], p1[:, 0:2]),
            waits=[("p0_sg", 3)], inc="p0_sg")
        for j in range(DC):
            par = j % 2
            B.dma2("sync", wst[par].ap(),
                   bass.AP(woTD, j * 128 * D, [[D, 128], [1, D]]), "aw_ld", j,
                   waits=[("aw_c", B.h("aw_c", B.n("aw_c") - 1)),
                          ("p0_bvc", 2)])
            B.emit("vector", lambda e, j=j, par=par: e.tensor_copy(
                wo_sb[:, j, :], wst[par].ap()),
                waits=[B.dtot("aw_ld", j)], inc="aw_c")

        def quant_w(src, rs_tile, dstT, stash, nrt_, fold_back, ncol):
            nchk = ncol // 128
            gpt = (nchk + 7) // 8
            nbuf = 2 if ncol == D else 1
            raws = [A(f"qraw{k}_" + stash.name, [128, ncol], F32,
                      offset=OFF_W + FO_XQ + k * 4096) for k in range(nbuf)]
            stgs = [A(f"qstg{k}_" + stash.name, [128, ncol], F32,
                      offset=OFF_W + FO_P0STG + k * 4096) for k in range(nbuf)]
            wints = [A(f"qwint{k}_" + stash.name, [128, ncol], BF16,
                       offset=OFF_W + FO_P0WI + k * 2048) for k in range(nbuf)]
            n_sc = B.n("q_sc")
            n_cl = B.n("q_cl")
            n_tr = B.n("q_tr")
            n_ev = B.n("q_ev")

            def stage_a(t):
                raw = raws[t % nbuf]
                stg = stgs[t % nbuf]
                p = t % 2
                B.dma2("sync", raw.ap(),
                       bass.AP(src, t * 128 * ncol, [[ncol, 128], [1, ncol]]),
                       "q_ld", t % nbuf,
                       waits=[("q_sc", B.h("q_sc", n_sc + t + 1 - nbuf))])
                B.emit("vector", lambda e, raw=raw, p=p: e.tensor_reduce(
                    sl(f"q_rmax{p}")[:, 0:1], raw.ap(), AX.X, ALU.max,
                    apply_absolute_value=True),
                    waits=[B.dtot("q_ld", t % nbuf)], inc="q_rm")
                B.emit("vector", lambda e, p=p: e.reciprocal(
                    sl(f"q_rinv{p}")[:, 0:1], sl(f"q_rmax{p}")[:, 0:1]),
                    waits=[("q_rm", B.c("q_rm"))], inc="q_rm")
                B.emit("vector", lambda e, p=p: e.tensor_scalar(
                    sl(f"q_r127{p}")[:, 0:1], sl(f"q_rinv{p}")[:, 0:1], 127.0,
                    None, ALU.mult), waits=[("q_rm", B.c("q_rm"))], inc="q_rm")
                if rs_tile is not None:
                    B.emit("vector", lambda e, t=t, rs_tile=rs_tile, p=p:
                           e.tensor_scalar(
                               rs_tile[:, t:t + 1], sl(f"q_rmax{p}")[:, 0:1],
                               1.0 / 127.0, None, ALU.mult), inc="q_rm")
                B.emit("scalar", lambda e, raw=raw, stg=stg, ncol=ncol, p=p:
                       e.activation(
                           out=stg[:, 0:ncol], in_=raw.ap(), func=AF.Identity,
                           scale=sl(f"q_r127{p}")[:, 0:1],
                           bias=sl("zb")[:, 0:1]),
                       waits=[("q_rm", B.c("q_rm")),
                              ("q_cl", B.h("q_cl", n_cl + t + 1 - nbuf))],
                       inc="q_sc")

            def stage_b(t):
                stg = stgs[t % nbuf]
                wint = wints[t % nbuf]
                p = t % 2
                wint_wait = ("q_tr", B.h("q_tr", n_tr + (t + 1 - nbuf) * gpt))
                if fold_back:
                    B.emit("vector", lambda e, stg=stg, ncol=ncol: e.tensor_scalar(
                        stg[:, 0:ncol], stg[:, 0:ncol], RND, RND, ALU.add,
                        ALU.subtract),
                        waits=[("q_sc", B.h("q_sc", n_sc + t + 1))], inc="q_rd")
                    B.emit("vector", lambda e, wint=wint, stg=stg, ncol=ncol, p=p:
                           e.tensor_scalar(
                               wint.ap(), stg[:, 0:ncol],
                               sl(f"q_rmax{p}")[:, 0:1],
                               1.0 / 127.0, ALU.mult, ALU.mult),
                           waits=[("q_rd", B.c("q_rd")), wint_wait], inc="q_cl")
                else:
                    B.emit("vector", lambda e, wint=wint, stg=stg, ncol=ncol:
                           e.tensor_scalar(
                               wint.ap(), stg[:, 0:ncol], RND, RND, ALU.add,
                               ALU.subtract),
                           waits=[("q_sc", B.h("q_sc", n_sc + t + 1)),
                                  wint_wait], inc="q_cl")
                for gi in range(gpt):
                    j0 = gi * 8
                    jn = min(8, nchk - j0)
                    gg = t * gpt + gi
                    par = gg % 2
                    for jj in range(jn):
                        B.emit("tensor",
                               lambda e, par=par, jj=jj, j0=j0, wint=wint:
                               e.transpose(
                                   pts[par][:, jj, :],
                                   wint[:, (j0 + jj) * 128:(j0 + jj + 1) * 128],
                                   ident.ap()),
                               waits=([("q_cl", B.c("q_cl")),
                                       ("q_ev", B.h("q_ev", n_ev + gg - 1))]
                                      if jj == 0 else ()),
                               inc=("q_tr" if jj == jn - 1 else None))
                    B.emit("scalar",
                           lambda e, par=par, j0=j0, jn=jn, t=t, dstT=dstT:
                           e.copy(
                               dstT[:, j0:j0 + jn, t * 128:(t + 1) * 128],
                               pts[par][:, 0:jn, :]),
                           waits=[("q_tr", B.c("q_tr"))], inc="q_ev")

            if nbuf == 2:
                stage_a(0)
                for t in range(nrt_):
                    if t + 1 < nrt_:
                        stage_a(t + 1)
                    stage_b(t)
            else:
                for t in range(nrt_):
                    stage_a(t)
                    stage_b(t)

            def stores(nchk=nchk, stash=stash, dstT=dstT, nrt_=nrt_,
                       ev=B.c("q_ev")):
                for j in range(nchk):
                    B.dma("sync", bass.AP(stash, j * 128 * nrt_ * 128,
                                          [[nrt_ * 128, 128],
                                           [1, nrt_ * 128]]),
                          dstT[:, j, :], "q_st", waits=[("q_ev", ev)])
            deferred_stores.append(stores)

        quant_w(wgD, rs_g, wgT, wgS, IC, False, D)
        quant_w(wuD, rs_u, wuT, wuS, IC, False, D)
        return B.c("q_ev"), quant_w

    class LNPass:

        def __init__(self, track_slot, act_norm=False):
            self.n_rc = B.n("l_rc"); self.n_nm = B.n("l_nm")
            self.n_tr = B.n("l_tr"); self.n_ev = B.n("l_ev")
            self.n_sq = B.n("l_sq")
            self.track = track_slot
            self.act_norm = act_norm
            if track_slot:
                B.emit("vector", lambda e, t=track_slot: e.memset(
                    sl(t)[:, 0:1], 0.0), inc="l_mx")

        def stats(self, i, src_ap, gates):
            p = i % 2
            B.emit("vector", lambda e, src_ap=src_ap, p=p: e.bn_stats(
                sl(f"lnst{p}", (128, 12))[:, 0:6], src_ap[:, 0:512]),
                waits=list(gates) + [("l_sq", B.h("l_sq", self.n_sq + i - 1))])
            B.emit("vector", lambda e, src_ap=src_ap, p=p: e.bn_stats(
                sl(f"lnst{p}", (128, 12))[:, 6:12], src_ap[:, 512:]))
            B.emit("vector", lambda e, p=p: e.bn_aggr(
                sl(f"lnmv{p}", (128, 2))[:, :], sl(f"lnst{p}", (128, 12)).ap()),
                inc="l_st")
            B.emit("scalar", lambda e, p=p: e.activation(
                out=sl(f"rstd{p}")[:, 0:1], in_=sl(f"lnmv{p}", (128, 2))[:, 1:2],
                func=AF.Sqrt, bias=sl("epsb")[:, 0:1]),
                waits=[("l_st", B.c("l_st"))], inc="l_sq")

        def norm(self, i, src_ap):
            par = i % 2
            p = i % 2
            B.emit("vector", lambda e, p=p: e.reciprocal(
                sl(f"rstd2{p}")[:, 0:1], sl(f"rstd{p}")[:, 0:1]),
                waits=[("l_sq", B.h("l_sq", self.n_sq + i + 1))], inc="l_rc")
            if self.act_norm:
                B.emit("vector", lambda e, p=p: e.tensor_scalar(
                    sl(f"lnnb{p}")[:, 0:1], sl(f"lnmv{p}", (128, 2))[:, 0:1],
                    sl(f"rstd2{p}")[:, 0:1], -1.0, ALU.mult, ALU.mult),
                    waits=[("l_rc", B.c("l_rc"))], inc="l_rc")
                B.emit("scalar", lambda e, par=par, src_ap=src_ap, p=p:
                       e.activation(
                           out=xnbf[par].ap(), in_=src_ap, func=AF.Identity,
                           scale=sl(f"rstd2{p}")[:, 0:1],
                           bias=sl(f"lnnb{p}")[:, 0:1]),
                       waits=[("l_rc", B.c("l_rc")),
                              ("l_tr", B.h("l_tr", self.n_tr + i - 1))],
                       inc="l_nm")
            else:
                B.emit("vector", lambda e, par=par, src_ap=src_ap, p=p:
                       e.tensor_scalar(
                           xnbf[par].ap(), src_ap,
                           sl(f"lnmv{p}", (128, 2))[:, 0:1],
                           sl(f"rstd2{p}")[:, 0:1], ALU.subtract, ALU.mult),
                       waits=[("l_rc", B.c("l_rc")),
                              ("l_tr", B.h("l_tr", self.n_tr + i - 1))],
                       inc="l_nm")
            if self.track:
                B.emit("vector", lambda e, par=par: e.tensor_reduce(
                    sl("tmax")[:, 0:1], xnbf[par].ap(), AX.X, ALU.max,
                    apply_absolute_value=True),
                    waits=([("l_nm", B.c("l_nm"))] if self.act_norm else ()))
                B.emit("vector", lambda e, t=self.track: e.tensor_max(
                    sl(t)[:, 0:1], sl(t)[:, 0:1], sl("tmax")[:, 0:1]),
                    inc="l_mx")
            for j in range(DC):
                B.emit("tensor", lambda e, par=par, j=j: e.transpose(
                    pts[par][:, j, :], xnbf[par][:, j * 128:(j + 1) * 128],
                    ident.ap()),
                    waits=([("l_nm", B.c("l_nm")),
                            ("l_ev", B.h("l_ev", self.n_ev + i - 1))]
                           if j == 0 else ()),
                    inc=("l_tr" if j == DC - 1 else None))
            B.emit("scalar", lambda e, par=par: e.copy(
                tevac[par].ap(), pts[par].ap()),
                waits=[("l_tr", B.c("l_tr")), B.dtot("l_xn", i)],
                inc="l_ev")
            B.dma2("sync",
                   bass.AP(xnTD, i * 128, [[T, 128], [128 * T, 8], [1, 128]]),
                   tevac[par].ap(), "l_xn", i, waits=[("l_ev", B.c("l_ev"))])

        def tile(self, i, src_ap, gates):
            self.stats(i, src_ap, gates)
            self.norm(i, src_ap)

        def norm_done(self, i):
            return ("l_nm", B.h("l_nm", self.n_nm + i + 1))

    def emit_ln_dram(src, track_slot, entry_waits=()):
        ln = LNPass(track_slot, act_norm=True)
        n_nm0 = ln.n_nm

        def load(i, w0=()):
            par = i % 2
            w = [("l_nm", B.h("l_nm", n_nm0 + i - 1))] + list(w0)
            B.dma2("sync", xt[par].ap(),
                   bass.AP(src, i * 128 * D, [[D, 128], [1, D]]), "l_ld", i,
                   waits=w)
            ln.stats(i, xt[par].ap(), [B.dtot("l_ld", i)])

        load(0, entry_waits)
        for i in range(nt):
            if i + 1 < nt:
                load(i + 1)
            ln.norm(i, xt[i % 2].ap())

    def emit_scales_x(Mslot):
        B.emit("vector", lambda e, M=Mslot: e.tensor_mul(
            sl("w_sx")[:, 0:1], M[:, 0:1], sl("sigc", (128, 2))[:, 0:1]), inc="sc")
        B.emit("vector", lambda e: e.reciprocal(
            sl("w_si")[:, 0:1], sl("w_sx")[:, 0:1]),
            waits=[("sc", B.c("sc"))], inc="sc")
        B.emit("vector", lambda e: e.tensor_scalar(
            sl("inv_sx")[:, 0:1], sl("w_si")[:, 0:1], 127.0, None, ALU.mult),
            waits=[("sc", B.c("sc"))], inc="sc")
        B.emit("vector", lambda e: e.tensor_scalar(
            sl("w_ss")[:, 0:1], sl("w_sx")[:, 0:1], 1.0 / 127.0, None, ALU.mult),
            waits=[("sc", B.c("sc"))], inc="sc")
        B.emit("vector", lambda e: e.tensor_scalar(
            csg.ap(), rs_g.ap(), sl("w_ss")[:, 0:1], None, ALU.mult),
            waits=[("sc", B.c("sc"))], inc="sc")
        B.emit("vector", lambda e: e.tensor_scalar(
            csu.ap(), rs_u.ap(), sl("w_ss")[:, 0:1], None, ALU.mult),
            waits=[("sc", B.c("sc"))], inc="sc")

    def emit_scales_h(Mslot):
        B.emit("vector", lambda e, M=Mslot: e.tensor_mul(
            sl("w_sd")[:, 0:1], M[:, 0:1], sl("sigc", (128, 2))[:, 1:2]), inc="sc")
        B.emit("vector", lambda e: e.reciprocal(
            sl("w_di")[:, 0:1], sl("w_sd")[:, 0:1]),
            waits=[("sc", B.c("sc"))], inc="sc")
        B.emit("vector", lambda e: e.tensor_scalar(
            sl("inv_sd")[:, 0:1], sl("w_di")[:, 0:1], 127.0, None, ALU.mult),
            waits=[("sc", B.c("sc"))], inc="sc")
        B.emit("vector", lambda e: e.tensor_scalar(
            sl("half_sd")[:, 0:1], sl("w_sd")[:, 0:1], 0.5 / 127.0, None, ALU.mult),
            waits=[("sc", B.c("sc"))], inc="sc")

    def emit_ffn_b(hamax_slot):
        n_q3 = B.n("b_q3"); n_mg = B.n("b_mg")
        n_mu = B.n("b_mu"); n_si = B.n("b_si"); n_h = B.n("b_h")
        n_hv = B.n("b_hv")
        xn_waits = B.dboth("l_xn")
        sc_val = B.c("sc")
        def quant_chunk(g, j):
            par = g % 2
            m = g * DC + j
            qp = m % 2
            B.emit("scalar", lambda e, par=par, j=j, qp=qp: e.activation(
                out=qscs[qp].ap(), in_=xq[par][:, j, :], func=AF.Identity,
                scale=sl("inv_sx")[:, 0:1], bias=sl("zb")[:, 0:1]),
                waits=[B.dtot("b_ld", g), ("sc", sc_val),
                       ("b_q3", B.h("b_q3", n_q3 + m - 1))],
                inc="b_q1")
            B.emit("vector", lambda e, qp=qp: e.tensor_scalar(
                qscs[qp].ap(), qscs[qp].ap(), RND, RND, ALU.add, ALU.subtract),
                waits=[("b_q1", B.c("b_q1"))], inc="b_q2")
            B.emit("vector", lambda e, par=par, j=j, qp=qp: e.tensor_scalar(
                xq[par][:, j, :], qscs[qp].ap(), -128.0, 127.0, ALU.max,
                ALU.min), waits=[("b_q2", B.c("b_q2"))], inc="b_q3")

        qsc_done = B.c("q_sc")
        def load_group(g):
            par = g % 2
            w = [("b_mu", B.h("b_mu", n_mu + (g - 1) * IC))] + xn_waits
            if g == 0:
                w = w + [("q_sc", qsc_done)]
            for hc in range(DC):
                B.dma2("sync", xq[par][:, hc, :],
                       bass.AP(xnTD, hc * 128 * T + g * 512,
                               [[T, 128], [1, 512]]),
                       "b_ld", g, waits=(w if hc == 0 else ()))

        load_group(0)
        for j in range(DC):
            quant_chunk(0, j)
        for g in range(ng):
            par = g % 2
            q3_need = B.c("b_q3")
            if g + 1 < ng:
                load_group(g + 1)
            for ic in range(IC):
                k = g * IC + ic
                pi = ic % 2
                pg = p0.ap() if pi == 0 else pacc0[:, 0:512]
                pu = p1.ap() if pi == 0 else pacc0[:, 512:]
                wgq = [("b_q3", q3_need), ("b_si", B.h("b_si", n_si + k - 1))]
                wuq = [("b_h", B.h("b_h", n_h + k - 1))]
                for hc in range(DC):
                    B.emit("tensor", lambda e, par=par, hc=hc, ic=ic, pg=pg:
                           e.matmul(pg, wgT[:, hc, ic * 128:(ic + 1) * 128],
                                    xq[par][:, hc, :], start=(hc == 0),
                                    stop=(hc == DC - 1)),
                           waits=(wgq if hc == 0 else ()),
                           inc=("b_mg" if hc == DC - 1 else None))
                for hc in range(DC):
                    B.emit("tensor", lambda e, par=par, hc=hc, ic=ic, pu=pu:
                           e.matmul(pu, wuT[:, hc, ic * 128:(ic + 1) * 128],
                                    xq[par][:, hc, :], start=(hc == 0),
                                    stop=(hc == DC - 1)),
                           waits=(wuq if hc == 0 else ()),
                           inc=("b_mu" if hc == DC - 1 else None))
                B.emit("scalar", lambda e, pg=pg, pi=pi, ic=ic: e.activation(
                    out=silu[pi].ap(), in_=pg, func=AF.Silu,
                    scale=csg[:, ic:ic + 1], bias=sl("zb")[:, 0:1]),
                    waits=[("b_mg", B.c("b_mg")), ("b_h", B.h("b_h", n_h + k - 1))],
                    inc="b_si")
                B.emit("vector", lambda e, pu=pu, pi=pi, ic=ic:
                       e.scalar_tensor_tensor(
                           hev[pi].ap(), pu, csu[:, ic:ic + 1], silu[pi].ap(),
                           ALU.mult, ALU.mult),
                       waits=[("b_mu", B.c("b_mu")), ("b_si", B.c("b_si")),
                              B.dtot("b_hd", k),
                              ("b_hv", B.h("b_hv", n_hv + k - 1))], inc="b_h")
                B.emit("vector", lambda e, pi=pi, k=k: e.tensor_reduce(
                    sl("hacc", (128, 176))[:, k:k + 1], hev[pi].ap(), AX.X,
                    ALU.max, apply_absolute_value=True), inc="b_hv")
                B.dma2("sync", bass.AP(hDD, ic * 128 * T + g * 512,
                                       [[T, 128], [1, 512]]),
                       hev[pi].ap(), "b_hd", k, waits=[("b_h", B.c("b_h"))])
                if g + 1 < ng and 6 <= ic < 6 + DC:
                    quant_chunk(g + 1, ic - 6)
        B.emit("vector", lambda e, t=hamax_slot, n=ng * IC: e.tensor_reduce(
            sl(t)[:, 0:1], sl("hacc", (128, 176))[:, 0:n], AX.X, ALU.max),
            inc="b_hm")

    def emit_ffn_c(res_src, xdst, first_mm_waits=(), tile_hook=None,
                   outt_guard=None):
        n_q3 = B.n("c_q3"); n_mm = B.n("c_mm")
        n_ev = B.n("c_ev")
        sc_val = B.c("sc")

        def tile_front(tt):
            par = tt % 2
            w = [("c_mm", B.h("c_mm", n_mm + tt - 1))]
            for half in range(2):
                ic0 = 11 * half
                B.dma2("sync", hq[par][:, ic0:ic0 + 11, :],
                       bass.AP(hDD, ic0 * 128 * T + tt * 128,
                               [[T, 128], [128 * T, 11], [1, 128]]),
                       "c_ld", tt, waits=(w if half == 0 else ()))
            for half in range(2):
                ic0 = 11 * half
                m = 2 * tt + half
                qp = m % 2
                B.emit("scalar", lambda e, par=par, ic0=ic0, qp=qp: e.activation(
                    out=hqfs[qp].ap(), in_=hq[par][:, ic0:ic0 + 11, :],
                    func=AF.Identity, scale=sl("inv_sd")[:, 0:1],
                    bias=sl("zb")[:, 0:1]),
                    waits=[B.dtot("c_ld", tt), ("sc", sc_val),
                           ("c_q3", B.h("c_q3", n_q3 + m - 1))],
                    inc="c_q1")
                B.emit("vector", lambda e, qp=qp: e.tensor_scalar(
                    hqfs[qp].ap(), hqfs[qp].ap(), RND, RND, ALU.add,
                    ALU.subtract),
                    waits=[("c_q1", B.c("c_q1"))], inc="c_q2")
                B.emit("vector", lambda e, par=par, ic0=ic0, qp=qp: e.tensor_scalar(
                    hq[par][:, ic0:ic0 + 11, :], hqfs[qp].ap(), -128.0, 127.0,
                    ALU.max, ALU.min),
                    waits=[("c_q2", B.c("c_q2"))], inc="c_q3")

        tile_front(0)
        for tt in range(nt):
            par = tt % 2
            acc = paccs[par]
            wmm = [("c_q3", B.c("c_q3")), ("c_ev", B.h("c_ev", n_ev + tt - 1))]
            if tt == 0:
                wmm = wmm + list(first_mm_waits)
            for ic in range(IC):
                B.emit("tensor", lambda e, par=par, ic=ic, acc=acc: e.matmul(
                    acc[:, 0:512], hq[par][:, ic, :], wdT[:, ic, 0:512],
                    start=(ic == 0), stop=(ic == IC - 1)),
                    waits=(wmm if ic == 0 else ()))
                B.emit("tensor", lambda e, par=par, ic=ic, acc=acc: e.matmul(
                    acc[:, 512:], hq[par][:, ic, :], wdT[:, ic, 512:],
                    start=(ic == 0), stop=(ic == IC - 1)),
                    inc=("c_mm" if ic == IC - 1 else None))
            B.dma2("sync", xt[par].ap(),
                   bass.AP(res_src, tt * 128 * D, [[D, 128], [1, D]]), "c_xl",
                   tt, waits=[("c_ev", B.h("c_ev", n_ev + tt - 1))])
            if tt + 1 < nt:
                tile_front(tt + 1)
            wev = [("c_mm", B.c("c_mm")), B.dtot("c_xl", tt),
                   B.dtot("c_xo", tt)]
            if outt_guard is not None:
                wev.append(outt_guard(tt))
            B.emit("vector", lambda e, par=par, acc=acc: e.scalar_tensor_tensor(
                outt[par].ap(), acc.ap(), sl("half_sd")[:, 0:1], xt[par].ap(),
                ALU.mult, ALU.add),
                waits=wev, inc="c_ev")
            B.dma2("sync", bass.AP(xdst, tt * 128 * D, [[D, 128], [1, D]]),
                   outt[par].ap(), "c_xo", tt, waits=[("c_ev", B.c("c_ev"))])
            if tile_hook is not None:
                tile_hook(tt)

    def emit_attn_wload(entry_waits):
        first = True
        for wsrc, dst in ((wqTD, awq), (wkTD, awk), (wvTD, awv)):
            for j in range(DC):
                idx = B.n("aw_c")
                par = idx % 2
                w = [("aw_c", B.h("aw_c", idx - 1))]
                if first:
                    w = w + list(entry_waits)
                    first = False
                B.dma2("sync", wst[par].ap(),
                       bass.AP(wsrc, j * 128 * D, [[D, 128], [1, D]]), "aw_ld",
                       idx, waits=w)
                B.emit("vector", lambda e, dst=dst, j=j, par=par: e.tensor_copy(
                    dst[:, j, :], wst[par].ap()),
                    waits=[B.dtot("aw_ld", idx)], inc="aw_c")

    def emit_proj():
        xn_waits = B.dboth("l_xn")
        aw_val = B.c("aw_c")

        def one_pass(pre, wsb, bias_pp, dstD, s_mm, s_ev, s_dma, prev_mm=None,
                     prev_dma=None):
            n_mm = B.n(s_mm); n_ev = B.n(s_ev)

            def g_load(g):
                par = g % 2
                w = [(s_mm, B.h(s_mm, n_mm + (g - 1) * DC)),
                     ("aw_c", aw_val)] + xn_waits
                if g < 2 and prev_mm is not None:
                    w.append(prev_mm)
                for hc in range(DC):
                    B.dma2("sync", xn2[par][:, hc, :],
                           bass.AP(xnTD, hc * 128 * T + g * 512,
                                   [[T, 128], [1, 512]]),
                           "pj_ld", g, waits=(w if hc == 0 else ()))

            g_load(0)
            for g in range(ng):
                par = g % 2
                for m in range(DC):
                    gm = g * DC + m
                    pp = p0 if m % 2 == 0 else p1
                    wmm = [B.dtot("pj_ld", g),
                           (s_ev, B.h(s_ev, n_ev + gm - 1))]
                    for hc in range(DC):
                        B.emit("tensor",
                               lambda e, par=par, hc=hc, m=m, pp=pp, wsb=wsb:
                               e.matmul(pp.ap(),
                                        wsb[:, hc, m * 128:(m + 1) * 128],
                                        xn2[par][:, hc, :],
                                        start=(hc == 0), stop=(hc == DC - 1)),
                               waits=(wmm if hc == 0 else ()),
                               inc=(s_mm if hc == DC - 1 else None))
                    wev = [(s_mm, B.c(s_mm)), B.dtot(s_dma, gm)]
                    if gm < 2 and prev_dma is not None:
                        wev += prev_dma
                    B.emit("vector",
                           lambda e, m=m, pp=pp, bias_pp=bias_pp: e.tensor_scalar(
                               kev[m % 2].ap(), pp.ap(), bias_pp[:, m:m + 1],
                               None, ALU.add),
                           waits=wev, inc=s_ev)
                    B.dma2("sync", bass.AP(dstD, m * 128 * T + g * 512,
                                           [[T, 128], [1, 512]]),
                           kev[m % 2].ap(), s_dma, gm, waits=[(s_ev, B.c(s_ev))])
                    if m == 0 and g + 1 < ng:
                        g_load(g + 1)

        one_pass("pk", awk, bk_pp, ktoD, "pk_mm", "pk_ev", "pk_dma")
        B.emit("gpsimd", lambda e: e.collective_compute(
            "AllGather", ALU.bypass,
            replica_groups=[[0, 1], [2, 3], [4, 5], [6, 7]],
            ins=[flat(ktoD, D * T)], outs=[flat(ktgD, 2 * D * T)]),
            waits=B.dboth("pk_dma"), inc="ag_kt")
        one_pass("pq", awq, bq_pp, qtD, "pq_mm", "pq_ev", "pq_dma",
                 prev_mm=("pk_mm", B.c("pk_mm")), prev_dma=B.dboth("pk_dma"))

        n_mm = B.n("pv_mm"); n_ev = B.n("pv_ev")
        pk_mm_done = ("pq_mm", B.c("pq_mm"))

        def v_load(g):
            par = g % 2
            w = [("pv_mm", B.h("pv_mm", n_mm + (g - 1) * 4)),
                 ("aw_c", aw_val)] + xn_waits
            if g < 2:
                w.append(pk_mm_done)
            for hc in range(DC):
                B.dma2("sync", xn2[par][:, hc, :],
                       bass.AP(xnTD, hc * 128 * T + g * 512,
                               [[T, 128], [1, 512]]),
                       "pj_ld", g, waits=(w if hc == 0 else ()))

        v_load(0)
        for g in range(ng):
            par = g % 2
            for sub in range(4):
                gs = g * 4 + sub
                acc = paccs[sub % 2]
                wmm = [B.dtot("pj_ld", g),
                       ("pv_ev", B.h("pv_ev", n_ev + gs - 1))]
                for hc in range(DC):
                    B.emit("tensor", lambda e, par=par, hc=hc, sub=sub, acc=acc:
                           e.matmul(acc[:, 0:512],
                                    xn2[par][:, hc, sub * 128:(sub + 1) * 128],
                                    awv[:, hc, 0:512], start=(hc == 0), stop=False),
                           waits=(wmm if hc == 0 else ()))
                    B.emit("tensor", lambda e, par=par, hc=hc, sub=sub, acc=acc:
                           e.matmul(acc[:, 512:],
                                    xn2[par][:, hc, sub * 128:(sub + 1) * 128],
                                    awv[:, hc, 512:], start=(hc == 0), stop=False))
                B.emit("tensor", lambda e, acc=acc: e.matmul(
                    acc[:, 0:512], ones_b[0:1, :], bv_row[0:1, 0:512],
                    start=False, stop=True))
                B.emit("tensor", lambda e, acc=acc: e.matmul(
                    acc[:, 512:], ones_b[0:1, :], bv_row[0:1, 512:],
                    start=False, stop=True), inc="pv_mm")
                B.emit("vector", lambda e, sub=sub, acc=acc: e.tensor_copy(
                    vev[sub % 2].ap(), acc.ap()),
                    waits=[("pv_mm", B.c("pv_mm")), B.dtot("pv_dma", gs)],
                    inc="pv_ev")
                B.dma2("sync", bass.AP(voD, (g * 512 + sub * 128) * D,
                                       [[D, 128], [1, D]]),
                       vev[sub % 2].ap(), "pv_dma", gs,
                       waits=[("pv_ev", B.c("pv_ev"))])
                if sub == 0 and g + 1 < ng:
                    v_load(g + 1)
        B.emit("gpsimd", lambda e: e.collective_compute(
            "AllGather", ALU.bypass,
            replica_groups=[[0, 1], [2, 3], [4, 5], [6, 7]],
            ins=[flat(voD, T * D)], outs=[flat(vgD, 2 * T * D)]),
            waits=B.dboth("pv_dma"), inc="ag_v")

    def emit_flash(res_src, xdst, sb_hook=None, outt_guard=None):
        entry = B.dboth("pq_dma") + [("ag_kt", B.c("ag_kt")),
                                     ("ag_v", B.c("ag_v"))]
        for e_ in ENGINES:
            B.wait_only(e_, entry)
        n_sm = B.n("f_sm"); n_ac = B.n("f_ac"); n_u = B.n("f_u")
        n_ex = B.n("f_ex"); n_ptr = B.n("f_ptr"); n_pev = B.n("f_pev")
        n_pv = B.n("f_pv"); n_ao = B.n("f_ao"); n_aoe = B.n("f_aoe")
        n_aot = B.n("f_aot"); n_oe = B.n("f_oe"); n_om = B.n("f_om")
        qtbs = [qtb[0], qtb[1], qtb3]
        QB = min(4, nkc)
        nseg = nkc // QB
        hk = nkc // 2
        npc = max(1, hk // 2)
        pwk = hk // npc
        bst = {"rr": 0, "lu": {}, "sb_pacc": -1}

        def scores_dma(sb, kc):
            sbpar = sb % 3
            if kc == 0:
                for hc in range(DC):
                    B.dma2("sync", qtbs[sbpar][:, hc, :],
                           bass.AP(qtD, hc * 128 * T + sb * 256,
                                   [[T, 128], [1, 256]]),
                           "f_qt", sb,
                           waits=([("f_sm",
                                    B.h("f_sm", n_sm + (sb - 2) * nkc * 2))]
                                  if hc == 0 else ()))
            kpar = kc % 2
            gi = sb * nkc + kc
            rank = 0 if kc * 512 < T else 1
            base = rank * D * T + (kc * 512 - rank * T)
            B.dma2("sync", ktb[kpar][:, :, :],
                   bass.AP(ktgD, base, [[T, 128], [128 * T, 8], [1, 512]]),
                   "f_kt", gi,
                   waits=[("f_sm", B.h("f_sm", n_sm + (gi - 1) * 2))])

        def scores_kc(sb, kc, dma_done=False):
            sbpar = sb % 3
            if not dma_done:
                scores_dma(sb, kc)
            kpar = kc % 2
            gi = sb * nkc + kc
            for qt2 in (0, 1):
                ci = gi * 2 + qt2
                use4 = (sb > 0 and kc >= 5)
                if use4:
                    if kc == 5 and qt2 == 0:
                        bst["rr"] = 0
                    bi = bst["rr"] % 4
                    bst["rr"] += 1
                else:
                    bi = ci % 2
                psap = (p0.ap(), p1.ap(), pacc0[:, 0:512],
                        pacc0[:, 512:])[bi]
                tpar = ci % 3
                wmm = [B.dtot("f_kt", gi), B.dtot("f_qt", sb)]
                lu = bst["lu"].get(bi)
                if lu is not None:
                    wmm.append(("f_ac", B.h("f_ac", n_ac + lu + 1)))
                if bi >= 2 and bst["sb_pacc"] != sb:
                    bst["sb_pacc"] = sb
                    wmm.append(("f_oe",
                                B.h("f_oe", n_oe + 2 * (sb - 1) + 2)))
                bst["lu"][bi] = ci
                for hc in range(DC):
                    B.emit("tensor", lambda e, sbpar=sbpar, kpar=kpar, hc=hc,
                           qt2=qt2, psap=psap: e.matmul(
                               psap,
                               qtbs[sbpar][:, hc, qt2 * 128:(qt2 + 1) * 128],
                               ktb[kpar][:, hc, :],
                               start=(hc == 0), stop=(hc == DC - 1)),
                           waits=(wmm if hc == 0 else ()),
                           inc=("f_sm" if hc == DC - 1 else None))
                B.emit("scalar", lambda e, psap=psap, tpar=tpar: e.activation(
                    out=t1s[tpar].ap(), in_=psap, func=AF.Tanh, scale=0.5,
                    bias=sl("zb")[:, 0:1]),
                    waits=[("f_sm", B.c("f_sm")),
                           ("f_u", B.h("f_u", n_u + ci - 2))])
                B.emit("scalar", lambda e, psap=psap, tpar=tpar: e.activation(
                    out=t2s[tpar].ap(), in_=psap, func=AF.Tanh,
                    bias=sl("zb")[:, 0:1]))
                B.emit("scalar", lambda e, psap=psap, tpar=tpar: e.activation(
                    out=rsc[tpar].ap(), in_=psap, func=AF.Relu,
                    bias=sl("zb")[:, 0:1]), inc="f_ac")
                B.emit("vector", lambda e, tpar=tpar: e.scalar_tensor_tensor(
                    t1s[tpar].ap(), t2s[tpar].ap(), 1.5, t1s[tpar].ap(),
                    ALU.mult, ALU.add),
                    waits=[("f_ac", B.c("f_ac"))])
                B.emit("vector", lambda e, qt2=qt2, kc=kc, tpar=tpar:
                       e.tensor_add(
                           Ubuf[qt2][:, kc * 512:(kc + 1) * 512],
                           t1s[tpar].ap(), rsc[tpar].ap()),
                       waits=[("f_ex",
                               B.h("f_ex", n_ex + 4 * npc * (sb - 1)
                                   + (kc // hk) * 2 * npc
                                   + 2 * ((kc % hk) // pwk) + qt2 + 1))],
                       inc="f_u")
                if (kc + 1) % QB == 0:
                    seg = (kc + 1) // QB - 1
                    B.emit("vector", lambda e, qt2=qt2, kc=kc, seg=seg:
                           e.tensor_reduce(
                               sl(f"mp{qt2}", (128, 16))[:, seg:seg + 1],
                               Ubuf[qt2][:, (kc + 1 - QB) * 512:
                                         (kc + 1) * 512],
                               AX.X, ALU.max))

        def exp_part(sb):
            for qt2 in (0, 1):
                B.emit("vector", lambda e, qt2=qt2: e.tensor_reduce(
                    sl(f"mrow{qt2}")[:, 0:1],
                    sl(f"mp{qt2}", (128, 16))[:, 0:nseg], AX.X, ALU.max),
                    waits=[("f_u",
                            B.h("f_u", n_u + 2 * nkc * sb + 2 * nkc - 1 + qt2))])
                B.emit("vector", lambda e, qt2=qt2: e.tensor_scalar(
                    sl(f"negm{qt2}")[:, 0:1], sl(f"mrow{qt2}")[:, 0:1], -0.1,
                    None, ALU.mult), inc="f_mxr")
            first = True
            for half in (0, 1):
                for p in range(npc):
                    for qt2 in (0, 1):
                        c0 = half * (S // 2) + p * pwk * 512
                        c1 = c0 + pwk * 512
                        idx = half * npc + p
                        vlast = half * hk + (p + 1) * pwk - 1
                        w = [("f_ptr", B.h("f_ptr", n_ptr
                                           + 2 * nkc * (sb - 1)
                                           + 2 * vlast + qt2 + 1))]
                        if first:
                            w.append(("f_mxr", B.c("f_mxr")))
                            first = False
                        B.emit("scalar", lambda e, qt2=qt2, c0=c0, c1=c1,
                               idx=idx: e.activation(
                                   out=Pbuf[qt2][:, c0:c1],
                                   in_=Ubuf[qt2][:, c0:c1],
                                   func=AF.Exp, bias=sl(f"negm{qt2}")[:, 0:1],
                                   scale=0.1,
                                   accum_out=lrp[qt2][:, idx:idx + 1]),
                               waits=w, inc="f_ex")

        def pv_part(sb):
            for vc in range(nkc):
                vpar = vc % 2
                B.dma2("sync", vtb[vpar][:, :, :],
                       bass.AP(vgD, vc * 512 * D,
                               [[D, 128], [128 * D, 4], [1, D]]),
                       "f_vt", sb * nkc + vc,
                       waits=[("f_pv",
                               B.h("f_pv", n_pv + 2 * nkc * sb + 2 * vc - 2))])
                for qt2 in (0, 1):
                    wtr = [("f_ex",
                            B.h("f_ex", n_ex + 4 * npc * sb
                                + (vc // hk) * 2 * npc
                                + 2 * ((vc % hk) // pwk) + qt2 + 1)),
                           ("f_pev",
                            B.h("f_pev", n_pev + 2 * nkc * sb + 2 * vc + qt2 - 1))]
                    if vc == 0:
                        wtr.append(("f_aoe", B.h("f_aoe", n_aoe + 2 * (sb - 1)
                                                 + qt2 + 1)))
                    for i4 in range(4):
                        B.emit("tensor", lambda e, qt2=qt2, vc=vc, i4=i4:
                               e.transpose(
                                   pts[qt2][:, i4, :],
                                   Pbuf[qt2][:, (vc * 4 + i4) * 128:
                                             (vc * 4 + i4 + 1) * 128],
                                   ident.ap()),
                               waits=(wtr if i4 == 0 else ()),
                               inc=("f_ptr" if i4 == 3 else None))
                for qt2 in (0, 1):
                    B.emit("vector", lambda e, qt2=qt2: e.tensor_copy(
                        ptsb[qt2][:, 0:4, :], pts[qt2][:, 0:4, :]),
                        waits=[("f_ptr", B.h("f_ptr",
                                             n_ptr + 2 * nkc * sb + 2 * vc
                                             + qt2 + 1)),
                               ("f_pv",
                                B.h("f_pv", n_pv + 2 * nkc * sb + 2 * (vc - 1)
                                    + qt2 + 1))], inc="f_pev")
                for qt2 in (0, 1):
                    acc = paccs[qt2]
                    wpv = [("f_pev", B.h("f_pev",
                                         n_pev + 2 * nkc * sb + 2 * vc
                                         + qt2 + 1)),
                           B.dtot("f_vt", sb * nkc + vc)]
                    if vc == 0:
                        wpv.append(("f_oe", B.h("f_oe", n_oe + 2 * sb + qt2 - 1)))
                    for half in (0, 1):
                        for i4 in range(4):
                            B.emit("tensor", lambda e, qt2=qt2, vc=vc, i4=i4,
                                   vpar=vpar, acc=acc, half=half: e.matmul(
                                       acc[:, half * 512:(half + 1) * 512],
                                       ptsb[qt2][:, i4, :],
                                       vtb[vpar][:, i4,
                                                 half * 512:(half + 1) * 512],
                                       start=(vc == 0 and i4 == 0),
                                       stop=(vc == nkc - 1 and i4 == 3)),
                                   waits=(wpv if (half == 0 and i4 == 0)
                                          else ()),
                                   inc=("f_pv" if (half == 1 and i4 == 3)
                                        else None))

        def epi_e0(sb):
            for qt2 in (0, 1):
                B.emit("vector", lambda e, qt2=qt2: e.tensor_reduce(
                    sl(f"lrow{qt2}")[:, 0:1], lrp[qt2][:, 0:2 * npc],
                    AX.X, ALU.add), waits=[("f_ex", B.c("f_ex"))])
                B.emit("vector", lambda e, qt2=qt2: e.reciprocal(
                    sl(f"invl{qt2}")[:, 0:1], sl(f"lrow{qt2}")[:, 0:1]),
                    inc="f_iv")
            for qt2 in (0, 1):
                acc = paccs[qt2]
                B.emit("vector", lambda e, qt2=qt2, acc=acc: e.tensor_scalar(
                    ao_sb2[qt2].ap(), acc.ap(), sl(f"invl{qt2}")[:, 0:1], None,
                    ALU.mult),
                    waits=[("f_pv", B.h("f_pv", n_pv + 2 * nkc * sb
                                        + 2 * (nkc - 1) + qt2 + 1)),
                           ("f_iv", B.c("f_iv")),
                           ("f_aot", B.h("f_aot", n_aot + 2 * sb + qt2 - 1))],
                    inc="f_ao")
            for qt2 in (0, 1):
                row = sb * 256 + qt2 * 128
                B.dma2("sync", xt[qt2].ap(),
                       bass.AP(res_src, row * D, [[D, 128], [1, D]]), "f_xl",
                       qt2,
                       waits=[("f_oe", B.h("f_oe", n_oe + 2 * sb + qt2 - 1))])

        def epi_aot(sb, qt2):
            for j in range(DC):
                B.emit("tensor", lambda e, qt2=qt2, j=j: e.transpose(
                    pts[qt2][:, j, :], ao_sb2[qt2][:, j * 128:(j + 1) * 128],
                    ident.ap()),
                    waits=([("f_ao", B.h("f_ao", n_ao + 2 * sb + qt2 + 1))]
                           if j == 0 else ()),
                    inc=("f_aot" if j == DC - 1 else None))

        def epi_aoe(sb, qt2):
            B.emit("vector", lambda e, qt2=qt2: e.tensor_copy(
                aoT_sb2[qt2].ap(), pts[qt2].ap()),
                waits=[("f_aot", B.h("f_aot", n_aot + 2 * sb + qt2 + 1)),
                       ("f_om", B.h("f_om", n_om + 2 * (sb - 1) + qt2 + 1))],
                inc="f_aoe")

        def epi_omm(sb, qt2):
            acc = paccs[qt2]
            womm = [("f_aoe", B.h("f_aoe", n_aoe + 2 * sb + qt2 + 1)),
                    ("f_ao", B.h("f_ao", n_ao + 2 * sb + 2))]
            for hc in range(DC):
                B.emit("tensor", lambda e, hc=hc, acc=acc, qt2=qt2: e.matmul(
                    acc[:, 0:512], aoT_sb2[qt2][:, hc, :],
                    wo_sb[:, hc, 0:512],
                    start=(hc == 0), stop=False),
                    waits=(womm if hc == 0 else ()))
                B.emit("tensor", lambda e, hc=hc, acc=acc, qt2=qt2: e.matmul(
                    acc[:, 512:], aoT_sb2[qt2][:, hc, :],
                    wo_sb[:, hc, 512:],
                    start=(hc == 0), stop=False))
            B.emit("tensor", lambda e, acc=acc: e.matmul(
                acc[:, 0:512], ones_b[0:1, :], bo_row[0:1, 0:512],
                start=False, stop=True))
            B.emit("tensor", lambda e, acc=acc: e.matmul(
                acc[:, 512:], ones_b[0:1, :], bo_row[0:1, 512:],
                start=False, stop=True), inc="f_om")

        def epi_out(sb, qt2):
            acc = paccs[qt2]
            row = sb * 256 + qt2 * 128
            wev = [("f_om", B.h("f_om", n_om + 2 * sb + qt2 + 1)),
                   B.dtot("f_xl", qt2), B.dtot("f_xo", qt2)]
            if outt_guard is not None:
                g_ = outt_guard(sb, qt2)
                if g_ is not None:
                    wev.append(g_)
            B.emit("vector", lambda e, qt2=qt2, acc=acc: e.tensor_add(
                outt[qt2][:, 0:512], acc[:, 0:512], xt[qt2][:, 0:512]),
                waits=wev)
            B.emit("vector", lambda e, qt2=qt2, acc=acc: e.tensor_add(
                outt[qt2][:, 512:], acc[:, 512:], xt[qt2][:, 512:]),
                inc="f_oe")
            B.dma2("sync", bass.AP(xdst, row * D, [[D, 128], [1, D]]),
                   outt[qt2].ap(), "f_xo", qt2,
                   waits=[("f_oe", B.c("f_oe"))])

        for kc in range(nkc):
            scores_kc(0, kc)
        for sb in range(nsb):
            exp_part(sb)
            if sb + 1 < nsb:
                scores_dma(sb + 1, 0)
                scores_dma(sb + 1, 1)
            pv_part(sb)
            stages = [lambda sb=sb: epi_e0(sb),
                      lambda sb=sb: epi_aot(sb, 0),
                      lambda sb=sb: (epi_aoe(sb, 0), epi_aot(sb, 1)),
                      lambda sb=sb: (epi_aoe(sb, 1), epi_omm(sb, 0)),
                      lambda sb=sb: (epi_omm(sb, 1), epi_out(sb, 0)),
                      lambda sb=sb: (epi_out(sb, 1),
                                     sb_hook(sb) if sb_hook else None)]
            if sb + 1 < nsb:
                stages[0]()
                done = 1
                for kc in range(nkc):
                    scores_kc(sb + 1, kc, dma_done=(kc < 2))
                    if done < len(stages):
                        stages[done]()
                        done += 1
                while done < len(stages):
                    stages[done]()
                    done += 1
            else:
                for f_ in stages:
                    f_()

    def emit_wreload(entry_waits):
        first = True
        for j in range(DC):
            w = (list(entry_waits) + [("q_st", B.c("q_st"))]) if first else ()
            first = False
            B.dma("sync", wgT[:, j, :],
                  bass.AP(wgS, j * 128 * I, [[I, 128], [1, I]]), "wr_ld", waits=w)
        for j in range(DC):
            B.dma("sync", wuT[:, j, :],
                  bass.AP(wuS, j * 128 * I, [[I, 128], [1, I]]), "wr_ld")
        for ic in range(IC):
            B.dma("sync", wdT[:, ic, :],
                  bass.AP(wdS, ic * 128 * D, [[D, 128], [1, D]]), "wr_ld")

    gu_ev, quant_w_fn = emit_setup()
    emit_ln_dram(xin, "amax1", entry_waits=[("p0_misc", 64)])
    xpart_request(sl("amax1"), ccB[0], [("l_mx", B.c("l_mx"))], [], "c1")
    quant_w_fn(wdD, None, wdT, wdS, DC, True, I)
    xpart_complete(sl("M1"), "c1")
    emit_scales_x(sl("M1"))
    for f_ in deferred_stores:
        f_()
    B.wait_only("tensor", [("q_ev", gu_ev), ("sc", B.c("sc"))])
    emit_ffn_b("hamax1")
    xpart_max_allreduce(sl("hamax1"), ccB[1], sl("M2"),
                        [("b_hm", B.c("b_hm"))] + B.dboth("b_hd"), [], "c2")
    emit_scales_h(sl("M2"))
    b1_h = B.c("b_h")
    ln2 = LNPass(None)
    emit_ffn_c(xin, x1D,
               first_mm_waits=[("b_h", b1_h), ("q_ev", B.c("q_ev"))],
               tile_hook=lambda tt: ln2.tile(
                   tt, outt[tt % 2].ap(), [("c_ev", B.c("c_ev"))]),
               outt_guard=lambda tt: ln2.norm_done(tt - 2) if tt >= 2 else None)
    emit_attn_wload([("c_mm", B.c("c_mm")), ("b_mu", B.c("b_mu")),
                     ("q_st", B.c("q_st"))] + B.dboth("c_xo"))
    emit_proj()
    ln3 = LNPass("amax3")

    def ln3_hook(sb):
        g = [("f_oe", B.c("f_oe"))]
        ln3.stats(2 * sb, outt[0].ap(), g)
        ln3.stats(2 * sb + 1, outt[1].ap(), g)
        ln3.norm(2 * sb, outt[0].ap())
        ln3.norm(2 * sb + 1, outt[1].ap())

    emit_flash(x1D, x2D, sb_hook=ln3_hook,
               outt_guard=lambda sb, qt2: (ln3.norm_done(2 * (sb - 1) + qt2)
                                           if sb >= 1 else None))
    emit_wreload(B.dboth("f_xo"))
    xpart_max_allreduce(sl("amax3"), ccB[2], sl("M1"),
                        [("l_mx", B.c("l_mx"))], [("f_oe", B.c("f_oe"))], "c3")
    emit_scales_x(sl("M1"))
    B.wait_only("tensor", [("wr_ld", B.c("wr_ld")), ("sc", B.c("sc"))])
    emit_ffn_b("hamax3")
    xpart_max_allreduce(sl("hamax3"), ccB[3], sl("M2"),
                        [("b_hm", B.c("b_hm"))] + B.dboth("b_hd"), [], "c4")
    emit_scales_h(sl("M2"))
    b3_h = B.c("b_h")
    emit_ffn_c(x2D, xout, first_mm_waits=[("b_h", b3_h)])
    B.wait_only("sync", B.dboth("c_xo"))
    B.wait_only("gpsimd", B.dboth("c_xo"))

    with nc.Block() as block:
        @block.sync
        def _(eng):
            for f in B.q["sync"]:
                f(eng)

        @block.scalar
        def _(eng):
            for f in B.q["scalar"]:
                f(eng)

        @block.vector
        def _(eng):
            for f in B.q["vector"]:
                f(eng)

        @block.tensor
        def _(eng):
            for f in B.q["tensor"]:
                f(eng)

        @block.gpsimd
        def _(eng):
            for f in B.q["gpsimd"]:
                f(eng)

    return nc


def make_in_maps(x_full, Wq, bq, Wk, bk, Wv, bv, Wo, bo, clip_x, clip_down,
                 w_gate, w_up, w_down, T=4096):
    f32 = np.float32
    ident = np.eye(128, dtype=ml_dtypes.bfloat16)
    com = {
        "wqT": np.ascontiguousarray(np.asarray(Wq, f32).T),
        "wkT": np.ascontiguousarray(np.asarray(Wk, f32).T),
        "wvT": np.ascontiguousarray(np.asarray(Wv, f32).T),
        "woT": np.ascontiguousarray(np.asarray(Wo, f32).T),
        "bqp": np.ascontiguousarray(np.asarray(bq, f32).reshape(8, 128).T),
        "bkp": np.ascontiguousarray(np.asarray(bk, f32).reshape(8, 128).T),
        "bvr": np.asarray(bv, f32).reshape(1, D),
        "bor": np.asarray(bo, f32).reshape(1, D),
        "wg": np.asarray(w_gate, f32),
        "wu": np.asarray(w_up, f32),
        "wd": np.asarray(w_down, f32),
        "clip": np.array([[np.asarray(clip_x).reshape(-1)[0],
                           np.asarray(clip_down).reshape(-1)[0]]], f32),
        "ident": ident,
    }
    in_maps = []
    for c in range(8):
        b, hh = c // 2, c % 2
        shard = np.ascontiguousarray(x_full[b, hh * T:(hh + 1) * T, :], dtype=f32)
        m = dict(com)
        m["xin"] = shard
        in_maps.append(m)
    return in_maps


def kernel(L0_attn_x, L0_attn_Wq, L0_attn_bq, L0_attn_Wk, L0_attn_bk,
           L0_attn_Wv, L0_attn_bv, L0_attn_Wo, L0_attn_bo,
           L0_ffn_clip_x, L0_ffn_clip_down, L0_ffn_w_gate, L0_ffn_w_up,
           L0_ffn_w_down):
    T = 4096
    x = np.asarray(L0_attn_x, np.float32)
    in_maps = make_in_maps(x, L0_attn_Wq, L0_attn_bq, L0_attn_Wk, L0_attn_bk,
                           L0_attn_Wv, L0_attn_bv, L0_attn_Wo, L0_attn_bo,
                           L0_ffn_clip_x, L0_ffn_clip_down, L0_ffn_w_gate,
                           L0_ffn_w_up, L0_ffn_w_down, T=T)
    nc = build_program(T)
    res = run_bass_kernel_spmd(nc, in_maps, core_ids=list(range(8)))
    out = np.empty((4, 2 * T, D), np.float32)
    for c in range(8):
        b, hh = c // 2, c % 2
        out[b, hh * T:(hh + 1) * T, :] = res.results[c]["xout"]
    return out



# version 28
# speedup vs baseline: 1.1418x; 1.0089x over previous
import numpy as np
import ml_dtypes
import concourse.bass as bass
import concourse.mybir as mybir
import concourse.bass_utils as _bass_utils
from concourse.bass_utils import run_bass_kernel_spmd

_orig_run_command = _bass_utils.run_command


def _run_command_ldwopt(argv, **kwargs):
    argv = ["--enable-ldw-opt=true" if a == "--enable-ldw-opt=false" else a
            for a in argv]
    return _orig_run_command(argv, **kwargs)


_bass_utils.run_command = _run_command_ldwopt

F32 = mybir.dt.float32
BF16 = mybir.dt.bfloat16
AF = mybir.ActivationFunctionType
ALU = mybir.AluOpType
AX = mybir.AxisListType

RND = 12582912.0
D = 1024
DC = 8
I = 2816
IC = 22
EPS = 1e-5

ENGINES = ("sync", "scalar", "vector", "tensor", "gpsimd")

OFF_IDENT = 2048
OFF_BQ = 2304
OFF_BK = 2336
OFF_BVROW = 2368
OFF_BOROW = 4416
OFF_ONESB = 6464
OFF_ONESF = 6720
OFF_CLIP = 7232
OFF_LNST = 7264
OFF_ROW128 = 7328
OFF_RSG = 7840
OFF_RSU = 7936
OFF_CSG = 8032
OFF_CSU = 8128
OFF_SLOTS = 8224
OFF_XT = 10752
OFF_OUTT = 19456
OFF_XNBF = 27648
OFF_TEVAC = 31744
OFF_WO = 35840
OFF_W = 52224
FO_WG = 0
FO_WU = 45056
FO_WD = 90112
FO_XQ = 135168
FO_HEV = 151552
FO_SILU = 153600
FO_HQ = 157696
FO_HQF = 168960
FO_P0STG = 151552
FO_P0WI = 162816
AO_WQ = 0
AO_WK = 16384
AO_WV = 32768
AO_XN2 = 49152
AO_KEV = 65536
AO_VEV = 67584
AO_WST = 71680
FL_QTB = 0
FL_KTB = 8192
FL_VTB = 24576
FL_U = 40960
FL_P = 106496
FL_T1 = 139264
FL_T2 = 143360
FL_R = 147456
FL_PT = 151552
FL_AO = 155648
FL_AOT = 157696


class Builder:
    def __init__(self, nc):
        self.nc = nc
        self.q = {e: [] for e in ENGINES}
        self.sems = {}
        self.cnt = {}
        self.hist = {}
        self._slots = {}
        self._slot_next = OFF_SLOTS

    def sem(self, name):
        if name not in self.sems:
            self.sems[name] = self.nc.alloc_semaphore(name)
            self.cnt[name] = 0
            self.hist[name] = []
        return self.sems[name]

    def c(self, name):
        self.sem(name)
        return self.cnt[name]

    def n(self, name):
        self.sem(name)
        return len(self.hist[name])

    def h(self, name, k):
        self.sem(name)
        if k <= 0:
            return 0
        hist = self.hist[name]
        assert k <= len(hist), (name, k, len(hist))
        return hist[k - 1]

    SERIAL = {"vector": "ser_v", "scalar": "ser_s", "gpsimd": "ser_g"}

    def emit(self, eng, fn, waits=(), inc=None, amt=1):
        waits = tuple(w for w in waits if w is not None)
        for s, _ in waits:
            self.sem(s)
        if inc is not None and amt > 0:
            self.sem(inc)
            self.cnt[inc] += amt
            self.hist[inc].append(self.cnt[inc])
        else:
            inc = None
        ser = self.SERIAL.get(eng)
        prev_tok = None
        my_inc = inc
        if ser is not None:
            prev_tok = getattr(self, "_tok", {}).get(eng)
            if not hasattr(self, "_tok"):
                self._tok = {}
            if inc is not None:
                self._tok[eng] = (inc, self.cnt[inc])
            else:
                self.sem(ser)
                self.cnt[ser] += 1
                self.hist[ser].append(self.cnt[ser])
                my_inc = ser
                amt = 1
                self._tok[eng] = (ser, self.cnt[ser])
        sems = self.sems

        def run(engine, fn=fn, waits=waits, inc=my_inc, amt=amt,
                prev_tok=prev_tok):
            if prev_tok is not None and prev_tok[1] > 0:
                engine.wait_ge(sems[prev_tok[0]], prev_tok[1])
            for s, v in waits:
                if v > 0:
                    engine.wait_ge(sems[s], v)
            ins = fn(engine)
            if inc is not None:
                ins.then_inc(sems[inc], amt)

        self.q[eng].append(run)

    def wait_only(self, eng, waits):
        waits = tuple(w for w in waits if w is not None and w[1] > 0)
        if not waits:
            return
        for s, _ in waits:
            self.sem(s)
        sems = self.sems

        def run(engine, waits=waits):
            for s, v in waits:
                engine.wait_ge(sems[s], v)

        self.q[eng].append(run)

    def dma(self, eng, out, in_, inc, waits=(), **kw):
        self.emit(eng, lambda e, o=out, i=in_, kw=kw: e.dma_start(out=o, in_=i, **kw),
                  waits=waits, inc=inc, amt=16)

    def dma2(self, eng, out, in_, fam, idx, waits=(), **kw):
        self.dma(eng, out, in_, f"{fam}{idx % 2}", waits=waits, **kw)

    def dtot(self, fam, idx):
        name = f"{fam}{idx % 2}"
        return (name, self.c(name))

    def dboth(self, fam):
        return [(f"{fam}0", self.c(f"{fam}0")), (f"{fam}1", self.c(f"{fam}1"))]

    def slot(self, name, shape=(128, 1)):
        if name not in self._slots:
            nbytes = int(np.prod(shape[1:])) * 4
            nbytes = (nbytes + 31) // 32 * 32
            off = self._slot_next
            assert off + nbytes <= OFF_XT, (name, off)
            self._slot_next = off + nbytes
            self._slots[name] = self.nc.alloc_sbuf_tensor_at(
                "slot_" + name, list(shape), F32, offset=off)
        return self._slots[name]


def flat(handle, n, off=0):
    return bass.AP(handle, off, [[1, 1], [1, n]])


def build_program(T=4096):
    nc = bass.Bass("TRN2", target_bir_lowering=False, num_devices=8,
                   dynamic_dma_scratch_size=2048)
    nt = T // 128
    ng = T // 512
    S = 2 * T
    nkc = S // 512
    nsb = T // 256

    xin = nc.dram_tensor("xin", [T, D], F32, kind="ExternalInput")
    wqTD = nc.dram_tensor("wqT", [D, D], F32, kind="ExternalInput")
    wkTD = nc.dram_tensor("wkT", [D, D], F32, kind="ExternalInput")
    wvTD = nc.dram_tensor("wvT", [D, D], F32, kind="ExternalInput")
    woTD = nc.dram_tensor("woT", [D, D], F32, kind="ExternalInput")
    bqpD = nc.dram_tensor("bqp", [128, 8], F32, kind="ExternalInput")
    bkpD = nc.dram_tensor("bkp", [128, 8], F32, kind="ExternalInput")
    bvrD = nc.dram_tensor("bvr", [1, D], F32, kind="ExternalInput")
    borD = nc.dram_tensor("bor", [1, D], F32, kind="ExternalInput")
    wgD = nc.dram_tensor("wg", [I, D], F32, kind="ExternalInput")
    wuD = nc.dram_tensor("wu", [I, D], F32, kind="ExternalInput")
    wdD = nc.dram_tensor("wd", [D, I], F32, kind="ExternalInput")
    clipD = nc.dram_tensor("clip", [1, 2], F32, kind="ExternalInput")
    identD = nc.dram_tensor("ident", [128, 128], BF16, kind="ExternalInput")
    xout = nc.dram_tensor("xout", [T, D], F32, kind="ExternalOutput")

    xnTD = nc.dram_tensor("xnT", [D, T], BF16)
    hDD = nc.dram_tensor("hd", [I, T], BF16)
    x1D = nc.dram_tensor("x1", [T, D], F32)
    x2D = nc.dram_tensor("x2", [T, D], F32)
    qtD = nc.dram_tensor("qt", [D, T], BF16)
    ktoD = nc.dram_tensor("kto", [D, T], BF16)
    ktgD = nc.dram_tensor("ktg", [2 * D, T], BF16)
    voD = nc.dram_tensor("vo", [T, D], BF16)
    vgD = nc.dram_tensor("vg", [2 * T, D], BF16)
    wgS = nc.dram_tensor("wgs", [D, I], BF16)
    wuS = nc.dram_tensor("wus", [D, I], BF16)
    wdS = nc.dram_tensor("wds", [I, D], BF16)
    scrD = nc.dram_tensor("scr", [1, 128], F32)
    ccB = [nc.dram_tensor(f"cc{j}", [1, 8], F32) for j in range(4)]

    A = nc.alloc_sbuf_tensor_at
    ident = A("ident", [128, 128], BF16, offset=OFF_IDENT)
    identF = A("identf", [128, 128], F32, offset=18944)
    bq_pp = A("bq_pp", [128, 8], F32, offset=OFF_BQ)
    bk_pp = A("bk_pp", [128, 8], F32, offset=OFF_BK)
    bv_row = A("bv_row", [1, D], BF16, offset=OFF_BVROW)
    bo_row = A("bo_row", [1, D], BF16, offset=OFF_BOROW)
    ones_b = A("ones_b", [1, 128], BF16, offset=OFF_ONESB)
    ones_f = A("ones_f", [1, 128], F32, offset=OFF_ONESF)
    clip_sb = A("clip_sb", [1, 2], F32, offset=OFF_CLIP)
    lnst = A("lnst", [128, 12], F32, offset=OFF_LNST)
    row128 = A("row128", [1, 128], F32, offset=OFF_ROW128)
    rs_g = A("rs_g", [128, IC], F32, offset=OFF_RSG)
    rs_u = A("rs_u", [128, IC], F32, offset=OFF_RSU)
    csg = A("csg", [128, IC], F32, offset=OFF_CSG)
    csu = A("csu", [128, IC], F32, offset=OFF_CSU)
    xt = [A(f"xt{k}", [128, D], F32, offset=OFF_XT + k * 4096) for k in range(2)]
    outt = [A(f"outt{k}", [128, D], F32, offset=OFF_OUTT + k * 4096) for k in range(2)]
    xnbf = [A(f"xnbf{k}", [128, D], BF16, offset=OFF_XNBF + k * 2048) for k in range(2)]
    tevac = [A(f"tevac{k}", [128, 8, 128], BF16, offset=OFF_TEVAC + k * 2048)
             for k in range(2)]
    wo_sb = A("wo_sb", [128, 8, D], BF16, offset=OFF_WO)
    wgT = A("wgT", [128, 8, I], BF16, offset=OFF_W + FO_WG)
    wuT = A("wuT", [128, 8, I], BF16, offset=OFF_W + FO_WU)
    wdT = A("wdT", [128, IC, D], BF16, offset=OFF_W + FO_WD)
    xq = [A(f"xq{k}", [128, 8, 512], BF16, offset=OFF_W + FO_XQ + k * 8192)
          for k in range(2)]
    hev = [A(f"hev{k}", [128, 512], BF16, offset=OFF_W + FO_HEV + k * 1024)
           for k in range(2)]
    silu = [A(f"silu{k}", [128, 512], F32, offset=OFF_W + FO_SILU + k * 2048)
            for k in range(2)]
    hq = [A(f"hq{k}", [128, IC, 128], BF16, offset=OFF_W + FO_HQ + k * 5632)
          for k in range(2)]
    hqfs = [A("hqf0", [128, 1408], F32, offset=OFF_W + FO_HQF),
            A("hqf1", [128, 1408], F32, offset=OFF_W + FO_XQ)]
    qscs = [A("qsc0", [128, 512], F32, offset=OFF_W + FO_HQF),
            A("qsc1", [128, 512], F32, offset=OFF_W + FO_HQF + 2048)]
    p0stg = A("p0stg", [128, I], F32, offset=OFF_W + FO_P0STG)
    awq = A("awq", [128, 8, D], BF16, offset=OFF_W + AO_WQ)
    awk = A("awk", [128, 8, D], BF16, offset=OFF_W + AO_WK)
    awv = A("awv", [128, 8, D], BF16, offset=OFF_W + AO_WV)
    xn2 = [A(f"xn2{k}", [128, 8, 512], BF16, offset=OFF_W + AO_XN2 + k * 8192)
           for k in range(2)]
    kev = [A(f"kev{k}", [128, 512], BF16, offset=OFF_W + AO_KEV + k * 1024)
           for k in range(2)]
    vev = [A(f"vev{k}", [128, D], BF16, offset=OFF_W + AO_VEV + k * 2048)
           for k in range(2)]
    wst = [A(f"wst{k}", [128, D], F32, offset=OFF_W + AO_WST + k * 4096)
           for k in range(2)]
    qtb = [A(f"qtb{k}", [128, 8, 256], BF16, offset=OFF_W + FL_QTB + k * 4096)
           for k in range(2)]
    qtb3 = A("qtb2x", [128, 8, 256], BF16, offset=OFF_W + FL_AOT + 2048)
    ktb = [A(f"ktb{k}", [128, 8, 512], BF16, offset=OFF_W + FL_KTB + k * 8192)
           for k in range(2)]
    vtb = [A(f"vtb{k}", [128, 4, D], BF16, offset=OFF_W + FL_VTB + k * 8192)
           for k in range(2)]
    Ubuf = [A(f"ubuf{k}", [128, S], F32, offset=OFF_W + FL_U + k * 32768)
            for k in range(2)]
    Pbuf = [A(f"pbuf{k}", [128, S], BF16, offset=OFF_W + FL_P + k * 16384)
            for k in range(2)]
    t1s = [A(f"t1s{k}", [128, 512], F32,
             offset=(OFF_W + FL_T1 + k * 2048 if k < 2 else OFF_W + 167936))
           for k in range(3)]
    t2s = [A(f"t2s{k}", [128, 512], F32,
             offset=(OFF_W + FL_T2 + k * 2048 if k < 2 else OFF_W + 169984))
           for k in range(3)]
    rsc = [A(f"rsc{k}", [128, 512], F32,
             offset=(OFF_W + FL_R + k * 2048 if k < 2 else OFF_W + 172032))
           for k in range(3)]
    ptsb = [A(f"ptsb{k}", [128, 8, 128], BF16, offset=OFF_W + FL_PT + k * 2048)
            for k in range(2)]
    ao_sb2 = [A(f"ao_sb{k}", [128, D], BF16,
                offset=OFF_W + FL_AOT + 6144 + k * 2048) for k in range(2)]
    aoT_sb2 = [A(f"aoT_sb{k}", [128, 8, 128], BF16,
                 offset=OFF_W + FL_AO + k * 2048) for k in range(2)]
    lrp = [A(f"lrp{k}", [128, 8], F32, offset=OFF_W + 174080 + k * 32)
           for k in range(2)]

    p0 = nc.alloc_psum_tensor("p0", [128, 512], F32)
    p1 = nc.alloc_psum_tensor("p1", [128, 512], F32)
    pacc0 = nc.alloc_psum_tensor("pacc0", [128, D], F32)
    pacc1 = nc.alloc_psum_tensor("pacc1", [128, D], F32)
    pt0 = nc.alloc_psum_tensor("pt0", [128, 8, 128], BF16)
    pt1 = nc.alloc_psum_tensor("pt1", [128, 8, 128], BF16)
    pts = [pt0, pt1]
    paccs = [pacc0, pacc1]

    B = Builder(nc)
    sl = B.slot

    def xpart_request(acc_slot, ccb, pre_waits, mm_extra, tag):
        B.emit("tensor", lambda e, acc_slot=acc_slot: e.matmul(
            p0[0:1, 0:128], acc_slot[:, 0:1], identF.ap(),
            start=True, stop=True),
            waits=list(pre_waits) + list(mm_extra) + [("p0_ones", 6)],
            inc="cc_bc")
        B.emit("vector", lambda e: e.tensor_reduce(
            sl("ccrow", (1, 8))[0:1, 0:1], p0[0:1, 0:128], AX.X, ALU.max),
            waits=[("cc_bc", B.c("cc_bc"))], inc="cc_rm")
        B.dma("sync", ccb.ap()[0:1, :], sl("ccrow", (1, 8))[0:1, :], "cc_d",
              waits=[("cc_rm", B.c("cc_rm"))])
        B.emit("gpsimd", lambda e, ccb=ccb: e.collective_compute(
            "AllReduce", ALU.max, replica_groups=[[0, 1, 2, 3, 4, 5, 6, 7]],
            ins=[flat(ccb, 8)], outs=[flat(ccb, 8)]),
            waits=[("cc_d", B.c("cc_d"))], inc="cc_cc")
        B.dma("sync", sl(tag + "_cr")[0:1, 0:1], ccb.ap()[0:1, 0:1], "cc_d",
              waits=[("cc_cc", B.c("cc_cc"))])

    def xpart_complete(out_slot, tag):
        B.emit("tensor", lambda e, tag=tag: e.matmul(
            p0[:, 0:1], ones_f[0:1, :], sl(tag + "_cr")[0:1, 0:1],
            start=True, stop=True),
            waits=[("cc_d", B.c("cc_d"))], inc="cc_bc")
        B.emit("vector", lambda e, out_slot=out_slot: e.tensor_copy(
            out_slot[:, 0:1], p0[:, 0:1]),
            waits=[("cc_bc", B.c("cc_bc"))], inc="cc_done")
        return "cc_done"

    def xpart_max_allreduce(acc_slot, ccb, out_slot, pre_waits, mm_extra, tag):
        xpart_request(acc_slot, ccb, pre_waits, mm_extra, tag)
        return xpart_complete(out_slot, tag)

    deferred_stores = []

    def emit_setup():
        B.dma("sync", ident.ap(), identD.ap(), "p0_misc")
        B.dma("sync", bq_pp.ap(), bqpD.ap(), "p0_misc")
        B.dma("sync", bk_pp.ap(), bkpD.ap(), "p0_misc")
        B.dma("sync", clip_sb.ap(), clipD.ap(), "p0_misc")
        B.emit("vector", lambda e: e.memset(ones_b.ap(), 1.0), inc="p0_ones")
        B.emit("vector", lambda e: e.memset(ones_f.ap(), 1.0), inc="p0_ones")
        B.emit("vector", lambda e: e.memset(sl("epsb")[:, 0:1], EPS), inc="p0_ones")
        B.emit("vector", lambda e: e.memset(sl("ccrow", (1, 8))[0:1, :], 0.0),
               inc="p0_ones")
        B.emit("vector", lambda e: e.memset(sl("zb")[:, 0:1], 0.0),
               inc="p0_ones")
        B.dma("sync", wst[0][0:1, :], bvrD.ap(), "p0_bv")
        B.emit("vector", lambda e: e.tensor_copy(bv_row.ap(), wst[0][0:1, :]),
               waits=[("p0_bv", 16)], inc="p0_bvc")
        B.dma("sync", wst[1][0:1, :], borD.ap(), "p0_bv",
              waits=[("p0_bvc", 1)])
        B.emit("vector", lambda e: e.tensor_copy(bo_row.ap(), wst[1][0:1, :]),
               waits=[("p0_bv", 32)], inc="p0_bvc")
        B.emit("scalar", lambda e: e.activation(
            out=identF.ap(), in_=ident.ap(), func=AF.Identity,
            scale=1.0, bias=sl("zb")[:, 0:1]),
            waits=[("p0_misc", 64), ("p0_ones", 5)], inc="p0_ones")
        B.emit("scalar", lambda e: e.activation(
            out=sl("sgtmp", (1, 2))[0:1, :], in_=clip_sb[0:1, :], func=AF.Tanh,
            scale=0.5, bias=sl("zb")[0:1, 0:1]),
            waits=[("p0_misc", 64), ("p0_ones", 5)], inc="p0_sg")
        B.emit("vector", lambda e: e.tensor_scalar(
            sl("sgrow", (1, 2))[0:1, :], sl("sgtmp", (1, 2))[0:1, :],
            0.5, 0.5, ALU.mult, ALU.add), waits=[("p0_sg", 1)], inc="p0_sg")
        B.emit("tensor", lambda e: e.matmul(
            p1[:, 0:2], ones_f[0:1, :], sl("sgrow", (1, 2))[0:1, :],
            start=True, stop=True),
            waits=[("p0_sg", 2), ("p0_ones", 2)], inc="p0_sg")
        B.emit("vector", lambda e: e.tensor_copy(
            sl("sigc", (128, 2))[:, :], p1[:, 0:2]),
            waits=[("p0_sg", 3)], inc="p0_sg")
        for j in range(DC):
            par = j % 2
            B.dma2("sync", wst[par].ap(),
                   bass.AP(woTD, j * 128 * D, [[D, 128], [1, D]]), "aw_ld", j,
                   waits=[("aw_c", B.h("aw_c", B.n("aw_c") - 1)),
                          ("p0_bvc", 2)])
            B.emit("vector", lambda e, j=j, par=par: e.tensor_copy(
                wo_sb[:, j, :], wst[par].ap()),
                waits=[B.dtot("aw_ld", j)], inc="aw_c")

        def quant_w(src, rs_tile, dstT, stash, nrt_, fold_back, ncol):
            nchk = ncol // 128
            gpt = (nchk + 7) // 8
            nbuf = 2 if ncol == D else 1
            raws = [A(f"qraw{k}_" + stash.name, [128, ncol], F32,
                      offset=OFF_W + FO_XQ + k * 4096) for k in range(nbuf)]
            stgs = [A(f"qstg{k}_" + stash.name, [128, ncol], F32,
                      offset=OFF_W + FO_P0STG + k * 4096) for k in range(nbuf)]
            wints = [A(f"qwint{k}_" + stash.name, [128, ncol], BF16,
                       offset=OFF_W + FO_P0WI + k * 2048) for k in range(nbuf)]
            n_sc = B.n("q_sc")
            n_cl = B.n("q_cl")
            n_tr = B.n("q_tr")
            n_ev = B.n("q_ev")

            def stage_a(t):
                raw = raws[t % nbuf]
                stg = stgs[t % nbuf]
                p = t % 2
                B.dma2("sync", raw.ap(),
                       bass.AP(src, t * 128 * ncol, [[ncol, 128], [1, ncol]]),
                       "q_ld", t % nbuf,
                       waits=[("q_sc", B.h("q_sc", n_sc + t + 1 - nbuf))])
                B.emit("vector", lambda e, raw=raw, p=p: e.tensor_reduce(
                    sl(f"q_rmax{p}")[:, 0:1], raw.ap(), AX.X, ALU.max,
                    apply_absolute_value=True),
                    waits=[B.dtot("q_ld", t % nbuf)], inc="q_rm")
                B.emit("vector", lambda e, p=p: e.reciprocal(
                    sl(f"q_rinv{p}")[:, 0:1], sl(f"q_rmax{p}")[:, 0:1]),
                    waits=[("q_rm", B.c("q_rm"))], inc="q_rm")
                B.emit("vector", lambda e, p=p: e.tensor_scalar(
                    sl(f"q_r127{p}")[:, 0:1], sl(f"q_rinv{p}")[:, 0:1], 127.0,
                    None, ALU.mult), waits=[("q_rm", B.c("q_rm"))], inc="q_rm")
                if rs_tile is not None:
                    B.emit("vector", lambda e, t=t, rs_tile=rs_tile, p=p:
                           e.tensor_scalar(
                               rs_tile[:, t:t + 1], sl(f"q_rmax{p}")[:, 0:1],
                               1.0 / 127.0, None, ALU.mult), inc="q_rm")
                B.emit("scalar", lambda e, raw=raw, stg=stg, ncol=ncol, p=p:
                       e.activation(
                           out=stg[:, 0:ncol], in_=raw.ap(), func=AF.Identity,
                           scale=sl(f"q_r127{p}")[:, 0:1],
                           bias=sl("zb")[:, 0:1]),
                       waits=[("q_rm", B.c("q_rm")),
                              ("q_cl", B.h("q_cl", n_cl + t + 1 - nbuf))],
                       inc="q_sc")

            def stage_b(t):
                stg = stgs[t % nbuf]
                wint = wints[t % nbuf]
                p = t % 2
                wint_wait = ("q_tr", B.h("q_tr", n_tr + (t + 1 - nbuf) * gpt))
                if fold_back:
                    B.emit("vector", lambda e, stg=stg, ncol=ncol: e.tensor_scalar(
                        stg[:, 0:ncol], stg[:, 0:ncol], RND, RND, ALU.add,
                        ALU.subtract),
                        waits=[("q_sc", B.h("q_sc", n_sc + t + 1))], inc="q_rd")
                    B.emit("vector", lambda e, wint=wint, stg=stg, ncol=ncol, p=p:
                           e.tensor_scalar(
                               wint.ap(), stg[:, 0:ncol],
                               sl(f"q_rmax{p}")[:, 0:1],
                               1.0 / 127.0, ALU.mult, ALU.mult),
                           waits=[("q_rd", B.c("q_rd")), wint_wait], inc="q_cl")
                else:
                    B.emit("vector", lambda e, wint=wint, stg=stg, ncol=ncol:
                           e.tensor_scalar(
                               wint.ap(), stg[:, 0:ncol], RND, RND, ALU.add,
                               ALU.subtract),
                           waits=[("q_sc", B.h("q_sc", n_sc + t + 1)),
                                  wint_wait], inc="q_cl")
                for gi in range(gpt):
                    j0 = gi * 8
                    jn = min(8, nchk - j0)
                    gg = t * gpt + gi
                    par = gg % 2
                    for jj in range(jn):
                        B.emit("tensor",
                               lambda e, par=par, jj=jj, j0=j0, wint=wint:
                               e.transpose(
                                   pts[par][:, jj, :],
                                   wint[:, (j0 + jj) * 128:(j0 + jj + 1) * 128],
                                   ident.ap()),
                               waits=([("q_cl", B.c("q_cl")),
                                       ("q_ev", B.h("q_ev", n_ev + gg - 1))]
                                      if jj == 0 else ()),
                               inc=("q_tr" if jj == jn - 1 else None))
                    B.emit("scalar",
                           lambda e, par=par, j0=j0, jn=jn, t=t, dstT=dstT:
                           e.copy(
                               dstT[:, j0:j0 + jn, t * 128:(t + 1) * 128],
                               pts[par][:, 0:jn, :]),
                           waits=[("q_tr", B.c("q_tr"))], inc="q_ev")

            if nbuf == 2:
                stage_a(0)
                for t in range(nrt_):
                    if t + 1 < nrt_:
                        stage_a(t + 1)
                    stage_b(t)
            else:
                for t in range(nrt_):
                    stage_a(t)
                    stage_b(t)

            def stores(nchk=nchk, stash=stash, dstT=dstT, nrt_=nrt_,
                       ev=B.c("q_ev")):
                for j in range(nchk):
                    B.dma("sync", bass.AP(stash, j * 128 * nrt_ * 128,
                                          [[nrt_ * 128, 128],
                                           [1, nrt_ * 128]]),
                          dstT[:, j, :], "q_st", waits=[("q_ev", ev)])
            deferred_stores.append(stores)

        quant_w(wgD, rs_g, wgT, wgS, IC, False, D)
        quant_w(wuD, rs_u, wuT, wuS, IC, False, D)
        return B.c("q_ev"), quant_w

    class LNPass:

        def __init__(self, track_slot, act_norm=False):
            self.n_rc = B.n("l_rc"); self.n_nm = B.n("l_nm")
            self.n_tr = B.n("l_tr"); self.n_ev = B.n("l_ev")
            self.n_sq = B.n("l_sq")
            self.track = track_slot
            self.act_norm = act_norm
            if track_slot:
                B.emit("vector", lambda e, t=track_slot: e.memset(
                    sl(t)[:, 0:1], 0.0), inc="l_mx")

        def stats(self, i, src_ap, gates):
            p = i % 2
            B.emit("vector", lambda e, src_ap=src_ap, p=p: e.bn_stats(
                sl(f"lnst{p}", (128, 12))[:, 0:6], src_ap[:, 0:512]),
                waits=list(gates) + [("l_sq", B.h("l_sq", self.n_sq + i - 1))])
            B.emit("vector", lambda e, src_ap=src_ap, p=p: e.bn_stats(
                sl(f"lnst{p}", (128, 12))[:, 6:12], src_ap[:, 512:]))
            B.emit("vector", lambda e, p=p: e.bn_aggr(
                sl(f"lnmv{p}", (128, 2))[:, :], sl(f"lnst{p}", (128, 12)).ap()),
                inc="l_st")
            B.emit("scalar", lambda e, p=p: e.activation(
                out=sl(f"rstd{p}")[:, 0:1], in_=sl(f"lnmv{p}", (128, 2))[:, 1:2],
                func=AF.Sqrt, bias=sl("epsb")[:, 0:1]),
                waits=[("l_st", B.c("l_st"))], inc="l_sq")

        def norm(self, i, src_ap):
            par = i % 2
            p = i % 2
            B.emit("vector", lambda e, p=p: e.reciprocal(
                sl(f"rstd2{p}")[:, 0:1], sl(f"rstd{p}")[:, 0:1]),
                waits=[("l_sq", B.h("l_sq", self.n_sq + i + 1))], inc="l_rc")
            if self.act_norm:
                B.emit("vector", lambda e, p=p: e.tensor_scalar(
                    sl(f"lnnb{p}")[:, 0:1], sl(f"lnmv{p}", (128, 2))[:, 0:1],
                    sl(f"rstd2{p}")[:, 0:1], -1.0, ALU.mult, ALU.mult),
                    waits=[("l_rc", B.c("l_rc"))], inc="l_rc")
                B.emit("scalar", lambda e, par=par, src_ap=src_ap, p=p:
                       e.activation(
                           out=xnbf[par].ap(), in_=src_ap, func=AF.Identity,
                           scale=sl(f"rstd2{p}")[:, 0:1],
                           bias=sl(f"lnnb{p}")[:, 0:1]),
                       waits=[("l_rc", B.c("l_rc")),
                              ("l_tr", B.h("l_tr", self.n_tr + i - 1))],
                       inc="l_nm")
            else:
                B.emit("vector", lambda e, par=par, src_ap=src_ap, p=p:
                       e.tensor_scalar(
                           xnbf[par].ap(), src_ap,
                           sl(f"lnmv{p}", (128, 2))[:, 0:1],
                           sl(f"rstd2{p}")[:, 0:1], ALU.subtract, ALU.mult),
                       waits=[("l_rc", B.c("l_rc")),
                              ("l_tr", B.h("l_tr", self.n_tr + i - 1))],
                       inc="l_nm")
            if self.track:
                B.emit("vector", lambda e, par=par: e.tensor_reduce(
                    sl("tmax")[:, 0:1], xnbf[par].ap(), AX.X, ALU.max,
                    apply_absolute_value=True),
                    waits=([("l_nm", B.c("l_nm"))] if self.act_norm else ()))
                B.emit("vector", lambda e, t=self.track: e.tensor_max(
                    sl(t)[:, 0:1], sl(t)[:, 0:1], sl("tmax")[:, 0:1]),
                    inc="l_mx")
            for j in range(DC):
                B.emit("tensor", lambda e, par=par, j=j: e.transpose(
                    pts[par][:, j, :], xnbf[par][:, j * 128:(j + 1) * 128],
                    ident.ap()),
                    waits=([("l_nm", B.c("l_nm")),
                            ("l_ev", B.h("l_ev", self.n_ev + i - 1))]
                           if j == 0 else ()),
                    inc=("l_tr" if j == DC - 1 else None))
            B.emit("scalar", lambda e, par=par: e.copy(
                tevac[par].ap(), pts[par].ap()),
                waits=[("l_tr", B.c("l_tr")), B.dtot("l_xn", i)],
                inc="l_ev")
            B.dma2("sync",
                   bass.AP(xnTD, i * 128, [[T, 128], [128 * T, 8], [1, 128]]),
                   tevac[par].ap(), "l_xn", i, waits=[("l_ev", B.c("l_ev"))])

        def tile(self, i, src_ap, gates):
            self.stats(i, src_ap, gates)
            self.norm(i, src_ap)

        def norm_done(self, i):
            return ("l_nm", B.h("l_nm", self.n_nm + i + 1))

    def emit_ln_dram(src, track_slot, entry_waits=()):
        ln = LNPass(track_slot, act_norm=True)
        n_nm0 = ln.n_nm

        def load(i, w0=()):
            par = i % 2
            w = [("l_nm", B.h("l_nm", n_nm0 + i - 1))] + list(w0)
            B.dma2("sync", xt[par].ap(),
                   bass.AP(src, i * 128 * D, [[D, 128], [1, D]]), "l_ld", i,
                   waits=w)
            ln.stats(i, xt[par].ap(), [B.dtot("l_ld", i)])

        load(0, entry_waits)
        for i in range(nt):
            if i + 1 < nt:
                load(i + 1)
            ln.norm(i, xt[i % 2].ap())

    def emit_scales_x(Mslot):
        B.emit("vector", lambda e, M=Mslot: e.tensor_mul(
            sl("w_sx")[:, 0:1], M[:, 0:1], sl("sigc", (128, 2))[:, 0:1]), inc="sc")
        B.emit("vector", lambda e: e.reciprocal(
            sl("w_si")[:, 0:1], sl("w_sx")[:, 0:1]),
            waits=[("sc", B.c("sc"))], inc="sc")
        B.emit("vector", lambda e: e.tensor_scalar(
            sl("inv_sx")[:, 0:1], sl("w_si")[:, 0:1], 127.0, None, ALU.mult),
            waits=[("sc", B.c("sc"))], inc="sc")
        B.emit("vector", lambda e: e.tensor_scalar(
            sl("w_ss")[:, 0:1], sl("w_sx")[:, 0:1], 1.0 / 127.0, None, ALU.mult),
            waits=[("sc", B.c("sc"))], inc="sc")
        B.emit("vector", lambda e: e.tensor_scalar(
            csg.ap(), rs_g.ap(), sl("w_ss")[:, 0:1], None, ALU.mult),
            waits=[("sc", B.c("sc"))], inc="sc")
        B.emit("vector", lambda e: e.tensor_scalar(
            csu.ap(), rs_u.ap(), sl("w_ss")[:, 0:1], None, ALU.mult),
            waits=[("sc", B.c("sc"))], inc="sc")

    def emit_scales_h(Mslot):
        B.emit("vector", lambda e, M=Mslot: e.tensor_mul(
            sl("w_sd")[:, 0:1], M[:, 0:1], sl("sigc", (128, 2))[:, 1:2]), inc="sc")
        B.emit("vector", lambda e: e.reciprocal(
            sl("w_di")[:, 0:1], sl("w_sd")[:, 0:1]),
            waits=[("sc", B.c("sc"))], inc="sc")
        B.emit("vector", lambda e: e.tensor_scalar(
            sl("inv_sd")[:, 0:1], sl("w_di")[:, 0:1], 127.0, None, ALU.mult),
            waits=[("sc", B.c("sc"))], inc="sc")
        B.emit("vector", lambda e: e.tensor_scalar(
            sl("half_sd")[:, 0:1], sl("w_sd")[:, 0:1], 0.5 / 127.0, None, ALU.mult),
            waits=[("sc", B.c("sc"))], inc="sc")

    def emit_ffn_b(hamax_slot):
        n_q3 = B.n("b_q3"); n_mg = B.n("b_mg")
        n_mu = B.n("b_mu"); n_si = B.n("b_si"); n_h = B.n("b_h")
        n_hv = B.n("b_hv")
        xn_waits = B.dboth("l_xn")
        sc_val = B.c("sc")
        def quant_chunk(g, j):
            par = g % 2
            m = g * DC + j
            qp = m % 2
            B.emit("scalar", lambda e, par=par, j=j, qp=qp: e.activation(
                out=qscs[qp].ap(), in_=xq[par][:, j, :], func=AF.Identity,
                scale=sl("inv_sx")[:, 0:1], bias=sl("zb")[:, 0:1]),
                waits=[B.dtot("b_ld", g), ("sc", sc_val),
                       ("b_q3", B.h("b_q3", n_q3 + m - 1))],
                inc="b_q1")
            B.emit("vector", lambda e, qp=qp: e.tensor_scalar(
                qscs[qp].ap(), qscs[qp].ap(), RND, RND, ALU.add, ALU.subtract),
                waits=[("b_q1", B.c("b_q1"))], inc="b_q2")
            B.emit("vector", lambda e, par=par, j=j, qp=qp: e.tensor_scalar(
                xq[par][:, j, :], qscs[qp].ap(), -128.0, 127.0, ALU.max,
                ALU.min), waits=[("b_q2", B.c("b_q2"))], inc="b_q3")

        qsc_done = B.c("q_sc")
        def load_group(g):
            par = g % 2
            w = [("b_mu", B.h("b_mu", n_mu + (g - 1) * IC))] + xn_waits
            if g == 0:
                w = w + [("q_sc", qsc_done)]
            for hc in range(DC):
                B.dma2("sync", xq[par][:, hc, :],
                       bass.AP(xnTD, hc * 128 * T + g * 512,
                               [[T, 128], [1, 512]]),
                       "b_ld", g, waits=(w if hc == 0 else ()))

        load_group(0)
        for j in range(DC):
            quant_chunk(0, j)
        for g in range(ng):
            par = g % 2
            q3_need = B.c("b_q3")
            if g + 1 < ng:
                load_group(g + 1)
            for ic in range(IC):
                k = g * IC + ic
                pi = ic % 2
                pg = p0.ap() if pi == 0 else pacc0[:, 0:512]
                pu = p1.ap() if pi == 0 else pacc0[:, 512:]
                wgq = [("b_q3", q3_need), ("b_si", B.h("b_si", n_si + k - 1))]
                wuq = [("b_h", B.h("b_h", n_h + k - 1))]
                for hc in range(DC):
                    B.emit("tensor", lambda e, par=par, hc=hc, ic=ic, pg=pg:
                           e.matmul(pg, wgT[:, hc, ic * 128:(ic + 1) * 128],
                                    xq[par][:, hc, :], start=(hc == 0),
                                    stop=(hc == DC - 1)),
                           waits=(wgq if hc == 0 else ()),
                           inc=("b_mg" if hc == DC - 1 else None))
                for hc in range(DC):
                    B.emit("tensor", lambda e, par=par, hc=hc, ic=ic, pu=pu:
                           e.matmul(pu, wuT[:, hc, ic * 128:(ic + 1) * 128],
                                    xq[par][:, hc, :], start=(hc == 0),
                                    stop=(hc == DC - 1)),
                           waits=(wuq if hc == 0 else ()),
                           inc=("b_mu" if hc == DC - 1 else None))
                B.emit("scalar", lambda e, pg=pg, pi=pi, ic=ic: e.activation(
                    out=silu[pi].ap(), in_=pg, func=AF.Silu,
                    scale=csg[:, ic:ic + 1], bias=sl("zb")[:, 0:1]),
                    waits=[("b_mg", B.c("b_mg")), ("b_h", B.h("b_h", n_h + k - 1))],
                    inc="b_si")
                B.emit("vector", lambda e, pu=pu, pi=pi, ic=ic:
                       e.scalar_tensor_tensor(
                           hev[pi].ap(), pu, csu[:, ic:ic + 1], silu[pi].ap(),
                           ALU.mult, ALU.mult),
                       waits=[("b_mu", B.c("b_mu")), ("b_si", B.c("b_si")),
                              B.dtot("b_hd", k),
                              ("b_hv", B.h("b_hv", n_hv + k - 1))], inc="b_h")
                B.emit("vector", lambda e, pi=pi, k=k: e.tensor_reduce(
                    sl("hacc", (128, 176))[:, k:k + 1], hev[pi].ap(), AX.X,
                    ALU.max, apply_absolute_value=True), inc="b_hv")
                B.dma2("sync", bass.AP(hDD, ic * 128 * T + g * 512,
                                       [[T, 128], [1, 512]]),
                       hev[pi].ap(), "b_hd", k, waits=[("b_h", B.c("b_h"))])
                if g + 1 < ng and 6 <= ic < 6 + DC:
                    quant_chunk(g + 1, ic - 6)
        B.emit("vector", lambda e, t=hamax_slot, n=ng * IC: e.tensor_reduce(
            sl(t)[:, 0:1], sl("hacc", (128, 176))[:, 0:n], AX.X, ALU.max),
            inc="b_hm")

    def emit_ffn_c(res_src, xdst, first_mm_waits=(), tile_hook=None,
                   outt_guard=None):
        n_q3 = B.n("c_q3"); n_mm = B.n("c_mm")
        n_ev = B.n("c_ev")
        sc_val = B.c("sc")

        def tile_front(tt):
            par = tt % 2
            w = [("c_mm", B.h("c_mm", n_mm + tt - 1))]
            for half in range(2):
                ic0 = 11 * half
                B.dma2("sync", hq[par][:, ic0:ic0 + 11, :],
                       bass.AP(hDD, ic0 * 128 * T + tt * 128,
                               [[T, 128], [128 * T, 11], [1, 128]]),
                       "c_ld", tt, waits=(w if half == 0 else ()))
            for half in range(2):
                ic0 = 11 * half
                m = 2 * tt + half
                qp = m % 2
                B.emit("scalar", lambda e, par=par, ic0=ic0, qp=qp: e.activation(
                    out=hqfs[qp].ap(), in_=hq[par][:, ic0:ic0 + 11, :],
                    func=AF.Identity, scale=sl("inv_sd")[:, 0:1],
                    bias=sl("zb")[:, 0:1]),
                    waits=[B.dtot("c_ld", tt), ("sc", sc_val),
                           ("c_q3", B.h("c_q3", n_q3 + m - 1))],
                    inc="c_q1")
                B.emit("vector", lambda e, qp=qp: e.tensor_scalar(
                    hqfs[qp].ap(), hqfs[qp].ap(), RND, RND, ALU.add,
                    ALU.subtract),
                    waits=[("c_q1", B.c("c_q1"))], inc="c_q2")
                B.emit("vector", lambda e, par=par, ic0=ic0, qp=qp: e.tensor_scalar(
                    hq[par][:, ic0:ic0 + 11, :], hqfs[qp].ap(), -128.0, 127.0,
                    ALU.max, ALU.min),
                    waits=[("c_q2", B.c("c_q2"))], inc="c_q3")

        tile_front(0)
        for tt in range(nt):
            par = tt % 2
            acc = paccs[par]
            wmm = [("c_q3", B.c("c_q3")), ("c_ev", B.h("c_ev", n_ev + tt - 1))]
            if tt == 0:
                wmm = wmm + list(first_mm_waits)
            for ic in range(IC):
                B.emit("tensor", lambda e, par=par, ic=ic, acc=acc: e.matmul(
                    acc[:, 0:512], hq[par][:, ic, :], wdT[:, ic, 0:512],
                    start=(ic == 0), stop=(ic == IC - 1)),
                    waits=(wmm if ic == 0 else ()))
                B.emit("tensor", lambda e, par=par, ic=ic, acc=acc: e.matmul(
                    acc[:, 512:], hq[par][:, ic, :], wdT[:, ic, 512:],
                    start=(ic == 0), stop=(ic == IC - 1)),
                    inc=("c_mm" if ic == IC - 1 else None))
            B.dma2("sync", xt[par].ap(),
                   bass.AP(res_src, tt * 128 * D, [[D, 128], [1, D]]), "c_xl",
                   tt, waits=[("c_ev", B.h("c_ev", n_ev + tt - 1))])
            if tt + 1 < nt:
                tile_front(tt + 1)
            wev = [("c_mm", B.c("c_mm")), B.dtot("c_xl", tt),
                   B.dtot("c_xo", tt)]
            if outt_guard is not None:
                wev.append(outt_guard(tt))
            B.emit("vector", lambda e, par=par, acc=acc: e.scalar_tensor_tensor(
                outt[par].ap(), acc.ap(), sl("half_sd")[:, 0:1], xt[par].ap(),
                ALU.mult, ALU.add),
                waits=wev, inc="c_ev")
            B.dma2("sync", bass.AP(xdst, tt * 128 * D, [[D, 128], [1, D]]),
                   outt[par].ap(), "c_xo", tt, waits=[("c_ev", B.c("c_ev"))])
            if tile_hook is not None:
                tile_hook(tt)

    def emit_attn_wload(entry_waits):
        first = True
        for wsrc, dst in ((wqTD, awq), (wkTD, awk), (wvTD, awv)):
            for j in range(DC):
                idx = B.n("aw_c")
                par = idx % 2
                w = [("aw_c", B.h("aw_c", idx - 1))]
                if first:
                    w = w + list(entry_waits)
                    first = False
                B.dma2("sync", wst[par].ap(),
                       bass.AP(wsrc, j * 128 * D, [[D, 128], [1, D]]), "aw_ld",
                       idx, waits=w)
                B.emit("vector", lambda e, dst=dst, j=j, par=par: e.tensor_copy(
                    dst[:, j, :], wst[par].ap()),
                    waits=[B.dtot("aw_ld", idx)], inc="aw_c")

    def emit_proj():
        xn_waits = B.dboth("l_xn")
        aw_val = B.c("aw_c")

        def one_pass(pre, wsb, bias_pp, dstD, s_mm, s_ev, s_dma, prev_mm=None,
                     prev_dma=None):
            n_mm = B.n(s_mm); n_ev = B.n(s_ev)

            def g_load(g):
                par = g % 2
                w = [(s_mm, B.h(s_mm, n_mm + (g - 1) * DC)),
                     ("aw_c", aw_val)] + xn_waits
                if g < 2 and prev_mm is not None:
                    w.append(prev_mm)
                for hc in range(DC):
                    B.dma2("sync", xn2[par][:, hc, :],
                           bass.AP(xnTD, hc * 128 * T + g * 512,
                                   [[T, 128], [1, 512]]),
                           "pj_ld", g, waits=(w if hc == 0 else ()))

            g_load(0)
            for g in range(ng):
                par = g % 2
                for m in range(DC):
                    gm = g * DC + m
                    pp = p0 if m % 2 == 0 else p1
                    wmm = [B.dtot("pj_ld", g),
                           (s_ev, B.h(s_ev, n_ev + gm - 1))]
                    for hc in range(DC):
                        B.emit("tensor",
                               lambda e, par=par, hc=hc, m=m, pp=pp, wsb=wsb:
                               e.matmul(pp.ap(),
                                        wsb[:, hc, m * 128:(m + 1) * 128],
                                        xn2[par][:, hc, :],
                                        start=(hc == 0), stop=(hc == DC - 1)),
                               waits=(wmm if hc == 0 else ()),
                               inc=(s_mm if hc == DC - 1 else None))
                    wev = [(s_mm, B.c(s_mm)), B.dtot(s_dma, gm)]
                    if gm < 2 and prev_dma is not None:
                        wev += prev_dma
                    B.emit("vector",
                           lambda e, m=m, pp=pp, bias_pp=bias_pp: e.tensor_scalar(
                               kev[m % 2].ap(), pp.ap(), bias_pp[:, m:m + 1],
                               None, ALU.add),
                           waits=wev, inc=s_ev)
                    B.dma2("sync", bass.AP(dstD, m * 128 * T + g * 512,
                                           [[T, 128], [1, 512]]),
                           kev[m % 2].ap(), s_dma, gm, waits=[(s_ev, B.c(s_ev))])
                    if m == 0 and g + 1 < ng:
                        g_load(g + 1)

        one_pass("pk", awk, bk_pp, ktoD, "pk_mm", "pk_ev", "pk_dma")
        B.emit("gpsimd", lambda e: e.collective_compute(
            "AllGather", ALU.bypass,
            replica_groups=[[0, 1], [2, 3], [4, 5], [6, 7]],
            ins=[flat(ktoD, D * T)], outs=[flat(ktgD, 2 * D * T)]),
            waits=B.dboth("pk_dma"), inc="ag_kt")

        n_mm = B.n("pv_mm"); n_ev = B.n("pv_ev")
        pk_mm_done = ("pk_mm", B.c("pk_mm"))

        def v_load(g):
            par = g % 2
            w = [("pv_mm", B.h("pv_mm", n_mm + (g - 1) * 4)),
                 ("aw_c", aw_val)] + xn_waits
            if g < 2:
                w.append(pk_mm_done)
            for hc in range(DC):
                B.dma2("sync", xn2[par][:, hc, :],
                       bass.AP(xnTD, hc * 128 * T + g * 512,
                               [[T, 128], [1, 512]]),
                       "pj_ld", g, waits=(w if hc == 0 else ()))

        v_load(0)
        for g in range(ng):
            par = g % 2
            for sub in range(4):
                gs = g * 4 + sub
                acc = paccs[sub % 2]
                wmm = [B.dtot("pj_ld", g),
                       ("pv_ev", B.h("pv_ev", n_ev + gs - 1))]
                for hc in range(DC):
                    B.emit("tensor", lambda e, par=par, hc=hc, sub=sub, acc=acc:
                           e.matmul(acc[:, 0:512],
                                    xn2[par][:, hc, sub * 128:(sub + 1) * 128],
                                    awv[:, hc, 0:512], start=(hc == 0), stop=False),
                           waits=(wmm if hc == 0 else ()))
                    B.emit("tensor", lambda e, par=par, hc=hc, sub=sub, acc=acc:
                           e.matmul(acc[:, 512:],
                                    xn2[par][:, hc, sub * 128:(sub + 1) * 128],
                                    awv[:, hc, 512:], start=(hc == 0), stop=False))
                B.emit("tensor", lambda e, acc=acc: e.matmul(
                    acc[:, 0:512], ones_b[0:1, :], bv_row[0:1, 0:512],
                    start=False, stop=True))
                B.emit("tensor", lambda e, acc=acc: e.matmul(
                    acc[:, 512:], ones_b[0:1, :], bv_row[0:1, 512:],
                    start=False, stop=True), inc="pv_mm")
                B.emit("vector", lambda e, sub=sub, acc=acc: e.tensor_copy(
                    vev[sub % 2].ap(), acc.ap()),
                    waits=[("pv_mm", B.c("pv_mm")), B.dtot("pv_dma", gs)],
                    inc="pv_ev")
                B.dma2("sync", bass.AP(voD, (g * 512 + sub * 128) * D,
                                       [[D, 128], [1, D]]),
                       vev[sub % 2].ap(), "pv_dma", gs,
                       waits=[("pv_ev", B.c("pv_ev"))])
                if sub == 0 and g + 1 < ng:
                    v_load(g + 1)
        B.emit("gpsimd", lambda e: e.collective_compute(
            "AllGather", ALU.bypass,
            replica_groups=[[0, 1], [2, 3], [4, 5], [6, 7]],
            ins=[flat(voD, T * D)], outs=[flat(vgD, 2 * T * D)]),
            waits=B.dboth("pv_dma"), inc="ag_v")

        one_pass("pq", awq, bq_pp, qtD, "pq_mm", "pq_ev", "pq_dma",
                 prev_mm=("pv_mm", B.c("pv_mm")), prev_dma=B.dboth("pk_dma"))

    def emit_flash(res_src, xdst, sb_hook=None, outt_guard=None):
        entry = B.dboth("pq_dma") + [("ag_kt", B.c("ag_kt")),
                                     ("ag_v", B.c("ag_v"))]
        for e_ in ENGINES:
            B.wait_only(e_, entry)
        n_sm = B.n("f_sm"); n_ac = B.n("f_ac"); n_u = B.n("f_u")
        n_ex = B.n("f_ex"); n_ptr = B.n("f_ptr"); n_pev = B.n("f_pev")
        n_pv = B.n("f_pv"); n_ao = B.n("f_ao"); n_aoe = B.n("f_aoe")
        n_aot = B.n("f_aot"); n_oe = B.n("f_oe"); n_om = B.n("f_om")
        qtbs = [qtb[0], qtb[1], qtb3]
        QB = min(4, nkc)
        nseg = nkc // QB
        hk = nkc // 2
        npc = max(1, hk // 2)
        pwk = hk // npc
        bst = {"rr": 0, "lu": {}, "sb_pacc": -1}

        def scores_dma(sb, kc):
            sbpar = sb % 3
            if kc == 0:
                for hc in range(DC):
                    B.dma2("sync", qtbs[sbpar][:, hc, :],
                           bass.AP(qtD, hc * 128 * T + sb * 256,
                                   [[T, 128], [1, 256]]),
                           "f_qt", sb,
                           waits=([("f_sm",
                                    B.h("f_sm", n_sm + (sb - 2) * nkc * 2))]
                                  if hc == 0 else ()))
            kpar = kc % 2
            gi = sb * nkc + kc
            rank = 0 if kc * 512 < T else 1
            base = rank * D * T + (kc * 512 - rank * T)
            B.dma2("sync", ktb[kpar][:, :, :],
                   bass.AP(ktgD, base, [[T, 128], [128 * T, 8], [1, 512]]),
                   "f_kt", gi,
                   waits=[("f_sm", B.h("f_sm", n_sm + (gi - 1) * 2))])

        def scores_kc(sb, kc, dma_done=False):
            sbpar = sb % 3
            if not dma_done:
                scores_dma(sb, kc)
            kpar = kc % 2
            gi = sb * nkc + kc
            for qt2 in (0, 1):
                ci = gi * 2 + qt2
                use4 = (sb > 0 and kc >= 5)
                if use4:
                    if kc == 5 and qt2 == 0:
                        bst["rr"] = 0
                    bi = bst["rr"] % 4
                    bst["rr"] += 1
                else:
                    bi = ci % 2
                psap = (p0.ap(), p1.ap(), pacc0[:, 0:512],
                        pacc0[:, 512:])[bi]
                tpar = ci % 3
                wmm = [B.dtot("f_kt", gi), B.dtot("f_qt", sb)]
                lu = bst["lu"].get(bi)
                if lu is not None:
                    wmm.append(("f_ac", B.h("f_ac", n_ac + lu + 1)))
                if bi >= 2 and bst["sb_pacc"] != sb:
                    bst["sb_pacc"] = sb
                    wmm.append(("f_oe",
                                B.h("f_oe", n_oe + 2 * (sb - 1) + 2)))
                bst["lu"][bi] = ci
                for hc in range(DC):
                    B.emit("tensor", lambda e, sbpar=sbpar, kpar=kpar, hc=hc,
                           qt2=qt2, psap=psap: e.matmul(
                               psap,
                               qtbs[sbpar][:, hc, qt2 * 128:(qt2 + 1) * 128],
                               ktb[kpar][:, hc, :],
                               start=(hc == 0), stop=(hc == DC - 1)),
                           waits=(wmm if hc == 0 else ()),
                           inc=("f_sm" if hc == DC - 1 else None))
                B.emit("scalar", lambda e, psap=psap, tpar=tpar: e.activation(
                    out=t1s[tpar].ap(), in_=psap, func=AF.Tanh, scale=0.5,
                    bias=sl("zb")[:, 0:1]),
                    waits=[("f_sm", B.c("f_sm")),
                           ("f_u", B.h("f_u", n_u + ci - 2))])
                B.emit("scalar", lambda e, psap=psap, tpar=tpar: e.activation(
                    out=t2s[tpar].ap(), in_=psap, func=AF.Tanh,
                    bias=sl("zb")[:, 0:1]))
                B.emit("scalar", lambda e, psap=psap, tpar=tpar: e.activation(
                    out=rsc[tpar].ap(), in_=psap, func=AF.Relu,
                    bias=sl("zb")[:, 0:1]), inc="f_ac")
                B.emit("vector", lambda e, tpar=tpar: e.scalar_tensor_tensor(
                    t1s[tpar].ap(), t2s[tpar].ap(), 1.5, t1s[tpar].ap(),
                    ALU.mult, ALU.add),
                    waits=[("f_ac", B.c("f_ac"))])
                B.emit("vector", lambda e, qt2=qt2, kc=kc, tpar=tpar:
                       e.tensor_add(
                           Ubuf[qt2][:, kc * 512:(kc + 1) * 512],
                           t1s[tpar].ap(), rsc[tpar].ap()),
                       waits=[("f_ex",
                               B.h("f_ex", n_ex + 4 * npc * (sb - 1)
                                   + (kc // hk) * 2 * npc
                                   + 2 * ((kc % hk) // pwk) + qt2 + 1))],
                       inc="f_u")
                if (kc + 1) % QB == 0:
                    seg = (kc + 1) // QB - 1
                    B.emit("vector", lambda e, qt2=qt2, kc=kc, seg=seg:
                           e.tensor_reduce(
                               sl(f"mp{qt2}", (128, 16))[:, seg:seg + 1],
                               Ubuf[qt2][:, (kc + 1 - QB) * 512:
                                         (kc + 1) * 512],
                               AX.X, ALU.max))

        def exp_part(sb):
            for qt2 in (0, 1):
                B.emit("vector", lambda e, qt2=qt2: e.tensor_reduce(
                    sl(f"mrow{qt2}")[:, 0:1],
                    sl(f"mp{qt2}", (128, 16))[:, 0:nseg], AX.X, ALU.max),
                    waits=[("f_u",
                            B.h("f_u", n_u + 2 * nkc * sb + 2 * nkc - 1 + qt2))])
                B.emit("vector", lambda e, qt2=qt2: e.tensor_scalar(
                    sl(f"negm{qt2}")[:, 0:1], sl(f"mrow{qt2}")[:, 0:1], -0.1,
                    None, ALU.mult), inc="f_mxr")
            first = True
            for half in (0, 1):
                for p in range(npc):
                    for qt2 in (0, 1):
                        c0 = half * (S // 2) + p * pwk * 512
                        c1 = c0 + pwk * 512
                        idx = half * npc + p
                        vlast = half * hk + (p + 1) * pwk - 1
                        w = [("f_ptr", B.h("f_ptr", n_ptr
                                           + 2 * nkc * (sb - 1)
                                           + 2 * vlast + qt2 + 1))]
                        if first:
                            w.append(("f_mxr", B.c("f_mxr")))
                            first = False
                        B.emit("scalar", lambda e, qt2=qt2, c0=c0, c1=c1,
                               idx=idx: e.activation(
                                   out=Pbuf[qt2][:, c0:c1],
                                   in_=Ubuf[qt2][:, c0:c1],
                                   func=AF.Exp, bias=sl(f"negm{qt2}")[:, 0:1],
                                   scale=0.1,
                                   accum_out=lrp[qt2][:, idx:idx + 1]),
                               waits=w, inc="f_ex")

        def pv_part(sb):
            for vc in range(nkc):
                vpar = vc % 2
                B.dma2("sync", vtb[vpar][:, :, :],
                       bass.AP(vgD, vc * 512 * D,
                               [[D, 128], [128 * D, 4], [1, D]]),
                       "f_vt", sb * nkc + vc,
                       waits=[("f_pv",
                               B.h("f_pv", n_pv + 2 * nkc * sb + 2 * vc - 2))])
                for qt2 in (0, 1):
                    wtr = [("f_ex",
                            B.h("f_ex", n_ex + 4 * npc * sb
                                + (vc // hk) * 2 * npc
                                + 2 * ((vc % hk) // pwk) + qt2 + 1)),
                           ("f_pev",
                            B.h("f_pev", n_pev + 2 * nkc * sb + 2 * vc + qt2 - 1))]
                    if vc == 0:
                        wtr.append(("f_aoe", B.h("f_aoe", n_aoe + 2 * (sb - 1)
                                                 + qt2 + 1)))
                    for i4 in range(4):
                        B.emit("tensor", lambda e, qt2=qt2, vc=vc, i4=i4:
                               e.transpose(
                                   pts[qt2][:, i4, :],
                                   Pbuf[qt2][:, (vc * 4 + i4) * 128:
                                             (vc * 4 + i4 + 1) * 128],
                                   ident.ap()),
                               waits=(wtr if i4 == 0 else ()),
                               inc=("f_ptr" if i4 == 3 else None))
                for qt2 in (0, 1):
                    B.emit("vector", lambda e, qt2=qt2: e.tensor_copy(
                        ptsb[qt2][:, 0:4, :], pts[qt2][:, 0:4, :]),
                        waits=[("f_ptr", B.h("f_ptr",
                                             n_ptr + 2 * nkc * sb + 2 * vc
                                             + qt2 + 1)),
                               ("f_pv",
                                B.h("f_pv", n_pv + 2 * nkc * sb + 2 * (vc - 1)
                                    + qt2 + 1))], inc="f_pev")
                for qt2 in (0, 1):
                    acc = paccs[qt2]
                    wpv = [("f_pev", B.h("f_pev",
                                         n_pev + 2 * nkc * sb + 2 * vc
                                         + qt2 + 1)),
                           B.dtot("f_vt", sb * nkc + vc)]
                    if vc == 0:
                        wpv.append(("f_oe", B.h("f_oe", n_oe + 2 * sb + qt2 - 1)))
                    for half in (0, 1):
                        for i4 in range(4):
                            B.emit("tensor", lambda e, qt2=qt2, vc=vc, i4=i4,
                                   vpar=vpar, acc=acc, half=half: e.matmul(
                                       acc[:, half * 512:(half + 1) * 512],
                                       ptsb[qt2][:, i4, :],
                                       vtb[vpar][:, i4,
                                                 half * 512:(half + 1) * 512],
                                       start=(vc == 0 and i4 == 0),
                                       stop=(vc == nkc - 1 and i4 == 3)),
                                   waits=(wpv if (half == 0 and i4 == 0)
                                          else ()),
                                   inc=("f_pv" if (half == 1 and i4 == 3)
                                        else None))

        def epi_e0(sb):
            for qt2 in (0, 1):
                B.emit("vector", lambda e, qt2=qt2: e.tensor_reduce(
                    sl(f"lrow{qt2}")[:, 0:1], lrp[qt2][:, 0:2 * npc],
                    AX.X, ALU.add), waits=[("f_ex", B.c("f_ex"))])
                B.emit("vector", lambda e, qt2=qt2: e.reciprocal(
                    sl(f"invl{qt2}")[:, 0:1], sl(f"lrow{qt2}")[:, 0:1]),
                    inc="f_iv")
            for qt2 in (0, 1):
                acc = paccs[qt2]
                B.emit("vector", lambda e, qt2=qt2, acc=acc: e.tensor_scalar(
                    ao_sb2[qt2].ap(), acc.ap(), sl(f"invl{qt2}")[:, 0:1], None,
                    ALU.mult),
                    waits=[("f_pv", B.h("f_pv", n_pv + 2 * nkc * sb
                                        + 2 * (nkc - 1) + qt2 + 1)),
                           ("f_iv", B.c("f_iv")),
                           ("f_aot", B.h("f_aot", n_aot + 2 * sb + qt2 - 1))],
                    inc="f_ao")
            for qt2 in (0, 1):
                row = sb * 256 + qt2 * 128
                B.dma2("sync", xt[qt2].ap(),
                       bass.AP(res_src, row * D, [[D, 128], [1, D]]), "f_xl",
                       qt2,
                       waits=[("f_oe", B.h("f_oe", n_oe + 2 * sb + qt2 - 1))])

        def epi_aot(sb, qt2):
            for j in range(DC):
                B.emit("tensor", lambda e, qt2=qt2, j=j: e.transpose(
                    pts[qt2][:, j, :], ao_sb2[qt2][:, j * 128:(j + 1) * 128],
                    ident.ap()),
                    waits=([("f_ao", B.h("f_ao", n_ao + 2 * sb + qt2 + 1))]
                           if j == 0 else ()),
                    inc=("f_aot" if j == DC - 1 else None))

        def epi_aoe(sb, qt2):
            B.emit("vector", lambda e, qt2=qt2: e.tensor_copy(
                aoT_sb2[qt2].ap(), pts[qt2].ap()),
                waits=[("f_aot", B.h("f_aot", n_aot + 2 * sb + qt2 + 1)),
                       ("f_om", B.h("f_om", n_om + 2 * (sb - 1) + qt2 + 1))],
                inc="f_aoe")

        def epi_omm(sb, qt2):
            acc = paccs[qt2]
            womm = [("f_aoe", B.h("f_aoe", n_aoe + 2 * sb + qt2 + 1)),
                    ("f_ao", B.h("f_ao", n_ao + 2 * sb + 2))]
            for hc in range(DC):
                B.emit("tensor", lambda e, hc=hc, acc=acc, qt2=qt2: e.matmul(
                    acc[:, 0:512], aoT_sb2[qt2][:, hc, :],
                    wo_sb[:, hc, 0:512],
                    start=(hc == 0), stop=False),
                    waits=(womm if hc == 0 else ()))
                B.emit("tensor", lambda e, hc=hc, acc=acc, qt2=qt2: e.matmul(
                    acc[:, 512:], aoT_sb2[qt2][:, hc, :],
                    wo_sb[:, hc, 512:],
                    start=(hc == 0), stop=False))
            B.emit("tensor", lambda e, acc=acc: e.matmul(
                acc[:, 0:512], ones_b[0:1, :], bo_row[0:1, 0:512],
                start=False, stop=True))
            B.emit("tensor", lambda e, acc=acc: e.matmul(
                acc[:, 512:], ones_b[0:1, :], bo_row[0:1, 512:],
                start=False, stop=True), inc="f_om")

        def epi_out(sb, qt2):
            acc = paccs[qt2]
            row = sb * 256 + qt2 * 128
            wev = [("f_om", B.h("f_om", n_om + 2 * sb + qt2 + 1)),
                   B.dtot("f_xl", qt2), B.dtot("f_xo", qt2)]
            if outt_guard is not None:
                g_ = outt_guard(sb, qt2)
                if g_ is not None:
                    wev.append(g_)
            B.emit("vector", lambda e, qt2=qt2, acc=acc: e.tensor_add(
                outt[qt2][:, 0:512], acc[:, 0:512], xt[qt2][:, 0:512]),
                waits=wev)
            B.emit("vector", lambda e, qt2=qt2, acc=acc: e.tensor_add(
                outt[qt2][:, 512:], acc[:, 512:], xt[qt2][:, 512:]),
                inc="f_oe")
            B.dma2("sync", bass.AP(xdst, row * D, [[D, 128], [1, D]]),
                   outt[qt2].ap(), "f_xo", qt2,
                   waits=[("f_oe", B.c("f_oe"))])

        for kc in range(nkc):
            scores_kc(0, kc)
        for sb in range(nsb):
            exp_part(sb)
            if sb + 1 < nsb:
                scores_dma(sb + 1, 0)
                scores_dma(sb + 1, 1)
            pv_part(sb)
            stages = [lambda sb=sb: epi_e0(sb),
                      lambda sb=sb: epi_aot(sb, 0),
                      lambda sb=sb: (epi_aoe(sb, 0), epi_aot(sb, 1)),
                      lambda sb=sb: (epi_aoe(sb, 1), epi_omm(sb, 0)),
                      lambda sb=sb: (epi_omm(sb, 1), epi_out(sb, 0)),
                      lambda sb=sb: (epi_out(sb, 1),
                                     sb_hook(sb) if sb_hook else None)]
            if sb + 1 < nsb:
                stages[0]()
                done = 1
                for kc in range(nkc):
                    scores_kc(sb + 1, kc, dma_done=(kc < 2))
                    if done < len(stages):
                        stages[done]()
                        done += 1
                while done < len(stages):
                    stages[done]()
                    done += 1
            else:
                for f_ in stages:
                    f_()

    def emit_wreload(entry_waits):
        first = True
        for j in range(DC):
            w = (list(entry_waits) + [("q_st", B.c("q_st"))]) if first else ()
            first = False
            B.dma("sync", wgT[:, j, :],
                  bass.AP(wgS, j * 128 * I, [[I, 128], [1, I]]), "wr_ld", waits=w)
        for j in range(DC):
            B.dma("sync", wuT[:, j, :],
                  bass.AP(wuS, j * 128 * I, [[I, 128], [1, I]]), "wr_ld")
        for ic in range(IC):
            B.dma("sync", wdT[:, ic, :],
                  bass.AP(wdS, ic * 128 * D, [[D, 128], [1, D]]), "wr_ld")

    gu_ev, quant_w_fn = emit_setup()
    emit_ln_dram(xin, "amax1", entry_waits=[("p0_misc", 64)])
    xpart_request(sl("amax1"), ccB[0], [("l_mx", B.c("l_mx"))], [], "c1")
    quant_w_fn(wdD, None, wdT, wdS, DC, True, I)
    xpart_complete(sl("M1"), "c1")
    emit_scales_x(sl("M1"))
    for f_ in deferred_stores:
        f_()
    B.wait_only("tensor", [("q_ev", gu_ev), ("sc", B.c("sc"))])
    emit_ffn_b("hamax1")
    xpart_max_allreduce(sl("hamax1"), ccB[1], sl("M2"),
                        [("b_hm", B.c("b_hm"))] + B.dboth("b_hd"), [], "c2")
    emit_scales_h(sl("M2"))
    b1_h = B.c("b_h")
    ln2 = LNPass(None)
    emit_ffn_c(xin, x1D,
               first_mm_waits=[("b_h", b1_h), ("q_ev", B.c("q_ev"))],
               tile_hook=lambda tt: ln2.tile(
                   tt, outt[tt % 2].ap(), [("c_ev", B.c("c_ev"))]),
               outt_guard=lambda tt: ln2.norm_done(tt - 2) if tt >= 2 else None)
    emit_attn_wload([("c_mm", B.c("c_mm")), ("b_mu", B.c("b_mu")),
                     ("q_st", B.c("q_st"))] + B.dboth("c_xo"))
    emit_proj()
    ln3 = LNPass("amax3")

    def ln3_hook(sb):
        g = [("f_oe", B.c("f_oe"))]
        ln3.stats(2 * sb, outt[0].ap(), g)
        ln3.stats(2 * sb + 1, outt[1].ap(), g)
        ln3.norm(2 * sb, outt[0].ap())
        ln3.norm(2 * sb + 1, outt[1].ap())

    emit_flash(x1D, x2D, sb_hook=ln3_hook,
               outt_guard=lambda sb, qt2: (ln3.norm_done(2 * (sb - 1) + qt2)
                                           if sb >= 1 else None))
    emit_wreload(B.dboth("f_xo"))
    xpart_max_allreduce(sl("amax3"), ccB[2], sl("M1"),
                        [("l_mx", B.c("l_mx"))], [("f_oe", B.c("f_oe"))], "c3")
    emit_scales_x(sl("M1"))
    B.wait_only("tensor", [("wr_ld", B.c("wr_ld")), ("sc", B.c("sc"))])
    emit_ffn_b("hamax3")
    xpart_max_allreduce(sl("hamax3"), ccB[3], sl("M2"),
                        [("b_hm", B.c("b_hm"))] + B.dboth("b_hd"), [], "c4")
    emit_scales_h(sl("M2"))
    b3_h = B.c("b_h")
    emit_ffn_c(x2D, xout, first_mm_waits=[("b_h", b3_h)])
    B.wait_only("sync", B.dboth("c_xo"))
    B.wait_only("gpsimd", B.dboth("c_xo"))

    with nc.Block() as block:
        @block.sync
        def _(eng):
            for f in B.q["sync"]:
                f(eng)

        @block.scalar
        def _(eng):
            for f in B.q["scalar"]:
                f(eng)

        @block.vector
        def _(eng):
            for f in B.q["vector"]:
                f(eng)

        @block.tensor
        def _(eng):
            for f in B.q["tensor"]:
                f(eng)

        @block.gpsimd
        def _(eng):
            for f in B.q["gpsimd"]:
                f(eng)

    return nc


def make_in_maps(x_full, Wq, bq, Wk, bk, Wv, bv, Wo, bo, clip_x, clip_down,
                 w_gate, w_up, w_down, T=4096):
    f32 = np.float32
    ident = np.eye(128, dtype=ml_dtypes.bfloat16)
    com = {
        "wqT": np.ascontiguousarray(np.asarray(Wq, f32).T),
        "wkT": np.ascontiguousarray(np.asarray(Wk, f32).T),
        "wvT": np.ascontiguousarray(np.asarray(Wv, f32).T),
        "woT": np.ascontiguousarray(np.asarray(Wo, f32).T),
        "bqp": np.ascontiguousarray(np.asarray(bq, f32).reshape(8, 128).T),
        "bkp": np.ascontiguousarray(np.asarray(bk, f32).reshape(8, 128).T),
        "bvr": np.asarray(bv, f32).reshape(1, D),
        "bor": np.asarray(bo, f32).reshape(1, D),
        "wg": np.asarray(w_gate, f32),
        "wu": np.asarray(w_up, f32),
        "wd": np.asarray(w_down, f32),
        "clip": np.array([[np.asarray(clip_x).reshape(-1)[0],
                           np.asarray(clip_down).reshape(-1)[0]]], f32),
        "ident": ident,
    }
    in_maps = []
    for c in range(8):
        b, hh = c // 2, c % 2
        shard = np.ascontiguousarray(x_full[b, hh * T:(hh + 1) * T, :], dtype=f32)
        m = dict(com)
        m["xin"] = shard
        in_maps.append(m)
    return in_maps


def kernel(L0_attn_x, L0_attn_Wq, L0_attn_bq, L0_attn_Wk, L0_attn_bk,
           L0_attn_Wv, L0_attn_bv, L0_attn_Wo, L0_attn_bo,
           L0_ffn_clip_x, L0_ffn_clip_down, L0_ffn_w_gate, L0_ffn_w_up,
           L0_ffn_w_down):
    T = 4096
    x = np.asarray(L0_attn_x, np.float32)
    in_maps = make_in_maps(x, L0_attn_Wq, L0_attn_bq, L0_attn_Wk, L0_attn_bk,
                           L0_attn_Wv, L0_attn_bv, L0_attn_Wo, L0_attn_bo,
                           L0_ffn_clip_x, L0_ffn_clip_down, L0_ffn_w_gate,
                           L0_ffn_w_up, L0_ffn_w_down, T=T)
    nc = build_program(T)
    res = run_bass_kernel_spmd(nc, in_maps, core_ids=list(range(8)))
    out = np.empty((4, 2 * T, D), np.float32)
    for c in range(8):
        b, hh = c // 2, c % 2
        out[b, hh * T:(hh + 1) * T, :] = res.results[c]["xout"]
    return out



# version 29
# speedup vs baseline: 1.1436x; 1.0016x over previous
import numpy as np
import ml_dtypes
import concourse.bass as bass
import concourse.mybir as mybir
import concourse.bass_utils as _bass_utils
from concourse.bass_utils import run_bass_kernel_spmd

_orig_run_command = _bass_utils.run_command


def _run_command_ldwopt(argv, **kwargs):
    argv = ["--enable-ldw-opt=true" if a == "--enable-ldw-opt=false" else a
            for a in argv]
    return _orig_run_command(argv, **kwargs)


_bass_utils.run_command = _run_command_ldwopt

F32 = mybir.dt.float32
BF16 = mybir.dt.bfloat16
AF = mybir.ActivationFunctionType
ALU = mybir.AluOpType
AX = mybir.AxisListType

RND = 12582912.0
D = 1024
DC = 8
I = 2816
IC = 22
EPS = 1e-5

ENGINES = ("sync", "scalar", "vector", "tensor", "gpsimd")

OFF_IDENT = 2048
OFF_BQ = 2304
OFF_BK = 2336
OFF_BVROW = 2368
OFF_BOROW = 4416
OFF_ONESB = 6464
OFF_ONESF = 6720
OFF_CLIP = 7232
OFF_LNST = 7264
OFF_ROW128 = 7328
OFF_RSG = 7840
OFF_RSU = 7936
OFF_CSG = 8032
OFF_CSU = 8128
OFF_SLOTS = 8224
OFF_XT = 10752
OFF_OUTT = 19456
OFF_XNBF = 27648
OFF_TEVAC = 31744
OFF_WO = 35840
OFF_W = 52224
FO_WG = 0
FO_WU = 45056
FO_WD = 90112
FO_XQ = 135168
FO_HEV = 151552
FO_SILU = 153600
FO_HQ = 157696
FO_HQF = 168960
FO_P0STG = 151552
FO_P0WI = 162816
AO_WQ = 0
AO_WK = 16384
AO_WV = 32768
AO_XN2 = 49152
AO_KEV = 65536
AO_VEV = 67584
AO_WST = 71680
FL_QTB = 0
FL_KTB = 8192
FL_VTB = 24576
FL_U = 40960
FL_P = 106496
FL_T1 = 139264
FL_T2 = 143360
FL_R = 147456
FL_PT = 151552
FL_AO = 155648
FL_AOT = 157696


class Builder:
    def __init__(self, nc):
        self.nc = nc
        self.q = {e: [] for e in ENGINES}
        self.sems = {}
        self.cnt = {}
        self.hist = {}
        self._slots = {}
        self._slot_next = OFF_SLOTS

    def sem(self, name):
        if name not in self.sems:
            self.sems[name] = self.nc.alloc_semaphore(name)
            self.cnt[name] = 0
            self.hist[name] = []
        return self.sems[name]

    def c(self, name):
        self.sem(name)
        return self.cnt[name]

    def n(self, name):
        self.sem(name)
        return len(self.hist[name])

    def h(self, name, k):
        self.sem(name)
        if k <= 0:
            return 0
        hist = self.hist[name]
        assert k <= len(hist), (name, k, len(hist))
        return hist[k - 1]

    SERIAL = {"vector": "ser_v", "scalar": "ser_s", "gpsimd": "ser_g"}

    def emit(self, eng, fn, waits=(), inc=None, amt=1):
        waits = tuple(w for w in waits if w is not None)
        for s, _ in waits:
            self.sem(s)
        if inc is not None and amt > 0:
            self.sem(inc)
            self.cnt[inc] += amt
            self.hist[inc].append(self.cnt[inc])
        else:
            inc = None
        ser = self.SERIAL.get(eng)
        prev_tok = None
        my_inc = inc
        if ser is not None:
            prev_tok = getattr(self, "_tok", {}).get(eng)
            if not hasattr(self, "_tok"):
                self._tok = {}
            if inc is not None:
                self._tok[eng] = (inc, self.cnt[inc])
            else:
                self.sem(ser)
                self.cnt[ser] += 1
                self.hist[ser].append(self.cnt[ser])
                my_inc = ser
                amt = 1
                self._tok[eng] = (ser, self.cnt[ser])
        sems = self.sems

        def run(engine, fn=fn, waits=waits, inc=my_inc, amt=amt,
                prev_tok=prev_tok):
            if prev_tok is not None and prev_tok[1] > 0:
                engine.wait_ge(sems[prev_tok[0]], prev_tok[1])
            for s, v in waits:
                if v > 0:
                    engine.wait_ge(sems[s], v)
            ins = fn(engine)
            if inc is not None:
                ins.then_inc(sems[inc], amt)

        self.q[eng].append(run)

    def wait_only(self, eng, waits):
        waits = tuple(w for w in waits if w is not None and w[1] > 0)
        if not waits:
            return
        for s, _ in waits:
            self.sem(s)
        sems = self.sems

        def run(engine, waits=waits):
            for s, v in waits:
                engine.wait_ge(sems[s], v)

        self.q[eng].append(run)

    def dma(self, eng, out, in_, inc, waits=(), **kw):
        self.emit(eng, lambda e, o=out, i=in_, kw=kw: e.dma_start(out=o, in_=i, **kw),
                  waits=waits, inc=inc, amt=16)

    def dma2(self, eng, out, in_, fam, idx, waits=(), **kw):
        self.dma(eng, out, in_, f"{fam}{idx % 2}", waits=waits, **kw)

    def dtot(self, fam, idx):
        name = f"{fam}{idx % 2}"
        return (name, self.c(name))

    def dboth(self, fam):
        return [(f"{fam}0", self.c(f"{fam}0")), (f"{fam}1", self.c(f"{fam}1"))]

    def slot(self, name, shape=(128, 1)):
        if name not in self._slots:
            nbytes = int(np.prod(shape[1:])) * 4
            nbytes = (nbytes + 31) // 32 * 32
            off = self._slot_next
            assert off + nbytes <= OFF_XT, (name, off)
            self._slot_next = off + nbytes
            self._slots[name] = self.nc.alloc_sbuf_tensor_at(
                "slot_" + name, list(shape), F32, offset=off)
        return self._slots[name]


def flat(handle, n, off=0):
    return bass.AP(handle, off, [[1, 1], [1, n]])


def build_program(T=4096):
    nc = bass.Bass("TRN2", target_bir_lowering=False, num_devices=8,
                   dynamic_dma_scratch_size=2048)
    nt = T // 128
    ng = T // 512
    S = 2 * T
    nkc = S // 512
    nsb = T // 256

    xin = nc.dram_tensor("xin", [T, D], F32, kind="ExternalInput")
    wqTD = nc.dram_tensor("wqT", [D, D], F32, kind="ExternalInput")
    wkTD = nc.dram_tensor("wkT", [D, D], F32, kind="ExternalInput")
    wvTD = nc.dram_tensor("wvT", [D, D], F32, kind="ExternalInput")
    woTD = nc.dram_tensor("woT", [D, D], F32, kind="ExternalInput")
    bqpD = nc.dram_tensor("bqp", [128, 8], F32, kind="ExternalInput")
    bkpD = nc.dram_tensor("bkp", [128, 8], F32, kind="ExternalInput")
    bvrD = nc.dram_tensor("bvr", [1, D], F32, kind="ExternalInput")
    borD = nc.dram_tensor("bor", [1, D], F32, kind="ExternalInput")
    wgD = nc.dram_tensor("wg", [I, D], F32, kind="ExternalInput")
    wuD = nc.dram_tensor("wu", [I, D], F32, kind="ExternalInput")
    wdD = nc.dram_tensor("wd", [D, I], F32, kind="ExternalInput")
    clipD = nc.dram_tensor("clip", [1, 2], F32, kind="ExternalInput")
    identD = nc.dram_tensor("ident", [128, 128], BF16, kind="ExternalInput")
    xout = nc.dram_tensor("xout", [T, D], F32, kind="ExternalOutput")

    xnTD = nc.dram_tensor("xnT", [D, T], BF16)
    hDD = nc.dram_tensor("hd", [I, T], BF16)
    x1D = nc.dram_tensor("x1", [T, D], F32)
    x2D = nc.dram_tensor("x2", [T, D], F32)
    qtD = nc.dram_tensor("qt", [D, T], BF16)
    ktoD = nc.dram_tensor("kto", [D, T], BF16)
    ktgD = nc.dram_tensor("ktg", [2 * D, T], BF16)
    voD = nc.dram_tensor("vo", [T, D], BF16)
    vgD = nc.dram_tensor("vg", [2 * T, D], BF16)
    wgS = nc.dram_tensor("wgs", [D, I], BF16)
    wuS = nc.dram_tensor("wus", [D, I], BF16)
    wdS = nc.dram_tensor("wds", [I, D], BF16)
    scrD = nc.dram_tensor("scr", [1, 128], F32)
    ccB = [nc.dram_tensor(f"cc{j}", [1, 8], F32) for j in range(4)]

    A = nc.alloc_sbuf_tensor_at
    ident = A("ident", [128, 128], BF16, offset=OFF_IDENT)
    identF = A("identf", [128, 128], F32, offset=18944)
    bq_pp = A("bq_pp", [128, 8], F32, offset=OFF_BQ)
    bk_pp = A("bk_pp", [128, 8], F32, offset=OFF_BK)
    bv_row = A("bv_row", [1, D], BF16, offset=OFF_BVROW)
    bo_row = A("bo_row", [1, D], BF16, offset=OFF_BOROW)
    ones_b = A("ones_b", [1, 128], BF16, offset=OFF_ONESB)
    ones_f = A("ones_f", [1, 128], F32, offset=OFF_ONESF)
    clip_sb = A("clip_sb", [1, 2], F32, offset=OFF_CLIP)
    lnst = A("lnst", [128, 12], F32, offset=OFF_LNST)
    row128 = A("row128", [1, 128], F32, offset=OFF_ROW128)
    rs_g = A("rs_g", [128, IC], F32, offset=OFF_RSG)
    rs_u = A("rs_u", [128, IC], F32, offset=OFF_RSU)
    csg = A("csg", [128, IC], F32, offset=OFF_CSG)
    csu = A("csu", [128, IC], F32, offset=OFF_CSU)
    xt = [A(f"xt{k}", [128, D], F32, offset=OFF_XT + k * 4096) for k in range(2)]
    outt = [A(f"outt{k}", [128, D], F32, offset=OFF_OUTT + k * 4096) for k in range(2)]
    xnbf = [A(f"xnbf{k}", [128, D], BF16, offset=OFF_XNBF + k * 2048) for k in range(2)]
    tevac = [A(f"tevac{k}", [128, 8, 128], BF16, offset=OFF_TEVAC + k * 2048)
             for k in range(2)]
    wo_sb = A("wo_sb", [128, 8, D], BF16, offset=OFF_WO)
    wgT = A("wgT", [128, 8, I], BF16, offset=OFF_W + FO_WG)
    wuT = A("wuT", [128, 8, I], BF16, offset=OFF_W + FO_WU)
    wdT = A("wdT", [128, IC, D], BF16, offset=OFF_W + FO_WD)
    xq = [A(f"xq{k}", [128, 8, 512], BF16, offset=OFF_W + FO_XQ + k * 8192)
          for k in range(2)]
    hev = [A(f"hev{k}", [128, 512], BF16, offset=OFF_W + FO_HEV + k * 1024)
           for k in range(2)]
    silu = [A(f"silu{k}", [128, 512], F32, offset=OFF_W + FO_SILU + k * 2048)
            for k in range(2)]
    hq = [A(f"hq{k}", [128, IC, 128], BF16, offset=OFF_W + FO_HQ + k * 5632)
          for k in range(2)]
    hqfs = [A("hqf0", [128, 1408], F32, offset=OFF_W + FO_HQF),
            A("hqf1", [128, 1408], F32, offset=OFF_W + FO_XQ)]
    qscs = [A("qsc0", [128, 512], F32, offset=OFF_W + FO_HQF),
            A("qsc1", [128, 512], F32, offset=OFF_W + FO_HQF + 2048)]
    p0stg = A("p0stg", [128, I], F32, offset=OFF_W + FO_P0STG)
    awq = A("awq", [128, 8, D], BF16, offset=OFF_W + AO_WQ)
    awk = A("awk", [128, 8, D], BF16, offset=OFF_W + AO_WK)
    awv = A("awv", [128, 8, D], BF16, offset=OFF_W + AO_WV)
    xn2 = [A(f"xn2{k}", [128, 8, 512], BF16, offset=OFF_W + AO_XN2 + k * 8192)
           for k in range(2)]
    kev = [A(f"kev{k}", [128, 512], BF16, offset=OFF_W + AO_KEV + k * 1024)
           for k in range(2)]
    vev = [A(f"vev{k}", [128, D], BF16, offset=OFF_W + AO_VEV + k * 2048)
           for k in range(2)]
    wst = [A(f"wst{k}", [128, D], F32, offset=OFF_W + AO_WST + k * 4096)
           for k in range(2)]
    qtb = [A(f"qtb{k}", [128, 8, 256], BF16, offset=OFF_W + FL_QTB + k * 4096)
           for k in range(2)]
    qtb3 = A("qtb2x", [128, 8, 256], BF16, offset=OFF_W + FL_AOT + 2048)
    ktb = [A(f"ktb{k}", [128, 8, 512], BF16, offset=OFF_W + FL_KTB + k * 8192)
           for k in range(2)]
    vtb = [A(f"vtb{k}", [128, 4, D], BF16, offset=OFF_W + FL_VTB + k * 8192)
           for k in range(2)]
    Ubuf = [A(f"ubuf{k}", [128, S], F32, offset=OFF_W + FL_U + k * 32768)
            for k in range(2)]
    Pbuf = [A(f"pbuf{k}", [128, S], BF16, offset=OFF_W + FL_P + k * 16384)
            for k in range(2)]
    t1s = [A(f"t1s{k}", [128, 512], F32,
             offset=(OFF_W + FL_T1 + k * 2048 if k < 2 else OFF_W + 167936))
           for k in range(3)]
    t2s = [A(f"t2s{k}", [128, 512], F32,
             offset=(OFF_W + FL_T2 + k * 2048 if k < 2 else OFF_W + 169984))
           for k in range(3)]
    rsc = [A(f"rsc{k}", [128, 512], F32,
             offset=(OFF_W + FL_R + k * 2048 if k < 2 else OFF_W + 172032))
           for k in range(3)]
    ptsb = [A(f"ptsb{k}", [128, 8, 128], BF16, offset=OFF_W + FL_PT + k * 2048)
            for k in range(2)]
    ao_sb2 = [A(f"ao_sb{k}", [128, D], BF16,
                offset=OFF_W + FL_AOT + 6144 + k * 2048) for k in range(2)]
    aoT_sb2 = [A(f"aoT_sb{k}", [128, 8, 128], BF16,
                 offset=OFF_W + FL_AO + k * 2048) for k in range(2)]
    lrp = [A(f"lrp{k}", [128, 8], F32, offset=OFF_W + 174080 + k * 32)
           for k in range(2)]

    p0 = nc.alloc_psum_tensor("p0", [128, 512], F32)
    p1 = nc.alloc_psum_tensor("p1", [128, 512], F32)
    pacc0 = nc.alloc_psum_tensor("pacc0", [128, D], F32)
    pacc1 = nc.alloc_psum_tensor("pacc1", [128, D], F32)
    pt0 = nc.alloc_psum_tensor("pt0", [128, 8, 128], BF16)
    pt1 = nc.alloc_psum_tensor("pt1", [128, 8, 128], BF16)
    pts = [pt0, pt1]
    paccs = [pacc0, pacc1]

    B = Builder(nc)
    sl = B.slot

    def xpart_request(acc_slot, ccb, pre_waits, mm_extra, tag):
        B.emit("tensor", lambda e, acc_slot=acc_slot: e.matmul(
            p0[0:1, 0:128], acc_slot[:, 0:1], identF.ap(),
            start=True, stop=True),
            waits=list(pre_waits) + list(mm_extra) + [("p0_ones", 6)],
            inc="cc_bc")
        B.emit("vector", lambda e: e.tensor_reduce(
            sl("ccrow", (1, 8))[0:1, 0:1], p0[0:1, 0:128], AX.X, ALU.max),
            waits=[("cc_bc", B.c("cc_bc"))], inc="cc_rm")
        B.dma("sync", ccb.ap()[0:1, :], sl("ccrow", (1, 8))[0:1, :], "cc_d",
              waits=[("cc_rm", B.c("cc_rm"))])
        B.emit("gpsimd", lambda e, ccb=ccb: e.collective_compute(
            "AllReduce", ALU.max, replica_groups=[[0, 1, 2, 3, 4, 5, 6, 7]],
            ins=[flat(ccb, 8)], outs=[flat(ccb, 8)]),
            waits=[("cc_d", B.c("cc_d"))], inc="cc_cc")
        B.dma("sync", sl(tag + "_cr")[0:1, 0:1], ccb.ap()[0:1, 0:1], "cc_d",
              waits=[("cc_cc", B.c("cc_cc"))])

    def xpart_complete(out_slot, tag):
        B.emit("tensor", lambda e, tag=tag: e.matmul(
            p0[:, 0:1], ones_f[0:1, :], sl(tag + "_cr")[0:1, 0:1],
            start=True, stop=True),
            waits=[("cc_d", B.c("cc_d"))], inc="cc_bc")
        B.emit("vector", lambda e, out_slot=out_slot: e.tensor_copy(
            out_slot[:, 0:1], p0[:, 0:1]),
            waits=[("cc_bc", B.c("cc_bc"))], inc="cc_done")
        return "cc_done"

    def xpart_max_allreduce(acc_slot, ccb, out_slot, pre_waits, mm_extra, tag):
        xpart_request(acc_slot, ccb, pre_waits, mm_extra, tag)
        return xpart_complete(out_slot, tag)

    deferred_stores = []

    def emit_setup():
        B.dma("sync", ident.ap(), identD.ap(), "p0_misc")
        B.dma("sync", bq_pp.ap(), bqpD.ap(), "p0_misc")
        B.dma("sync", bk_pp.ap(), bkpD.ap(), "p0_misc")
        B.dma("sync", clip_sb.ap(), clipD.ap(), "p0_misc")
        B.emit("vector", lambda e: e.memset(ones_b.ap(), 1.0), inc="p0_ones")
        B.emit("vector", lambda e: e.memset(ones_f.ap(), 1.0), inc="p0_ones")
        B.emit("vector", lambda e: e.memset(sl("epsb")[:, 0:1], EPS), inc="p0_ones")
        B.emit("vector", lambda e: e.memset(sl("ccrow", (1, 8))[0:1, :], 0.0),
               inc="p0_ones")
        B.emit("vector", lambda e: e.memset(sl("zb")[:, 0:1], 0.0),
               inc="p0_ones")
        B.dma("sync", wst[0][0:1, :], bvrD.ap(), "p0_bv")
        B.emit("vector", lambda e: e.tensor_copy(bv_row.ap(), wst[0][0:1, :]),
               waits=[("p0_bv", 16)], inc="p0_bvc")
        B.dma("sync", wst[1][0:1, :], borD.ap(), "p0_bv",
              waits=[("p0_bvc", 1)])
        B.emit("vector", lambda e: e.tensor_copy(bo_row.ap(), wst[1][0:1, :]),
               waits=[("p0_bv", 32)], inc="p0_bvc")
        B.emit("scalar", lambda e: e.activation(
            out=identF.ap(), in_=ident.ap(), func=AF.Identity,
            scale=1.0, bias=sl("zb")[:, 0:1]),
            waits=[("p0_misc", 64), ("p0_ones", 5)], inc="p0_ones")
        B.emit("scalar", lambda e: e.activation(
            out=sl("sgtmp", (1, 2))[0:1, :], in_=clip_sb[0:1, :], func=AF.Tanh,
            scale=0.5, bias=sl("zb")[0:1, 0:1]),
            waits=[("p0_misc", 64), ("p0_ones", 5)], inc="p0_sg")
        B.emit("vector", lambda e: e.tensor_scalar(
            sl("sgrow", (1, 2))[0:1, :], sl("sgtmp", (1, 2))[0:1, :],
            0.5, 0.5, ALU.mult, ALU.add), waits=[("p0_sg", 1)], inc="p0_sg")
        B.emit("tensor", lambda e: e.matmul(
            p1[:, 0:2], ones_f[0:1, :], sl("sgrow", (1, 2))[0:1, :],
            start=True, stop=True),
            waits=[("p0_sg", 2), ("p0_ones", 2)], inc="p0_sg")
        B.emit("vector", lambda e: e.tensor_copy(
            sl("sigc", (128, 2))[:, :], p1[:, 0:2]),
            waits=[("p0_sg", 3)], inc="p0_sg")
        for j in range(DC):
            par = j % 2
            B.dma2("sync", wst[par].ap(),
                   bass.AP(woTD, j * 128 * D, [[D, 128], [1, D]]), "aw_ld", j,
                   waits=[("aw_c", B.h("aw_c", B.n("aw_c") - 1)),
                          ("p0_bvc", 2)])
            B.emit("vector", lambda e, j=j, par=par: e.tensor_copy(
                wo_sb[:, j, :], wst[par].ap()),
                waits=[B.dtot("aw_ld", j)], inc="aw_c")

        def quant_w(src, rs_tile, dstT, stash, nrt_, fold_back, ncol):
            nchk = ncol // 128
            gpt = (nchk + 7) // 8
            nbuf = 2 if ncol == D else 1
            raws = [A(f"qraw{k}_" + stash.name, [128, ncol], F32,
                      offset=OFF_W + FO_XQ + k * 4096) for k in range(nbuf)]
            stgs = [A(f"qstg{k}_" + stash.name, [128, ncol], F32,
                      offset=OFF_W + FO_P0STG + k * 4096) for k in range(nbuf)]
            wints = [A(f"qwint{k}_" + stash.name, [128, ncol], BF16,
                       offset=OFF_W + FO_P0WI + k * 2048) for k in range(nbuf)]
            n_sc = B.n("q_sc")
            n_cl = B.n("q_cl")
            n_tr = B.n("q_tr")
            n_ev = B.n("q_ev")

            def stage_a(t):
                raw = raws[t % nbuf]
                stg = stgs[t % nbuf]
                p = t % 2
                B.dma2("sync", raw.ap(),
                       bass.AP(src, t * 128 * ncol, [[ncol, 128], [1, ncol]]),
                       "q_ld", t % nbuf,
                       waits=[("q_sc", B.h("q_sc", n_sc + t + 1 - nbuf))])
                B.emit("vector", lambda e, raw=raw, p=p: e.tensor_reduce(
                    sl(f"q_rmax{p}")[:, 0:1], raw.ap(), AX.X, ALU.max,
                    apply_absolute_value=True),
                    waits=[B.dtot("q_ld", t % nbuf)], inc="q_rm")
                B.emit("vector", lambda e, p=p: e.reciprocal(
                    sl(f"q_rinv{p}")[:, 0:1], sl(f"q_rmax{p}")[:, 0:1]),
                    waits=[("q_rm", B.c("q_rm"))], inc="q_rm")
                B.emit("vector", lambda e, p=p: e.tensor_scalar(
                    sl(f"q_r127{p}")[:, 0:1], sl(f"q_rinv{p}")[:, 0:1], 127.0,
                    None, ALU.mult), waits=[("q_rm", B.c("q_rm"))], inc="q_rm")
                if rs_tile is not None:
                    B.emit("vector", lambda e, t=t, rs_tile=rs_tile, p=p:
                           e.tensor_scalar(
                               rs_tile[:, t:t + 1], sl(f"q_rmax{p}")[:, 0:1],
                               1.0 / 127.0, None, ALU.mult), inc="q_rm")
                B.emit("scalar", lambda e, raw=raw, stg=stg, ncol=ncol, p=p:
                       e.activation(
                           out=stg[:, 0:ncol], in_=raw.ap(), func=AF.Identity,
                           scale=sl(f"q_r127{p}")[:, 0:1],
                           bias=sl("zb")[:, 0:1]),
                       waits=[("q_rm", B.c("q_rm")),
                              ("q_cl", B.h("q_cl", n_cl + t + 1 - nbuf))],
                       inc="q_sc")

            def stage_b(t):
                stg = stgs[t % nbuf]
                wint = wints[t % nbuf]
                p = t % 2
                wint_wait = ("q_tr", B.h("q_tr", n_tr + (t + 1 - nbuf) * gpt))
                if fold_back:
                    B.emit("vector", lambda e, stg=stg, ncol=ncol: e.tensor_scalar(
                        stg[:, 0:ncol], stg[:, 0:ncol], RND, RND, ALU.add,
                        ALU.subtract),
                        waits=[("q_sc", B.h("q_sc", n_sc + t + 1))], inc="q_rd")
                    B.emit("vector", lambda e, wint=wint, stg=stg, ncol=ncol, p=p:
                           e.tensor_scalar(
                               wint.ap(), stg[:, 0:ncol],
                               sl(f"q_rmax{p}")[:, 0:1],
                               1.0 / 127.0, ALU.mult, ALU.mult),
                           waits=[("q_rd", B.c("q_rd")), wint_wait], inc="q_cl")
                else:
                    B.emit("vector", lambda e, wint=wint, stg=stg, ncol=ncol:
                           e.tensor_scalar(
                               wint.ap(), stg[:, 0:ncol], RND, RND, ALU.add,
                               ALU.subtract),
                           waits=[("q_sc", B.h("q_sc", n_sc + t + 1)),
                                  wint_wait], inc="q_cl")
                for gi in range(gpt):
                    j0 = gi * 8
                    jn = min(8, nchk - j0)
                    gg = t * gpt + gi
                    par = gg % 2
                    for jj in range(jn):
                        B.emit("tensor",
                               lambda e, par=par, jj=jj, j0=j0, wint=wint:
                               e.transpose(
                                   pts[par][:, jj, :],
                                   wint[:, (j0 + jj) * 128:(j0 + jj + 1) * 128],
                                   ident.ap()),
                               waits=([("q_cl", B.c("q_cl")),
                                       ("q_ev", B.h("q_ev", n_ev + gg - 1))]
                                      if jj == 0 else ()),
                               inc=("q_tr" if jj == jn - 1 else None))
                    B.emit("scalar",
                           lambda e, par=par, j0=j0, jn=jn, t=t, dstT=dstT:
                           e.copy(
                               dstT[:, j0:j0 + jn, t * 128:(t + 1) * 128],
                               pts[par][:, 0:jn, :]),
                           waits=[("q_tr", B.c("q_tr"))], inc="q_ev")

            if nbuf == 2:
                stage_a(0)
                for t in range(nrt_):
                    if t + 1 < nrt_:
                        stage_a(t + 1)
                    stage_b(t)
            else:
                for t in range(nrt_):
                    stage_a(t)
                    stage_b(t)

            def stores(nchk=nchk, stash=stash, dstT=dstT, nrt_=nrt_,
                       ev=B.c("q_ev")):
                for j in range(nchk):
                    B.dma("sync", bass.AP(stash, j * 128 * nrt_ * 128,
                                          [[nrt_ * 128, 128],
                                           [1, nrt_ * 128]]),
                          dstT[:, j, :], "q_st", waits=[("q_ev", ev)])
            deferred_stores.append(stores)

        quant_w(wgD, rs_g, wgT, wgS, IC, False, D)
        quant_w(wuD, rs_u, wuT, wuS, IC, False, D)
        return B.c("q_ev"), quant_w

    class LNPass:

        def __init__(self, track_slot, act_norm=False):
            self.n_rc = B.n("l_rc"); self.n_nm = B.n("l_nm")
            self.n_tr = B.n("l_tr"); self.n_ev = B.n("l_ev")
            self.n_sq = B.n("l_sq")
            self.track = track_slot
            self.act_norm = act_norm
            if track_slot:
                B.emit("vector", lambda e, t=track_slot: e.memset(
                    sl(t)[:, 0:1], 0.0), inc="l_mx")

        def stats(self, i, src_ap, gates):
            p = i % 2
            B.emit("vector", lambda e, src_ap=src_ap, p=p: e.bn_stats(
                sl(f"lnst{p}", (128, 12))[:, 0:6], src_ap[:, 0:512]),
                waits=list(gates) + [("l_sq", B.h("l_sq", self.n_sq + i - 1))])
            B.emit("vector", lambda e, src_ap=src_ap, p=p: e.bn_stats(
                sl(f"lnst{p}", (128, 12))[:, 6:12], src_ap[:, 512:]))
            B.emit("vector", lambda e, p=p: e.bn_aggr(
                sl(f"lnmv{p}", (128, 2))[:, :], sl(f"lnst{p}", (128, 12)).ap()),
                inc="l_st")
            B.emit("scalar", lambda e, p=p: e.activation(
                out=sl(f"rstd{p}")[:, 0:1], in_=sl(f"lnmv{p}", (128, 2))[:, 1:2],
                func=AF.Sqrt, bias=sl("epsb")[:, 0:1]),
                waits=[("l_st", B.c("l_st"))], inc="l_sq")

        def norm(self, i, src_ap):
            par = i % 2
            p = i % 2
            B.emit("vector", lambda e, p=p: e.reciprocal(
                sl(f"rstd2{p}")[:, 0:1], sl(f"rstd{p}")[:, 0:1]),
                waits=[("l_sq", B.h("l_sq", self.n_sq + i + 1))], inc="l_rc")
            if self.act_norm:
                B.emit("vector", lambda e, p=p: e.tensor_scalar(
                    sl(f"lnnb{p}")[:, 0:1], sl(f"lnmv{p}", (128, 2))[:, 0:1],
                    sl(f"rstd2{p}")[:, 0:1], -1.0, ALU.mult, ALU.mult),
                    waits=[("l_rc", B.c("l_rc"))], inc="l_rc")
                B.emit("scalar", lambda e, par=par, src_ap=src_ap, p=p:
                       e.activation(
                           out=xnbf[par].ap(), in_=src_ap, func=AF.Identity,
                           scale=sl(f"rstd2{p}")[:, 0:1],
                           bias=sl(f"lnnb{p}")[:, 0:1]),
                       waits=[("l_rc", B.c("l_rc")),
                              ("l_tr", B.h("l_tr", self.n_tr + i - 1))],
                       inc="l_nm")
            else:
                B.emit("vector", lambda e, par=par, src_ap=src_ap, p=p:
                       e.tensor_scalar(
                           xnbf[par].ap(), src_ap,
                           sl(f"lnmv{p}", (128, 2))[:, 0:1],
                           sl(f"rstd2{p}")[:, 0:1], ALU.subtract, ALU.mult),
                       waits=[("l_rc", B.c("l_rc")),
                              ("l_tr", B.h("l_tr", self.n_tr + i - 1))],
                       inc="l_nm")
            if self.track:
                B.emit("vector", lambda e, par=par: e.tensor_reduce(
                    sl("tmax")[:, 0:1], xnbf[par].ap(), AX.X, ALU.max,
                    apply_absolute_value=True),
                    waits=([("l_nm", B.c("l_nm"))] if self.act_norm else ()))
                B.emit("vector", lambda e, t=self.track: e.tensor_max(
                    sl(t)[:, 0:1], sl(t)[:, 0:1], sl("tmax")[:, 0:1]),
                    inc="l_mx")
            for j in range(DC):
                B.emit("tensor", lambda e, par=par, j=j: e.transpose(
                    pts[par][:, j, :], xnbf[par][:, j * 128:(j + 1) * 128],
                    ident.ap()),
                    waits=([("l_nm", B.c("l_nm")),
                            ("l_ev", B.h("l_ev", self.n_ev + i - 1))]
                           if j == 0 else ()),
                    inc=("l_tr" if j == DC - 1 else None))
            B.emit("scalar", lambda e, par=par: e.copy(
                tevac[par].ap(), pts[par].ap()),
                waits=[("l_tr", B.c("l_tr")), B.dtot("l_xn", i)],
                inc="l_ev")
            B.dma2("sync",
                   bass.AP(xnTD, i * 128, [[T, 128], [128 * T, 8], [1, 128]]),
                   tevac[par].ap(), "l_xn", i, waits=[("l_ev", B.c("l_ev"))])

        def tile(self, i, src_ap, gates):
            self.stats(i, src_ap, gates)
            self.norm(i, src_ap)

        def norm_done(self, i):
            return ("l_nm", B.h("l_nm", self.n_nm + i + 1))

    def emit_ln_dram(src, track_slot, entry_waits=()):
        ln = LNPass(track_slot, act_norm=True)
        n_nm0 = ln.n_nm

        def load(i, w0=()):
            par = i % 2
            w = [("l_nm", B.h("l_nm", n_nm0 + i - 1))] + list(w0)
            B.dma2("sync", xt[par].ap(),
                   bass.AP(src, i * 128 * D, [[D, 128], [1, D]]), "l_ld", i,
                   waits=w)
            ln.stats(i, xt[par].ap(), [B.dtot("l_ld", i)])

        load(0, entry_waits)
        for i in range(nt):
            if i + 1 < nt:
                load(i + 1)
            ln.norm(i, xt[i % 2].ap())

    def emit_scales_x(Mslot):
        B.emit("vector", lambda e, M=Mslot: e.tensor_mul(
            sl("w_sx")[:, 0:1], M[:, 0:1], sl("sigc", (128, 2))[:, 0:1]), inc="sc")
        B.emit("vector", lambda e: e.reciprocal(
            sl("w_si")[:, 0:1], sl("w_sx")[:, 0:1]),
            waits=[("sc", B.c("sc"))], inc="sc")
        B.emit("vector", lambda e: e.tensor_scalar(
            sl("inv_sx")[:, 0:1], sl("w_si")[:, 0:1], 127.0, None, ALU.mult),
            waits=[("sc", B.c("sc"))], inc="sc")
        B.emit("vector", lambda e: e.tensor_scalar(
            sl("w_ss")[:, 0:1], sl("w_sx")[:, 0:1], 1.0 / 127.0, None, ALU.mult),
            waits=[("sc", B.c("sc"))], inc="sc")
        B.emit("vector", lambda e: e.tensor_scalar(
            csg.ap(), rs_g.ap(), sl("w_ss")[:, 0:1], None, ALU.mult),
            waits=[("sc", B.c("sc"))], inc="sc")
        B.emit("vector", lambda e: e.tensor_scalar(
            csu.ap(), rs_u.ap(), sl("w_ss")[:, 0:1], None, ALU.mult),
            waits=[("sc", B.c("sc"))], inc="sc")

    def emit_scales_h(Mslot):
        B.emit("vector", lambda e, M=Mslot: e.tensor_mul(
            sl("w_sd")[:, 0:1], M[:, 0:1], sl("sigc", (128, 2))[:, 1:2]), inc="sc")
        B.emit("vector", lambda e: e.reciprocal(
            sl("w_di")[:, 0:1], sl("w_sd")[:, 0:1]),
            waits=[("sc", B.c("sc"))], inc="sc")
        B.emit("vector", lambda e: e.tensor_scalar(
            sl("inv_sd")[:, 0:1], sl("w_di")[:, 0:1], 127.0, None, ALU.mult),
            waits=[("sc", B.c("sc"))], inc="sc")
        B.emit("vector", lambda e: e.tensor_scalar(
            sl("half_sd")[:, 0:1], sl("w_sd")[:, 0:1], 0.5 / 127.0, None, ALU.mult),
            waits=[("sc", B.c("sc"))], inc="sc")

    def emit_ffn_b(hamax_slot):
        n_q3 = B.n("b_q3"); n_mg = B.n("b_mg")
        n_mu = B.n("b_mu"); n_si = B.n("b_si"); n_h = B.n("b_h")
        n_hv = B.n("b_hv")
        xn_waits = B.dboth("l_xn")
        sc_val = B.c("sc")
        def quant_chunk(g, j):
            par = g % 2
            m = g * DC + j
            qp = m % 2
            B.emit("scalar", lambda e, par=par, j=j, qp=qp: e.activation(
                out=qscs[qp].ap(), in_=xq[par][:, j, :], func=AF.Identity,
                scale=sl("inv_sx")[:, 0:1], bias=sl("zb")[:, 0:1]),
                waits=[B.dtot("b_ld", g), ("sc", sc_val),
                       ("b_q3", B.h("b_q3", n_q3 + m - 1))],
                inc="b_q1")
            B.emit("vector", lambda e, qp=qp: e.tensor_scalar(
                qscs[qp].ap(), qscs[qp].ap(), RND, RND, ALU.add, ALU.subtract),
                waits=[("b_q1", B.c("b_q1"))], inc="b_q2")
            B.emit("vector", lambda e, par=par, j=j, qp=qp: e.tensor_scalar(
                xq[par][:, j, :], qscs[qp].ap(), -128.0, 127.0, ALU.max,
                ALU.min), waits=[("b_q2", B.c("b_q2"))], inc="b_q3")

        qsc_done = B.c("q_sc")
        def load_group(g):
            par = g % 2
            w = [("b_mu", B.h("b_mu", n_mu + (g - 1) * IC))] + xn_waits
            if g == 0:
                w = w + [("q_sc", qsc_done)]
            for hc in range(DC):
                B.dma2("sync", xq[par][:, hc, :],
                       bass.AP(xnTD, hc * 128 * T + g * 512,
                               [[T, 128], [1, 512]]),
                       "b_ld", g, waits=(w if hc == 0 else ()))

        load_group(0)
        for j in range(DC):
            quant_chunk(0, j)
        for g in range(ng):
            par = g % 2
            q3_need = B.c("b_q3")
            if g + 1 < ng:
                load_group(g + 1)
            for ic in range(IC):
                k = g * IC + ic
                pi = ic % 2
                pg = p0.ap() if pi == 0 else pacc0[:, 0:512]
                pu = p1.ap() if pi == 0 else pacc0[:, 512:]
                wgq = [("b_q3", q3_need), ("b_si", B.h("b_si", n_si + k - 1))]
                wuq = [("b_h", B.h("b_h", n_h + k - 1))]
                for hc in range(DC):
                    B.emit("tensor", lambda e, par=par, hc=hc, ic=ic, pg=pg:
                           e.matmul(pg, wgT[:, hc, ic * 128:(ic + 1) * 128],
                                    xq[par][:, hc, :], start=(hc == 0),
                                    stop=(hc == DC - 1)),
                           waits=(wgq if hc == 0 else ()),
                           inc=("b_mg" if hc == DC - 1 else None))
                for hc in range(DC):
                    B.emit("tensor", lambda e, par=par, hc=hc, ic=ic, pu=pu:
                           e.matmul(pu, wuT[:, hc, ic * 128:(ic + 1) * 128],
                                    xq[par][:, hc, :], start=(hc == 0),
                                    stop=(hc == DC - 1)),
                           waits=(wuq if hc == 0 else ()),
                           inc=("b_mu" if hc == DC - 1 else None))
                B.emit("scalar", lambda e, pg=pg, pi=pi, ic=ic: e.activation(
                    out=silu[pi].ap(), in_=pg, func=AF.Silu,
                    scale=csg[:, ic:ic + 1], bias=sl("zb")[:, 0:1]),
                    waits=[("b_mg", B.c("b_mg")), ("b_h", B.h("b_h", n_h + k - 1))],
                    inc="b_si")
                B.emit("vector", lambda e, pu=pu, pi=pi, ic=ic:
                       e.scalar_tensor_tensor(
                           hev[pi].ap(), pu, csu[:, ic:ic + 1], silu[pi].ap(),
                           ALU.mult, ALU.mult),
                       waits=[("b_mu", B.c("b_mu")), ("b_si", B.c("b_si")),
                              B.dtot("b_hd", k),
                              ("b_hv", B.h("b_hv", n_hv + k - 1))], inc="b_h")
                B.emit("vector", lambda e, pi=pi, k=k: e.tensor_reduce(
                    sl("hacc", (128, 176))[:, k:k + 1], hev[pi].ap(), AX.X,
                    ALU.max, apply_absolute_value=True), inc="b_hv")
                B.dma2("sync", bass.AP(hDD, ic * 128 * T + g * 512,
                                       [[T, 128], [1, 512]]),
                       hev[pi].ap(), "b_hd", k, waits=[("b_h", B.c("b_h"))])
                if g + 1 < ng and 6 <= ic < 6 + DC:
                    quant_chunk(g + 1, ic - 6)
        B.emit("vector", lambda e, t=hamax_slot, n=ng * IC: e.tensor_reduce(
            sl(t)[:, 0:1], sl("hacc", (128, 176))[:, 0:n], AX.X, ALU.max),
            inc="b_hm")

    def emit_ffn_c(res_src, xdst, first_mm_waits=(), tile_hook=None,
                   outt_guard=None):
        n_q3 = B.n("c_q3"); n_mm = B.n("c_mm")
        n_ev = B.n("c_ev")
        sc_val = B.c("sc")

        def tile_front(tt):
            par = tt % 2
            w = [("c_mm", B.h("c_mm", n_mm + tt - 1))]
            for half in range(2):
                ic0 = 11 * half
                B.dma2("sync", hq[par][:, ic0:ic0 + 11, :],
                       bass.AP(hDD, ic0 * 128 * T + tt * 128,
                               [[T, 128], [128 * T, 11], [1, 128]]),
                       "c_ld", tt, waits=(w if half == 0 else ()))
            for half in range(2):
                ic0 = 11 * half
                m = 2 * tt + half
                qp = m % 2
                B.emit("scalar", lambda e, par=par, ic0=ic0, qp=qp: e.activation(
                    out=hqfs[qp].ap(), in_=hq[par][:, ic0:ic0 + 11, :],
                    func=AF.Identity, scale=sl("inv_sd")[:, 0:1],
                    bias=sl("zb")[:, 0:1]),
                    waits=[B.dtot("c_ld", tt), ("sc", sc_val),
                           ("c_q3", B.h("c_q3", n_q3 + m - 1))],
                    inc="c_q1")
                B.emit("vector", lambda e, qp=qp: e.tensor_scalar(
                    hqfs[qp].ap(), hqfs[qp].ap(), RND, RND, ALU.add,
                    ALU.subtract),
                    waits=[("c_q1", B.c("c_q1"))], inc="c_q2")
                B.emit("vector", lambda e, par=par, ic0=ic0, qp=qp: e.tensor_scalar(
                    hq[par][:, ic0:ic0 + 11, :], hqfs[qp].ap(), -128.0, 127.0,
                    ALU.max, ALU.min),
                    waits=[("c_q2", B.c("c_q2"))], inc="c_q3")

        tile_front(0)
        for tt in range(nt):
            par = tt % 2
            acc = paccs[par]
            wmm = [("c_q3", B.c("c_q3")), ("c_ev", B.h("c_ev", n_ev + tt - 1))]
            if tt == 0:
                wmm = wmm + list(first_mm_waits)
            for ic in range(IC):
                B.emit("tensor", lambda e, par=par, ic=ic, acc=acc: e.matmul(
                    acc[:, 0:512], hq[par][:, ic, :], wdT[:, ic, 0:512],
                    start=(ic == 0), stop=(ic == IC - 1)),
                    waits=(wmm if ic == 0 else ()))
                B.emit("tensor", lambda e, par=par, ic=ic, acc=acc: e.matmul(
                    acc[:, 512:], hq[par][:, ic, :], wdT[:, ic, 512:],
                    start=(ic == 0), stop=(ic == IC - 1)),
                    inc=("c_mm" if ic == IC - 1 else None))
            B.dma2("sync", xt[par].ap(),
                   bass.AP(res_src, tt * 128 * D, [[D, 128], [1, D]]), "c_xl",
                   tt, waits=[("c_ev", B.h("c_ev", n_ev + tt - 1))])
            if tt + 1 < nt:
                tile_front(tt + 1)
            wev = [("c_mm", B.c("c_mm")), B.dtot("c_xl", tt),
                   B.dtot("c_xo", tt)]
            if outt_guard is not None:
                wev.append(outt_guard(tt))
            B.emit("vector", lambda e, par=par, acc=acc: e.scalar_tensor_tensor(
                outt[par].ap(), acc.ap(), sl("half_sd")[:, 0:1], xt[par].ap(),
                ALU.mult, ALU.add),
                waits=wev, inc="c_ev")
            B.dma2("sync", bass.AP(xdst, tt * 128 * D, [[D, 128], [1, D]]),
                   outt[par].ap(), "c_xo", tt, waits=[("c_ev", B.c("c_ev"))])
            if tile_hook is not None:
                tile_hook(tt)

    def emit_attn_wload(entry_waits):
        first = True
        for wsrc, dst in ((wqTD, awq), (wkTD, awk), (wvTD, awv)):
            for j in range(DC):
                idx = B.n("aw_c")
                par = idx % 2
                w = [("aw_c", B.h("aw_c", idx - 1))]
                if first:
                    w = w + list(entry_waits)
                    first = False
                B.dma2("sync", wst[par].ap(),
                       bass.AP(wsrc, j * 128 * D, [[D, 128], [1, D]]), "aw_ld",
                       idx, waits=w)
                B.emit("vector", lambda e, dst=dst, j=j, par=par: e.tensor_copy(
                    dst[:, j, :], wst[par].ap()),
                    waits=[B.dtot("aw_ld", idx)], inc="aw_c")

    def emit_proj():
        xn_waits = B.dboth("l_xn")
        aw_val = B.c("aw_c")

        def one_pass(pre, wsb, bias_pp, dstD, s_mm, s_ev, s_dma, prev_mm=None,
                     prev_dma=None):
            n_mm = B.n(s_mm); n_ev = B.n(s_ev)

            def g_load(g):
                par = g % 2
                w = [(s_mm, B.h(s_mm, n_mm + (g - 1) * DC)),
                     ("aw_c", aw_val)] + xn_waits
                if g < 2 and prev_mm is not None:
                    w.append(prev_mm)
                for hc in range(DC):
                    B.dma2("sync", xn2[par][:, hc, :],
                           bass.AP(xnTD, hc * 128 * T + g * 512,
                                   [[T, 128], [1, 512]]),
                           "pj_ld", g, waits=(w if hc == 0 else ()))

            g_load(0)
            for g in range(ng):
                par = g % 2
                for m in range(DC):
                    gm = g * DC + m
                    pp = p0 if m % 2 == 0 else p1
                    wmm = [B.dtot("pj_ld", g),
                           (s_ev, B.h(s_ev, n_ev + gm - 1))]
                    for hc in range(DC):
                        B.emit("tensor",
                               lambda e, par=par, hc=hc, m=m, pp=pp, wsb=wsb:
                               e.matmul(pp.ap(),
                                        wsb[:, hc, m * 128:(m + 1) * 128],
                                        xn2[par][:, hc, :],
                                        start=(hc == 0), stop=(hc == DC - 1)),
                               waits=(wmm if hc == 0 else ()),
                               inc=(s_mm if hc == DC - 1 else None))
                    wev = [(s_mm, B.c(s_mm)), B.dtot(s_dma, gm)]
                    if gm < 2 and prev_dma is not None:
                        wev += prev_dma
                    B.emit("vector",
                           lambda e, m=m, pp=pp, bias_pp=bias_pp: e.tensor_scalar(
                               kev[m % 2].ap(), pp.ap(), bias_pp[:, m:m + 1],
                               None, ALU.add),
                           waits=wev, inc=s_ev)
                    B.dma2("sync", bass.AP(dstD, m * 128 * T + g * 512,
                                           [[T, 128], [1, 512]]),
                           kev[m % 2].ap(), s_dma, gm, waits=[(s_ev, B.c(s_ev))])
                    if m == 0 and g + 1 < ng:
                        g_load(g + 1)

        one_pass("pk", awk, bk_pp, ktoD, "pk_mm", "pk_ev", "pk_dma")
        B.emit("gpsimd", lambda e: e.collective_compute(
            "AllGather", ALU.bypass,
            replica_groups=[[0, 1], [2, 3], [4, 5], [6, 7]],
            ins=[flat(ktoD, D * T)], outs=[flat(ktgD, 2 * D * T)]),
            waits=B.dboth("pk_dma"), inc="ag_kt")

        n_mm = B.n("pv_mm"); n_ev = B.n("pv_ev")
        pk_mm_done = ("pk_mm", B.c("pk_mm"))

        def v_load(g):
            par = g % 2
            w = [("pv_mm", B.h("pv_mm", n_mm + (g - 1) * 4)),
                 ("aw_c", aw_val)] + xn_waits
            if g < 2:
                w.append(pk_mm_done)
            for hc in range(DC):
                B.dma2("sync", xn2[par][:, hc, :],
                       bass.AP(xnTD, hc * 128 * T + g * 512,
                               [[T, 128], [1, 512]]),
                       "pj_ld", g, waits=(w if hc == 0 else ()))

        v_load(0)
        for g in range(ng):
            par = g % 2
            for sub in range(4):
                gs = g * 4 + sub
                acc = paccs[sub % 2]
                wmm = [B.dtot("pj_ld", g),
                       ("pv_ev", B.h("pv_ev", n_ev + gs - 1))]
                for hc in range(DC):
                    B.emit("tensor", lambda e, par=par, hc=hc, sub=sub, acc=acc:
                           e.matmul(acc[:, 0:512],
                                    xn2[par][:, hc, sub * 128:(sub + 1) * 128],
                                    awv[:, hc, 0:512], start=(hc == 0), stop=False),
                           waits=(wmm if hc == 0 else ()))
                    B.emit("tensor", lambda e, par=par, hc=hc, sub=sub, acc=acc:
                           e.matmul(acc[:, 512:],
                                    xn2[par][:, hc, sub * 128:(sub + 1) * 128],
                                    awv[:, hc, 512:], start=(hc == 0), stop=False))
                B.emit("tensor", lambda e, acc=acc: e.matmul(
                    acc[:, 0:512], ones_b[0:1, :], bv_row[0:1, 0:512],
                    start=False, stop=True))
                B.emit("tensor", lambda e, acc=acc: e.matmul(
                    acc[:, 512:], ones_b[0:1, :], bv_row[0:1, 512:],
                    start=False, stop=True), inc="pv_mm")
                B.emit("vector", lambda e, sub=sub, acc=acc: e.tensor_copy(
                    vev[sub % 2].ap(), acc.ap()),
                    waits=[("pv_mm", B.c("pv_mm")), B.dtot("pv_dma", gs)],
                    inc="pv_ev")
                B.dma2("sync", bass.AP(voD, (g * 512 + sub * 128) * D,
                                       [[D, 128], [1, D]]),
                       vev[sub % 2].ap(), "pv_dma", gs,
                       waits=[("pv_ev", B.c("pv_ev"))])
                if sub == 0 and g + 1 < ng:
                    v_load(g + 1)
        B.emit("gpsimd", lambda e: e.collective_compute(
            "AllGather", ALU.bypass,
            replica_groups=[[0, 1], [2, 3], [4, 5], [6, 7]],
            ins=[flat(voD, T * D)], outs=[flat(vgD, 2 * T * D)]),
            waits=B.dboth("pv_dma"), inc="ag_v")

        one_pass("pq", awq, bq_pp, qtD, "pq_mm", "pq_ev", "pq_dma",
                 prev_mm=("pv_mm", B.c("pv_mm")), prev_dma=B.dboth("pk_dma"))

    def emit_flash(res_src, xdst, sb_hook=None, outt_guard=None):
        entry = B.dboth("pq_dma") + [("ag_kt", B.c("ag_kt")),
                                     ("ag_v", B.c("ag_v"))]
        for e_ in ENGINES:
            B.wait_only(e_, entry)
        n_sm = B.n("f_sm"); n_ac = B.n("f_ac"); n_u = B.n("f_u")
        n_ex = B.n("f_ex"); n_ptr = B.n("f_ptr"); n_pev = B.n("f_pev")
        n_pv = B.n("f_pv"); n_ao = B.n("f_ao"); n_aoe = B.n("f_aoe")
        n_aot = B.n("f_aot"); n_oe = B.n("f_oe"); n_om = B.n("f_om")
        qtbs = [qtb[0], qtb[1], qtb3]
        QB = min(4, nkc)
        nseg = nkc // QB
        hk = nkc // 2
        npc = max(1, hk // 2)
        pwk = hk // npc
        bst = {"rr": 0, "lu": {}, "sb_pacc": -1}

        def scores_dma(sb, kc):
            sbpar = sb % 3
            if kc == 0:
                for hc in range(DC):
                    B.dma2("sync", qtbs[sbpar][:, hc, :],
                           bass.AP(qtD, hc * 128 * T + sb * 256,
                                   [[T, 128], [1, 256]]),
                           "f_qt", sb,
                           waits=([("f_sm",
                                    B.h("f_sm", n_sm + (sb - 2) * nkc * 2))]
                                  if hc == 0 else ()))
            kpar = kc % 2
            gi = sb * nkc + kc
            rank = 0 if kc * 512 < T else 1
            base = rank * D * T + (kc * 512 - rank * T)
            B.dma2("sync", ktb[kpar][:, :, :],
                   bass.AP(ktgD, base, [[T, 128], [128 * T, 8], [1, 512]]),
                   "f_kt", gi,
                   waits=[("f_sm", B.h("f_sm", n_sm + (gi - 1) * 2))])

        def scores_kc(sb, kc, dma_done=False):
            sbpar = sb % 3
            if not dma_done:
                scores_dma(sb, kc)
            kpar = kc % 2
            gi = sb * nkc + kc
            for qt2 in (0, 1):
                ci = gi * 2 + qt2
                use4 = (sb > 0 and kc >= 5)
                if use4:
                    if kc == 5 and qt2 == 0:
                        bst["rr"] = 0
                    bi = bst["rr"] % 4
                    bst["rr"] += 1
                else:
                    bi = ci % 2
                psap = (p0.ap(), p1.ap(), pacc0[:, 0:512],
                        pacc0[:, 512:])[bi]
                tpar = ci % 3
                wmm = [B.dtot("f_kt", gi), B.dtot("f_qt", sb)]
                lu = bst["lu"].get(bi)
                if lu is not None:
                    wmm.append(("f_ac", B.h("f_ac", n_ac + lu + 1)))
                if bi >= 2 and bst["sb_pacc"] != sb:
                    bst["sb_pacc"] = sb
                    wmm.append(("f_oe",
                                B.h("f_oe", n_oe + 2 * (sb - 1) + 2)))
                bst["lu"][bi] = ci
                for hc in range(DC):
                    B.emit("tensor", lambda e, sbpar=sbpar, kpar=kpar, hc=hc,
                           qt2=qt2, psap=psap: e.matmul(
                               psap,
                               qtbs[sbpar][:, hc, qt2 * 128:(qt2 + 1) * 128],
                               ktb[kpar][:, hc, :],
                               start=(hc == 0), stop=(hc == DC - 1)),
                           waits=(wmm if hc == 0 else ()),
                           inc=("f_sm" if hc == DC - 1 else None))
                B.emit("scalar", lambda e, psap=psap, tpar=tpar: e.activation(
                    out=t1s[tpar].ap(), in_=psap, func=AF.Tanh, scale=0.5,
                    bias=sl("zb")[:, 0:1]),
                    waits=[("f_sm", B.c("f_sm")),
                           ("f_u", B.h("f_u", n_u + ci - 2))])
                B.emit("scalar", lambda e, psap=psap, tpar=tpar: e.activation(
                    out=t2s[tpar].ap(), in_=psap, func=AF.Tanh,
                    bias=sl("zb")[:, 0:1]))
                B.emit("scalar", lambda e, psap=psap, tpar=tpar: e.activation(
                    out=rsc[tpar].ap(), in_=psap, func=AF.Relu,
                    bias=sl("zb")[:, 0:1]), inc="f_ac")
                B.emit("vector", lambda e, tpar=tpar: e.scalar_tensor_tensor(
                    t1s[tpar].ap(), t2s[tpar].ap(), 1.5, t1s[tpar].ap(),
                    ALU.mult, ALU.add),
                    waits=[("f_ac", B.c("f_ac"))])
                B.emit("vector", lambda e, qt2=qt2, kc=kc, tpar=tpar:
                       e.tensor_add(
                           Ubuf[qt2][:, kc * 512:(kc + 1) * 512],
                           t1s[tpar].ap(), rsc[tpar].ap()),
                       waits=[("f_ex",
                               B.h("f_ex", n_ex + 4 * npc * (sb - 1)
                                   + (kc // hk) * 2 * npc
                                   + 2 * ((kc % hk) // pwk) + qt2 + 1))],
                       inc="f_u")
                if (kc + 1) % QB == 0:
                    seg = (kc + 1) // QB - 1
                    B.emit("vector", lambda e, qt2=qt2, kc=kc, seg=seg:
                           e.tensor_reduce(
                               sl(f"mp{qt2}", (128, 16))[:, seg:seg + 1],
                               Ubuf[qt2][:, (kc + 1 - QB) * 512:
                                         (kc + 1) * 512],
                               AX.X, ALU.max))

        def exp_part(sb):
            for qt2 in (0, 1):
                B.emit("vector", lambda e, qt2=qt2: e.tensor_reduce(
                    sl(f"mrow{qt2}")[:, 0:1],
                    sl(f"mp{qt2}", (128, 16))[:, 0:nseg], AX.X, ALU.max),
                    waits=[("f_u",
                            B.h("f_u", n_u + 2 * nkc * sb + 2 * nkc - 1 + qt2))])
                B.emit("vector", lambda e, qt2=qt2: e.tensor_scalar(
                    sl(f"negm{qt2}")[:, 0:1], sl(f"mrow{qt2}")[:, 0:1], -0.1,
                    None, ALU.mult), inc="f_mxr")
            first = True
            for half in (0, 1):
                for p in range(npc):
                    for qt2 in (0, 1):
                        c0 = half * (S // 2) + p * pwk * 512
                        c1 = c0 + pwk * 512
                        idx = half * npc + p
                        vlast = half * hk + (p + 1) * pwk - 1
                        w = [("f_ptr", B.h("f_ptr", n_ptr
                                           + 2 * nkc * (sb - 1)
                                           + 2 * vlast + qt2 + 1))]
                        if first:
                            w.append(("f_mxr", B.c("f_mxr")))
                            first = False
                        B.emit("scalar", lambda e, qt2=qt2, c0=c0, c1=c1,
                               idx=idx: e.activation(
                                   out=Pbuf[qt2][:, c0:c1],
                                   in_=Ubuf[qt2][:, c0:c1],
                                   func=AF.Exp, bias=sl(f"negm{qt2}")[:, 0:1],
                                   scale=0.1,
                                   accum_out=lrp[qt2][:, idx:idx + 1]),
                               waits=w, inc="f_ex")

        def pv_part(sb):
            for vc in range(nkc):
                vpar = vc % 2
                B.dma2("sync", vtb[vpar][:, :, :],
                       bass.AP(vgD, vc * 512 * D,
                               [[D, 128], [128 * D, 4], [1, D]]),
                       "f_vt", sb * nkc + vc,
                       waits=[("f_pv",
                               B.h("f_pv", n_pv + 2 * nkc * sb + 2 * vc - 2))])
                for qt2 in (0, 1):
                    wtr = [("f_ex",
                            B.h("f_ex", n_ex + 4 * npc * sb
                                + (vc // hk) * 2 * npc
                                + 2 * ((vc % hk) // pwk) + qt2 + 1)),
                           ("f_pev",
                            B.h("f_pev", n_pev + 2 * nkc * sb + 2 * vc + qt2 - 1))]
                    if vc == 0:
                        wtr.append(("f_aoe", B.h("f_aoe", n_aoe + 2 * (sb - 1)
                                                 + qt2 + 1)))
                    for i4 in range(4):
                        B.emit("tensor", lambda e, qt2=qt2, vc=vc, i4=i4:
                               e.transpose(
                                   pts[qt2][:, i4, :],
                                   Pbuf[qt2][:, (vc * 4 + i4) * 128:
                                             (vc * 4 + i4 + 1) * 128],
                                   ident.ap()),
                               waits=(wtr if i4 == 0 else ()),
                               inc=("f_ptr" if i4 == 3 else None))
                for qt2 in (0, 1):
                    B.emit("vector", lambda e, qt2=qt2: e.tensor_copy(
                        ptsb[qt2][:, 0:4, :], pts[qt2][:, 0:4, :]),
                        waits=[("f_ptr", B.h("f_ptr",
                                             n_ptr + 2 * nkc * sb + 2 * vc
                                             + qt2 + 1)),
                               ("f_pv",
                                B.h("f_pv", n_pv + 2 * nkc * sb + 2 * (vc - 1)
                                    + qt2 + 1))], inc="f_pev")
                for qt2 in (0, 1):
                    acc = paccs[qt2]
                    wpv = [("f_pev", B.h("f_pev",
                                         n_pev + 2 * nkc * sb + 2 * vc
                                         + qt2 + 1)),
                           B.dtot("f_vt", sb * nkc + vc)]
                    if vc == 0:
                        wpv.append(("f_oe", B.h("f_oe", n_oe + 2 * sb + qt2 - 1)))
                    for half in (0, 1):
                        for i4 in range(4):
                            B.emit("tensor", lambda e, qt2=qt2, vc=vc, i4=i4,
                                   vpar=vpar, acc=acc, half=half: e.matmul(
                                       acc[:, half * 512:(half + 1) * 512],
                                       ptsb[qt2][:, i4, :],
                                       vtb[vpar][:, i4,
                                                 half * 512:(half + 1) * 512],
                                       start=(vc == 0 and i4 == 0),
                                       stop=(vc == nkc - 1 and i4 == 3)),
                                   waits=(wpv if (half == 0 and i4 == 0)
                                          else ()),
                                   inc=("f_pv" if (half == 1 and i4 == 3)
                                        else None))

        def epi_e0(sb):
            for qt2 in (0, 1):
                B.emit("vector", lambda e, qt2=qt2: e.tensor_reduce(
                    sl(f"lrow{qt2}")[:, 0:1], lrp[qt2][:, 0:2 * npc],
                    AX.X, ALU.add), waits=[("f_ex", B.c("f_ex"))])
                B.emit("vector", lambda e, qt2=qt2: e.reciprocal(
                    sl(f"invl{qt2}")[:, 0:1], sl(f"lrow{qt2}")[:, 0:1]),
                    inc="f_iv")
            for qt2 in (0, 1):
                acc = paccs[qt2]
                B.emit("vector", lambda e, qt2=qt2, acc=acc: e.tensor_scalar(
                    ao_sb2[qt2].ap(), acc.ap(), sl(f"invl{qt2}")[:, 0:1], None,
                    ALU.mult),
                    waits=[("f_pv", B.h("f_pv", n_pv + 2 * nkc * sb
                                        + 2 * (nkc - 1) + qt2 + 1)),
                           ("f_iv", B.c("f_iv")),
                           ("f_aot", B.h("f_aot", n_aot + 2 * sb + qt2 - 1))],
                    inc="f_ao")
            for qt2 in (0, 1):
                row = sb * 256 + qt2 * 128
                B.dma2("sync", xt[qt2].ap(),
                       bass.AP(res_src, row * D, [[D, 128], [1, D]]), "f_xl",
                       qt2,
                       waits=[("f_oe", B.h("f_oe", n_oe + 2 * sb + qt2 - 1))])

        def epi_aot(sb, qt2):
            for j in range(DC):
                B.emit("tensor", lambda e, qt2=qt2, j=j: e.transpose(
                    pts[qt2][:, j, :], ao_sb2[qt2][:, j * 128:(j + 1) * 128],
                    ident.ap()),
                    waits=([("f_ao", B.h("f_ao", n_ao + 2 * sb + qt2 + 1))]
                           if j == 0 else ()),
                    inc=("f_aot" if j == DC - 1 else None))

        def epi_aoe(sb, qt2):
            B.emit("vector", lambda e, qt2=qt2: e.tensor_copy(
                aoT_sb2[qt2].ap(), pts[qt2].ap()),
                waits=[("f_aot", B.h("f_aot", n_aot + 2 * sb + qt2 + 1)),
                       ("f_om", B.h("f_om", n_om + 2 * (sb - 1) + qt2 + 1))],
                inc="f_aoe")

        def epi_omm(sb, qt2):
            acc = paccs[qt2]
            womm = [("f_aoe", B.h("f_aoe", n_aoe + 2 * sb + qt2 + 1)),
                    ("f_ao", B.h("f_ao", n_ao + 2 * sb + 2))]
            for hc in range(DC):
                B.emit("tensor", lambda e, hc=hc, acc=acc, qt2=qt2: e.matmul(
                    acc[:, 0:512], aoT_sb2[qt2][:, hc, :],
                    wo_sb[:, hc, 0:512],
                    start=(hc == 0), stop=False),
                    waits=(womm if hc == 0 else ()))
                B.emit("tensor", lambda e, hc=hc, acc=acc, qt2=qt2: e.matmul(
                    acc[:, 512:], aoT_sb2[qt2][:, hc, :],
                    wo_sb[:, hc, 512:],
                    start=(hc == 0), stop=False))
            B.emit("tensor", lambda e, acc=acc: e.matmul(
                acc[:, 0:512], ones_b[0:1, :], bo_row[0:1, 0:512],
                start=False, stop=True))
            B.emit("tensor", lambda e, acc=acc: e.matmul(
                acc[:, 512:], ones_b[0:1, :], bo_row[0:1, 512:],
                start=False, stop=True), inc="f_om")

        def epi_out(sb, qt2):
            acc = paccs[qt2]
            row = sb * 256 + qt2 * 128
            wev = [("f_om", B.h("f_om", n_om + 2 * sb + qt2 + 1)),
                   B.dtot("f_xl", qt2), B.dtot("f_xo", qt2)]
            if outt_guard is not None:
                g_ = outt_guard(sb, qt2)
                if g_ is not None:
                    wev.append(g_)
            B.emit("vector", lambda e, qt2=qt2, acc=acc: e.tensor_add(
                outt[qt2][:, 0:512], acc[:, 0:512], xt[qt2][:, 0:512]),
                waits=wev)
            B.emit("vector", lambda e, qt2=qt2, acc=acc: e.tensor_add(
                outt[qt2][:, 512:], acc[:, 512:], xt[qt2][:, 512:]),
                inc="f_oe")
            B.dma2("sync", bass.AP(xdst, row * D, [[D, 128], [1, D]]),
                   outt[qt2].ap(), "f_xo", qt2,
                   waits=[("f_oe", B.c("f_oe"))])

        for kc in range(nkc):
            scores_kc(0, kc)
        for sb in range(nsb):
            exp_part(sb)
            if sb + 1 < nsb:
                scores_dma(sb + 1, 0)
                scores_dma(sb + 1, 1)
            pv_part(sb)
            stages = [lambda sb=sb: epi_e0(sb),
                      lambda sb=sb: epi_aot(sb, 0),
                      lambda sb=sb: (epi_aoe(sb, 0), epi_aot(sb, 1)),
                      lambda sb=sb: (epi_aoe(sb, 1), epi_omm(sb, 0)),
                      lambda sb=sb: (epi_omm(sb, 1), epi_out(sb, 0)),
                      lambda sb=sb: (epi_out(sb, 1),
                                     sb_hook(sb) if sb_hook else None)]
            if sb + 1 < nsb:
                stages[0]()
                done = 1
                for kc in range(nkc):
                    scores_kc(sb + 1, kc, dma_done=(kc < 2))
                    if done < len(stages):
                        stages[done]()
                        done += 1
                while done < len(stages):
                    stages[done]()
                    done += 1
            else:
                for f_ in stages:
                    f_()

    def emit_wreload(entry_waits):
        first = True
        for j in range(DC):
            w = (list(entry_waits) + [("q_st", B.c("q_st"))]) if first else ()
            first = False
            B.dma("sync", wgT[:, j, :],
                  bass.AP(wgS, j * 128 * I, [[I, 128], [1, I]]), "wr_ld", waits=w)
        for j in range(DC):
            B.dma("sync", wuT[:, j, :],
                  bass.AP(wuS, j * 128 * I, [[I, 128], [1, I]]), "wr_ld")
        for ic in range(IC):
            B.dma("sync", wdT[:, ic, :],
                  bass.AP(wdS, ic * 128 * D, [[D, 128], [1, D]]), "wr_ld")

    gu_ev, quant_w_fn = emit_setup()
    emit_ln_dram(xin, "amax1", entry_waits=[("p0_misc", 64)])
    xpart_request(sl("amax1"), ccB[0], [("l_mx", B.c("l_mx"))], [], "c1")
    quant_w_fn(wdD, None, wdT, wdS, DC, True, I)
    xpart_complete(sl("M1"), "c1")
    emit_scales_x(sl("M1"))
    for f_ in deferred_stores:
        f_()
    B.wait_only("tensor", [("q_ev", gu_ev), ("sc", B.c("sc"))])
    emit_ffn_b("hamax1")
    xpart_max_allreduce(sl("hamax1"), ccB[1], sl("M2"),
                        [("b_hm", B.c("b_hm"))] + B.dboth("b_hd"), [], "c2")
    emit_scales_h(sl("M2"))
    b1_h = B.c("b_h")
    ln2 = LNPass(None, act_norm=True)
    emit_ffn_c(xin, x1D,
               first_mm_waits=[("b_h", b1_h), ("q_ev", B.c("q_ev"))],
               tile_hook=lambda tt: ln2.tile(
                   tt, outt[tt % 2].ap(), [("c_ev", B.c("c_ev"))]),
               outt_guard=lambda tt: ln2.norm_done(tt - 2) if tt >= 2 else None)
    emit_attn_wload([("c_mm", B.c("c_mm")), ("b_mu", B.c("b_mu")),
                     ("q_st", B.c("q_st"))] + B.dboth("c_xo"))
    emit_proj()
    ln3 = LNPass("amax3")

    def ln3_hook(sb):
        g = [("f_oe", B.c("f_oe"))]
        ln3.stats(2 * sb, outt[0].ap(), g)
        ln3.stats(2 * sb + 1, outt[1].ap(), g)
        ln3.norm(2 * sb, outt[0].ap())
        ln3.norm(2 * sb + 1, outt[1].ap())

    emit_flash(x1D, x2D, sb_hook=ln3_hook,
               outt_guard=lambda sb, qt2: (ln3.norm_done(2 * (sb - 1) + qt2)
                                           if sb >= 1 else None))
    emit_wreload(B.dboth("f_xo"))
    xpart_max_allreduce(sl("amax3"), ccB[2], sl("M1"),
                        [("l_mx", B.c("l_mx"))], [("f_oe", B.c("f_oe"))], "c3")
    emit_scales_x(sl("M1"))
    B.wait_only("tensor", [("wr_ld", B.c("wr_ld")), ("sc", B.c("sc"))])
    emit_ffn_b("hamax3")
    xpart_max_allreduce(sl("hamax3"), ccB[3], sl("M2"),
                        [("b_hm", B.c("b_hm"))] + B.dboth("b_hd"), [], "c4")
    emit_scales_h(sl("M2"))
    b3_h = B.c("b_h")
    emit_ffn_c(x2D, xout, first_mm_waits=[("b_h", b3_h)])
    B.wait_only("sync", B.dboth("c_xo"))
    B.wait_only("gpsimd", B.dboth("c_xo"))

    with nc.Block() as block:
        @block.sync
        def _(eng):
            for f in B.q["sync"]:
                f(eng)

        @block.scalar
        def _(eng):
            for f in B.q["scalar"]:
                f(eng)

        @block.vector
        def _(eng):
            for f in B.q["vector"]:
                f(eng)

        @block.tensor
        def _(eng):
            for f in B.q["tensor"]:
                f(eng)

        @block.gpsimd
        def _(eng):
            for f in B.q["gpsimd"]:
                f(eng)

    return nc


def make_in_maps(x_full, Wq, bq, Wk, bk, Wv, bv, Wo, bo, clip_x, clip_down,
                 w_gate, w_up, w_down, T=4096):
    f32 = np.float32
    ident = np.eye(128, dtype=ml_dtypes.bfloat16)
    com = {
        "wqT": np.ascontiguousarray(np.asarray(Wq, f32).T),
        "wkT": np.ascontiguousarray(np.asarray(Wk, f32).T),
        "wvT": np.ascontiguousarray(np.asarray(Wv, f32).T),
        "woT": np.ascontiguousarray(np.asarray(Wo, f32).T),
        "bqp": np.ascontiguousarray(np.asarray(bq, f32).reshape(8, 128).T),
        "bkp": np.ascontiguousarray(np.asarray(bk, f32).reshape(8, 128).T),
        "bvr": np.asarray(bv, f32).reshape(1, D),
        "bor": np.asarray(bo, f32).reshape(1, D),
        "wg": np.asarray(w_gate, f32),
        "wu": np.asarray(w_up, f32),
        "wd": np.asarray(w_down, f32),
        "clip": np.array([[np.asarray(clip_x).reshape(-1)[0],
                           np.asarray(clip_down).reshape(-1)[0]]], f32),
        "ident": ident,
    }
    in_maps = []
    for c in range(8):
        b, hh = c // 2, c % 2
        shard = np.ascontiguousarray(x_full[b, hh * T:(hh + 1) * T, :], dtype=f32)
        m = dict(com)
        m["xin"] = shard
        in_maps.append(m)
    return in_maps


def kernel(L0_attn_x, L0_attn_Wq, L0_attn_bq, L0_attn_Wk, L0_attn_bk,
           L0_attn_Wv, L0_attn_bv, L0_attn_Wo, L0_attn_bo,
           L0_ffn_clip_x, L0_ffn_clip_down, L0_ffn_w_gate, L0_ffn_w_up,
           L0_ffn_w_down):
    T = 4096
    x = np.asarray(L0_attn_x, np.float32)
    in_maps = make_in_maps(x, L0_attn_Wq, L0_attn_bq, L0_attn_Wk, L0_attn_bk,
                           L0_attn_Wv, L0_attn_bv, L0_attn_Wo, L0_attn_bo,
                           L0_ffn_clip_x, L0_ffn_clip_down, L0_ffn_w_gate,
                           L0_ffn_w_up, L0_ffn_w_down, T=T)
    nc = build_program(T)
    res = run_bass_kernel_spmd(nc, in_maps, core_ids=list(range(8)))
    out = np.empty((4, 2 * T, D), np.float32)
    for c in range(8):
        b, hh = c // 2, c % 2
        out[b, hh * T:(hh + 1) * T, :] = res.results[c]["xout"]
    return out



# version 30 (final)
# speedup vs baseline: 1.1528x; 1.0081x over previous
import numpy as np
import ml_dtypes
import concourse.bass as bass
import concourse.mybir as mybir
import concourse.bass_utils as _bass_utils
from concourse.bass_utils import run_bass_kernel_spmd

_orig_run_command = _bass_utils.run_command


def _run_command_ldwopt(argv, **kwargs):
    argv = ["--enable-ldw-opt=true" if a == "--enable-ldw-opt=false" else a
            for a in argv]
    return _orig_run_command(argv, **kwargs)


_bass_utils.run_command = _run_command_ldwopt

F32 = mybir.dt.float32
BF16 = mybir.dt.bfloat16
AF = mybir.ActivationFunctionType
ALU = mybir.AluOpType
AX = mybir.AxisListType

RND = 12582912.0
D = 1024
DC = 8
I = 2816
IC = 22
EPS = 1e-5

ENGINES = ("sync", "scalar", "vector", "tensor", "gpsimd")

OFF_IDENT = 2048
OFF_BQ = 2304
OFF_BK = 2336
OFF_BVROW = 2368
OFF_BOROW = 4416
OFF_ONESB = 6464
OFF_ONESF = 6720
OFF_CLIP = 7232
OFF_LNST = 7264
OFF_ROW128 = 7328
OFF_RSG = 7840
OFF_RSU = 7936
OFF_CSG = 8032
OFF_CSU = 8128
OFF_SLOTS = 8224
OFF_XT = 10752
OFF_OUTT = 19456
OFF_XNBF = 27648
OFF_TEVAC = 31744
OFF_WO = 35840
OFF_W = 52224
FO_WG = 0
FO_WU = 45056
FO_WD = 90112
FO_XQ = 135168
FO_HEV = 151552
FO_SILU = 153600
FO_HQ = 157696
FO_HQF = 168960
FO_P0STG = 151552
FO_P0WI = 162816
AO_WQ = 0
AO_WK = 16384
AO_WV = 32768
AO_XN2 = 49152
AO_KEV = 65536
AO_VEV = 67584
AO_WST = 71680
FL_QTB = 0
FL_KTB = 8192
FL_VTB = 24576
FL_U = 40960
FL_P = 106496
FL_T1 = 139264
FL_T2 = 143360
FL_R = 147456
FL_PT = 151552
FL_AO = 155648
FL_AOT = 157696


class Builder:
    def __init__(self, nc):
        self.nc = nc
        self.q = {e: [] for e in ENGINES}
        self.sems = {}
        self.cnt = {}
        self.hist = {}
        self._slots = {}
        self._slot_next = OFF_SLOTS

    def sem(self, name):
        if name not in self.sems:
            self.sems[name] = self.nc.alloc_semaphore(name)
            self.cnt[name] = 0
            self.hist[name] = []
        return self.sems[name]

    def c(self, name):
        self.sem(name)
        return self.cnt[name]

    def n(self, name):
        self.sem(name)
        return len(self.hist[name])

    def h(self, name, k):
        self.sem(name)
        if k <= 0:
            return 0
        hist = self.hist[name]
        assert k <= len(hist), (name, k, len(hist))
        return hist[k - 1]

    SERIAL = {"vector": "ser_v", "scalar": "ser_s", "gpsimd": "ser_g"}

    def emit(self, eng, fn, waits=(), inc=None, amt=1):
        waits = tuple(w for w in waits if w is not None)
        for s, _ in waits:
            self.sem(s)
        if inc is not None and amt > 0:
            self.sem(inc)
            self.cnt[inc] += amt
            self.hist[inc].append(self.cnt[inc])
        else:
            inc = None
        ser = self.SERIAL.get(eng)
        prev_tok = None
        my_inc = inc
        if ser is not None:
            prev_tok = getattr(self, "_tok", {}).get(eng)
            if not hasattr(self, "_tok"):
                self._tok = {}
            if inc is not None:
                self._tok[eng] = (inc, self.cnt[inc])
            else:
                self.sem(ser)
                self.cnt[ser] += 1
                self.hist[ser].append(self.cnt[ser])
                my_inc = ser
                amt = 1
                self._tok[eng] = (ser, self.cnt[ser])
        sems = self.sems

        def run(engine, fn=fn, waits=waits, inc=my_inc, amt=amt,
                prev_tok=prev_tok):
            if prev_tok is not None and prev_tok[1] > 0:
                engine.wait_ge(sems[prev_tok[0]], prev_tok[1])
            for s, v in waits:
                if v > 0:
                    engine.wait_ge(sems[s], v)
            ins = fn(engine)
            if inc is not None:
                ins.then_inc(sems[inc], amt)

        self.q[eng].append(run)

    def wait_only(self, eng, waits):
        waits = tuple(w for w in waits if w is not None and w[1] > 0)
        if not waits:
            return
        for s, _ in waits:
            self.sem(s)
        sems = self.sems

        def run(engine, waits=waits):
            for s, v in waits:
                engine.wait_ge(sems[s], v)

        self.q[eng].append(run)

    def dma(self, eng, out, in_, inc, waits=(), **kw):
        self.emit(eng, lambda e, o=out, i=in_, kw=kw: e.dma_start(out=o, in_=i, **kw),
                  waits=waits, inc=inc, amt=16)

    def dma2(self, eng, out, in_, fam, idx, waits=(), **kw):
        self.dma(eng, out, in_, f"{fam}{idx % 2}", waits=waits, **kw)

    def dtot(self, fam, idx):
        name = f"{fam}{idx % 2}"
        return (name, self.c(name))

    def dboth(self, fam):
        return [(f"{fam}0", self.c(f"{fam}0")), (f"{fam}1", self.c(f"{fam}1"))]

    def slot(self, name, shape=(128, 1)):
        if name not in self._slots:
            nbytes = int(np.prod(shape[1:])) * 4
            nbytes = (nbytes + 31) // 32 * 32
            off = self._slot_next
            assert off + nbytes <= OFF_XT, (name, off)
            self._slot_next = off + nbytes
            self._slots[name] = self.nc.alloc_sbuf_tensor_at(
                "slot_" + name, list(shape), F32, offset=off)
        return self._slots[name]


def flat(handle, n, off=0):
    return bass.AP(handle, off, [[1, 1], [1, n]])


def build_program(T=4096):
    nc = bass.Bass("TRN2", target_bir_lowering=False, num_devices=8,
                   dynamic_dma_scratch_size=2048)
    nt = T // 128
    ng = T // 512
    S = 2 * T
    nkc = S // 512
    nsb = T // 256

    xin = nc.dram_tensor("xin", [T, D], F32, kind="ExternalInput")
    wqTD = nc.dram_tensor("wqT", [D, D], F32, kind="ExternalInput")
    wkTD = nc.dram_tensor("wkT", [D, D], F32, kind="ExternalInput")
    wvTD = nc.dram_tensor("wvT", [D, D], F32, kind="ExternalInput")
    woTD = nc.dram_tensor("woT", [D, D], F32, kind="ExternalInput")
    bqpD = nc.dram_tensor("bqp", [128, 8], F32, kind="ExternalInput")
    bkpD = nc.dram_tensor("bkp", [128, 8], F32, kind="ExternalInput")
    bvrD = nc.dram_tensor("bvr", [1, D], F32, kind="ExternalInput")
    borD = nc.dram_tensor("bor", [1, D], F32, kind="ExternalInput")
    wgD = nc.dram_tensor("wg", [I, D], F32, kind="ExternalInput")
    wuD = nc.dram_tensor("wu", [I, D], F32, kind="ExternalInput")
    wdD = nc.dram_tensor("wd", [D, I], F32, kind="ExternalInput")
    clipD = nc.dram_tensor("clip", [1, 2], F32, kind="ExternalInput")
    identD = nc.dram_tensor("ident", [128, 128], BF16, kind="ExternalInput")
    xout = nc.dram_tensor("xout", [T, D], F32, kind="ExternalOutput")

    xnTD = nc.dram_tensor("xnT", [D, T], BF16)
    hDD = nc.dram_tensor("hd", [I, T], BF16)
    x1D = nc.dram_tensor("x1", [T, D], F32)
    x2D = nc.dram_tensor("x2", [T, D], F32)
    qtD = nc.dram_tensor("qt", [D, T], BF16)
    ktoD = nc.dram_tensor("kto", [D, T], BF16)
    ktgD = nc.dram_tensor("ktg", [2 * D, T], BF16)
    voD = nc.dram_tensor("vo", [T, D], BF16)
    vgD = nc.dram_tensor("vg", [2 * T, D], BF16)
    wgS = nc.dram_tensor("wgs", [D, I], BF16)
    wuS = nc.dram_tensor("wus", [D, I], BF16)
    wdS = nc.dram_tensor("wds", [I, D], BF16)
    scrD = nc.dram_tensor("scr", [1, 128], F32)
    ccB = [nc.dram_tensor(f"cc{j}", [1, 8], F32) for j in range(4)]

    A = nc.alloc_sbuf_tensor_at
    ident = A("ident", [128, 128], BF16, offset=OFF_IDENT)
    identF = A("identf", [128, 128], F32, offset=18944)
    bq_pp = A("bq_pp", [128, 8], F32, offset=OFF_BQ)
    bk_pp = A("bk_pp", [128, 8], F32, offset=OFF_BK)
    bv_row = A("bv_row", [1, D], BF16, offset=OFF_BVROW)
    bo_row = A("bo_row", [1, D], BF16, offset=OFF_BOROW)
    ones_b = A("ones_b", [1, 128], BF16, offset=OFF_ONESB)
    ones_f = A("ones_f", [1, 128], F32, offset=OFF_ONESF)
    clip_sb = A("clip_sb", [1, 2], F32, offset=OFF_CLIP)
    lnst = A("lnst", [128, 12], F32, offset=OFF_LNST)
    row128 = A("row128", [1, 128], F32, offset=OFF_ROW128)
    rs_g = A("rs_g", [128, IC], F32, offset=OFF_RSG)
    rs_u = A("rs_u", [128, IC], F32, offset=OFF_RSU)
    csg = A("csg", [128, IC], F32, offset=OFF_CSG)
    csu = A("csu", [128, IC], F32, offset=OFF_CSU)
    xt = [A(f"xt{k}", [128, D], F32, offset=OFF_XT + k * 4096) for k in range(2)]
    outt = [A(f"outt{k}", [128, D], F32, offset=OFF_OUTT + k * 4096) for k in range(2)]
    xnbf = [A(f"xnbf{k}", [128, D], BF16, offset=OFF_XNBF + k * 2048) for k in range(2)]
    tevac = [A(f"tevac{k}", [128, 8, 128], BF16, offset=OFF_TEVAC + k * 2048)
             for k in range(2)]
    wo_sb = A("wo_sb", [128, 8, D], BF16, offset=OFF_WO)
    wgT = A("wgT", [128, 8, I], BF16, offset=OFF_W + FO_WG)
    wuT = A("wuT", [128, 8, I], BF16, offset=OFF_W + FO_WU)
    wdT = A("wdT", [128, IC, D], BF16, offset=OFF_W + FO_WD)
    xq = [A(f"xq{k}", [128, 8, 512], BF16, offset=OFF_W + FO_XQ + k * 8192)
          for k in range(2)]
    hev = [A(f"hev{k}", [128, 512], BF16, offset=OFF_W + FO_HEV + k * 1024)
           for k in range(2)]
    silu = [A(f"silu{k}", [128, 512], F32, offset=OFF_W + FO_SILU + k * 2048)
            for k in range(2)]
    hq = [A(f"hq{k}", [128, IC, 128], BF16, offset=OFF_W + FO_HQ + k * 5632)
          for k in range(2)]
    hqfs = [A("hqf0", [128, 1408], F32, offset=OFF_W + FO_HQF),
            A("hqf1", [128, 1408], F32, offset=OFF_W + FO_XQ)]
    qscs = [A("qsc0", [128, 512], F32, offset=OFF_W + FO_HQF),
            A("qsc1", [128, 512], F32, offset=OFF_W + FO_HQF + 2048)]
    p0stg = A("p0stg", [128, I], F32, offset=OFF_W + FO_P0STG)
    awq = A("awq", [128, 8, D], BF16, offset=OFF_W + AO_WQ)
    awk = A("awk", [128, 8, D], BF16, offset=OFF_W + AO_WK)
    awv = A("awv", [128, 8, D], BF16, offset=OFF_W + AO_WV)
    xn2 = [A(f"xn2{k}", [128, 8, 512], BF16, offset=OFF_W + AO_XN2 + k * 8192)
           for k in range(2)]
    kev = [A(f"kev{k}", [128, 512], BF16, offset=OFF_W + AO_KEV + k * 1024)
           for k in range(2)]
    vev = [A(f"vev{k}", [128, D], BF16, offset=OFF_W + AO_VEV + k * 2048)
           for k in range(2)]
    wst = [A(f"wst{k}", [128, D], F32, offset=OFF_W + AO_WST + k * 4096)
           for k in range(2)]
    qtb = [A(f"qtb{k}", [128, 8, 256], BF16, offset=OFF_W + FL_QTB + k * 4096)
           for k in range(2)]
    qtb3 = A("qtb2x", [128, 8, 256], BF16, offset=OFF_W + FL_AOT + 2048)
    ktb = [A(f"ktb{k}", [128, 8, 512], BF16, offset=OFF_W + FL_KTB + k * 8192)
           for k in range(2)]
    vtb = [A(f"vtb{k}", [128, 4, D], BF16, offset=OFF_W + FL_VTB + k * 8192)
           for k in range(2)]
    Ubuf = [A(f"ubuf{k}", [128, S], F32, offset=OFF_W + FL_U + k * 32768)
            for k in range(2)]
    Pbuf = [A(f"pbuf{k}", [128, S], BF16, offset=OFF_W + FL_P + k * 16384)
            for k in range(2)]
    t1s = [A(f"t1s{k}", [128, 512], F32,
             offset=(OFF_W + FL_T1 + k * 2048 if k < 2 else OFF_W + 167936))
           for k in range(3)]
    t2s = [A(f"t2s{k}", [128, 512], F32,
             offset=(OFF_W + FL_T2 + k * 2048 if k < 2 else OFF_W + 169984))
           for k in range(3)]
    rsc = [A(f"rsc{k}", [128, 512], F32,
             offset=(OFF_W + FL_R + k * 2048 if k < 2 else OFF_W + 172032))
           for k in range(3)]
    ptsb = [A(f"ptsb{k}", [128, 8, 128], BF16, offset=OFF_W + FL_PT + k * 2048)
            for k in range(2)]
    ao_sb2 = [A(f"ao_sb{k}", [128, D], BF16,
                offset=OFF_W + FL_AOT + 6144 + k * 2048) for k in range(2)]
    aoT_sb2 = [A(f"aoT_sb{k}", [128, 8, 128], BF16,
                 offset=OFF_W + FL_AO + k * 2048) for k in range(2)]
    lrp = [A(f"lrp{k}", [128, 8], F32, offset=OFF_W + 174080 + k * 32)
           for k in range(2)]

    p0 = nc.alloc_psum_tensor("p0", [128, 512], F32)
    p1 = nc.alloc_psum_tensor("p1", [128, 512], F32)
    pacc0 = nc.alloc_psum_tensor("pacc0", [128, D], F32)
    pacc1 = nc.alloc_psum_tensor("pacc1", [128, D], F32)
    pt0 = nc.alloc_psum_tensor("pt0", [128, 8, 128], BF16)
    pt1 = nc.alloc_psum_tensor("pt1", [128, 8, 128], BF16)
    pts = [pt0, pt1]
    paccs = [pacc0, pacc1]

    B = Builder(nc)
    sl = B.slot

    def xpart_request(acc_slot, ccb, pre_waits, mm_extra, tag):
        B.emit("tensor", lambda e, acc_slot=acc_slot: e.matmul(
            p0[0:1, 0:128], acc_slot[:, 0:1], identF.ap(),
            start=True, stop=True),
            waits=list(pre_waits) + list(mm_extra) + [("p0_ones", 6)],
            inc="cc_bc")
        B.emit("vector", lambda e: e.tensor_reduce(
            sl("ccrow", (1, 8))[0:1, 0:1], p0[0:1, 0:128], AX.X, ALU.max),
            waits=[("cc_bc", B.c("cc_bc"))], inc="cc_rm")
        B.dma("scalar", ccb.ap()[0:1, :], sl("ccrow", (1, 8))[0:1, :], "cc_d",
              waits=[("cc_rm", B.c("cc_rm"))])
        B.emit("gpsimd", lambda e, ccb=ccb: e.collective_compute(
            "AllReduce", ALU.max, replica_groups=[[0, 1, 2, 3, 4, 5, 6, 7]],
            ins=[flat(ccb, 8)], outs=[flat(ccb, 8)]),
            waits=[("cc_d", B.c("cc_d"))], inc="cc_cc")
        B.dma("scalar", sl(tag + "_cr")[0:1, 0:1], ccb.ap()[0:1, 0:1], "cc_d",
              waits=[("cc_cc", B.c("cc_cc"))])

    def xpart_complete(out_slot, tag):
        B.emit("tensor", lambda e, tag=tag: e.matmul(
            p0[:, 0:1], ones_f[0:1, :], sl(tag + "_cr")[0:1, 0:1],
            start=True, stop=True),
            waits=[("cc_d", B.c("cc_d"))], inc="cc_bc")
        B.emit("vector", lambda e, out_slot=out_slot: e.tensor_copy(
            out_slot[:, 0:1], p0[:, 0:1]),
            waits=[("cc_bc", B.c("cc_bc"))], inc="cc_done")
        return "cc_done"

    def xpart_max_allreduce(acc_slot, ccb, out_slot, pre_waits, mm_extra, tag):
        xpart_request(acc_slot, ccb, pre_waits, mm_extra, tag)
        return xpart_complete(out_slot, tag)

    deferred_stores = []

    def emit_setup():
        B.dma("sync", ident.ap(), identD.ap(), "p0_misc")
        B.dma("sync", bq_pp.ap(), bqpD.ap(), "p0_misc")
        B.dma("sync", bk_pp.ap(), bkpD.ap(), "p0_misc")
        B.dma("sync", clip_sb.ap(), clipD.ap(), "p0_misc")
        B.emit("vector", lambda e: e.memset(ones_b.ap(), 1.0), inc="p0_ones")
        B.emit("vector", lambda e: e.memset(ones_f.ap(), 1.0), inc="p0_ones")
        B.emit("vector", lambda e: e.memset(sl("epsb")[:, 0:1], EPS), inc="p0_ones")
        B.emit("vector", lambda e: e.memset(sl("ccrow", (1, 8))[0:1, :], 0.0),
               inc="p0_ones")
        B.emit("vector", lambda e: e.memset(sl("zb")[:, 0:1], 0.0),
               inc="p0_ones")
        B.dma("sync", wst[0][0:1, :], bvrD.ap(), "p0_bv")
        B.emit("vector", lambda e: e.tensor_copy(bv_row.ap(), wst[0][0:1, :]),
               waits=[("p0_bv", 16)], inc="p0_bvc")
        B.dma("sync", wst[1][0:1, :], borD.ap(), "p0_bv",
              waits=[("p0_bvc", 1)])
        B.emit("vector", lambda e: e.tensor_copy(bo_row.ap(), wst[1][0:1, :]),
               waits=[("p0_bv", 32)], inc="p0_bvc")
        B.emit("scalar", lambda e: e.activation(
            out=identF.ap(), in_=ident.ap(), func=AF.Identity,
            scale=1.0, bias=sl("zb")[:, 0:1]),
            waits=[("p0_misc", 64), ("p0_ones", 5)], inc="p0_ones")
        B.emit("scalar", lambda e: e.activation(
            out=sl("sgtmp", (1, 2))[0:1, :], in_=clip_sb[0:1, :], func=AF.Tanh,
            scale=0.5, bias=sl("zb")[0:1, 0:1]),
            waits=[("p0_misc", 64), ("p0_ones", 5)], inc="p0_sg")
        B.emit("vector", lambda e: e.tensor_scalar(
            sl("sgrow", (1, 2))[0:1, :], sl("sgtmp", (1, 2))[0:1, :],
            0.5, 0.5, ALU.mult, ALU.add), waits=[("p0_sg", 1)], inc="p0_sg")
        B.emit("tensor", lambda e: e.matmul(
            p1[:, 0:2], ones_f[0:1, :], sl("sgrow", (1, 2))[0:1, :],
            start=True, stop=True),
            waits=[("p0_sg", 2), ("p0_ones", 2)], inc="p0_sg")
        B.emit("vector", lambda e: e.tensor_copy(
            sl("sigc", (128, 2))[:, :], p1[:, 0:2]),
            waits=[("p0_sg", 3)], inc="p0_sg")
        for j in range(DC):
            par = j % 2
            B.dma2("sync", wst[par].ap(),
                   bass.AP(woTD, j * 128 * D, [[D, 128], [1, D]]), "aw_ld", j,
                   waits=[("aw_c", B.h("aw_c", B.n("aw_c") - 1)),
                          ("p0_bvc", 2)])
            B.emit("vector", lambda e, j=j, par=par: e.tensor_copy(
                wo_sb[:, j, :], wst[par].ap()),
                waits=[B.dtot("aw_ld", j)], inc="aw_c")

        def quant_w(src, rs_tile, dstT, stash, nrt_, fold_back, ncol):
            nchk = ncol // 128
            gpt = (nchk + 7) // 8
            nbuf = 2 if ncol == D else 1
            raws = [A(f"qraw{k}_" + stash.name, [128, ncol], F32,
                      offset=OFF_W + FO_XQ + k * 4096) for k in range(nbuf)]
            stgs = [A(f"qstg{k}_" + stash.name, [128, ncol], F32,
                      offset=OFF_W + FO_P0STG + k * 4096) for k in range(nbuf)]
            wints = [A(f"qwint{k}_" + stash.name, [128, ncol], BF16,
                       offset=OFF_W + FO_P0WI + k * 2048) for k in range(nbuf)]
            n_sc = B.n("q_sc")
            n_cl = B.n("q_cl")
            n_tr = B.n("q_tr")
            n_ev = B.n("q_ev")

            def stage_a(t):
                raw = raws[t % nbuf]
                stg = stgs[t % nbuf]
                p = t % 2
                B.dma2("sync", raw.ap(),
                       bass.AP(src, t * 128 * ncol, [[ncol, 128], [1, ncol]]),
                       "q_ld", t % nbuf,
                       waits=[("q_sc", B.h("q_sc", n_sc + t + 1 - nbuf))])
                B.emit("vector", lambda e, raw=raw, p=p: e.tensor_reduce(
                    sl(f"q_rmax{p}")[:, 0:1], raw.ap(), AX.X, ALU.max,
                    apply_absolute_value=True),
                    waits=[B.dtot("q_ld", t % nbuf)], inc="q_rm")
                B.emit("vector", lambda e, p=p: e.reciprocal(
                    sl(f"q_rinv{p}")[:, 0:1], sl(f"q_rmax{p}")[:, 0:1]),
                    waits=[("q_rm", B.c("q_rm"))], inc="q_rm")
                B.emit("vector", lambda e, p=p: e.tensor_scalar(
                    sl(f"q_r127{p}")[:, 0:1], sl(f"q_rinv{p}")[:, 0:1], 127.0,
                    None, ALU.mult), waits=[("q_rm", B.c("q_rm"))], inc="q_rm")
                if rs_tile is not None:
                    B.emit("vector", lambda e, t=t, rs_tile=rs_tile, p=p:
                           e.tensor_scalar(
                               rs_tile[:, t:t + 1], sl(f"q_rmax{p}")[:, 0:1],
                               1.0 / 127.0, None, ALU.mult), inc="q_rm")
                B.emit("scalar", lambda e, raw=raw, stg=stg, ncol=ncol, p=p:
                       e.activation(
                           out=stg[:, 0:ncol], in_=raw.ap(), func=AF.Identity,
                           scale=sl(f"q_r127{p}")[:, 0:1],
                           bias=sl("zb")[:, 0:1]),
                       waits=[("q_rm", B.c("q_rm")),
                              ("q_cl", B.h("q_cl", n_cl + t + 1 - nbuf))],
                       inc="q_sc")

            def stage_b(t):
                stg = stgs[t % nbuf]
                wint = wints[t % nbuf]
                p = t % 2
                wint_wait = ("q_tr", B.h("q_tr", n_tr + (t + 1 - nbuf) * gpt))
                if fold_back:
                    B.emit("vector", lambda e, stg=stg, ncol=ncol: e.tensor_scalar(
                        stg[:, 0:ncol], stg[:, 0:ncol], RND, RND, ALU.add,
                        ALU.subtract),
                        waits=[("q_sc", B.h("q_sc", n_sc + t + 1))], inc="q_rd")
                    B.emit("vector", lambda e, wint=wint, stg=stg, ncol=ncol, p=p:
                           e.tensor_scalar(
                               wint.ap(), stg[:, 0:ncol],
                               sl(f"q_rmax{p}")[:, 0:1],
                               1.0 / 127.0, ALU.mult, ALU.mult),
                           waits=[("q_rd", B.c("q_rd")), wint_wait], inc="q_cl")
                else:
                    B.emit("vector", lambda e, wint=wint, stg=stg, ncol=ncol:
                           e.tensor_scalar(
                               wint.ap(), stg[:, 0:ncol], RND, RND, ALU.add,
                               ALU.subtract),
                           waits=[("q_sc", B.h("q_sc", n_sc + t + 1)),
                                  wint_wait], inc="q_cl")
                for gi in range(gpt):
                    j0 = gi * 8
                    jn = min(8, nchk - j0)
                    gg = t * gpt + gi
                    par = gg % 2
                    for jj in range(jn):
                        B.emit("tensor",
                               lambda e, par=par, jj=jj, j0=j0, wint=wint:
                               e.transpose(
                                   pts[par][:, jj, :],
                                   wint[:, (j0 + jj) * 128:(j0 + jj + 1) * 128],
                                   ident.ap()),
                               waits=([("q_cl", B.c("q_cl")),
                                       ("q_ev", B.h("q_ev", n_ev + gg - 1))]
                                      if jj == 0 else ()),
                               inc=("q_tr" if jj == jn - 1 else None))
                    B.emit("scalar",
                           lambda e, par=par, j0=j0, jn=jn, t=t, dstT=dstT:
                           e.copy(
                               dstT[:, j0:j0 + jn, t * 128:(t + 1) * 128],
                               pts[par][:, 0:jn, :]),
                           waits=[("q_tr", B.c("q_tr"))], inc="q_ev")

            if nbuf == 2:
                stage_a(0)
                for t in range(nrt_):
                    if t + 1 < nrt_:
                        stage_a(t + 1)
                    stage_b(t)
            else:
                for t in range(nrt_):
                    stage_a(t)
                    stage_b(t)

            def stores(nchk=nchk, stash=stash, dstT=dstT, nrt_=nrt_,
                       ev=B.c("q_ev")):
                for j in range(nchk):
                    B.dma("sync", bass.AP(stash, j * 128 * nrt_ * 128,
                                          [[nrt_ * 128, 128],
                                           [1, nrt_ * 128]]),
                          dstT[:, j, :], "q_st", waits=[("q_ev", ev)])
            deferred_stores.append(stores)

        quant_w(wgD, rs_g, wgT, wgS, IC, False, D)
        quant_w(wuD, rs_u, wuT, wuS, IC, False, D)
        return B.c("q_ev"), quant_w

    class LNPass:

        def __init__(self, track_slot, act_norm=False):
            self.n_rc = B.n("l_rc"); self.n_nm = B.n("l_nm")
            self.n_tr = B.n("l_tr"); self.n_ev = B.n("l_ev")
            self.n_sq = B.n("l_sq")
            self.track = track_slot
            self.act_norm = act_norm
            if track_slot:
                B.emit("vector", lambda e, t=track_slot: e.memset(
                    sl(t)[:, 0:1], 0.0), inc="l_mx")

        def stats(self, i, src_ap, gates):
            p = i % 2
            B.emit("vector", lambda e, src_ap=src_ap, p=p: e.bn_stats(
                sl(f"lnst{p}", (128, 12))[:, 0:6], src_ap[:, 0:512]),
                waits=list(gates) + [("l_sq", B.h("l_sq", self.n_sq + i - 1))])
            B.emit("vector", lambda e, src_ap=src_ap, p=p: e.bn_stats(
                sl(f"lnst{p}", (128, 12))[:, 6:12], src_ap[:, 512:]))
            B.emit("vector", lambda e, p=p: e.bn_aggr(
                sl(f"lnmv{p}", (128, 2))[:, :], sl(f"lnst{p}", (128, 12)).ap()),
                inc="l_st")
            B.emit("scalar", lambda e, p=p: e.activation(
                out=sl(f"rstd{p}")[:, 0:1], in_=sl(f"lnmv{p}", (128, 2))[:, 1:2],
                func=AF.Sqrt, bias=sl("epsb")[:, 0:1]),
                waits=[("l_st", B.c("l_st"))], inc="l_sq")

        def norm(self, i, src_ap):
            par = i % 2
            p = i % 2
            B.emit("vector", lambda e, p=p: e.reciprocal(
                sl(f"rstd2{p}")[:, 0:1], sl(f"rstd{p}")[:, 0:1]),
                waits=[("l_sq", B.h("l_sq", self.n_sq + i + 1))], inc="l_rc")
            if self.act_norm:
                B.emit("vector", lambda e, p=p: e.tensor_scalar(
                    sl(f"lnnb{p}")[:, 0:1], sl(f"lnmv{p}", (128, 2))[:, 0:1],
                    sl(f"rstd2{p}")[:, 0:1], -1.0, ALU.mult, ALU.mult),
                    waits=[("l_rc", B.c("l_rc"))], inc="l_rc")
                B.emit("scalar", lambda e, par=par, src_ap=src_ap, p=p:
                       e.activation(
                           out=xnbf[par].ap(), in_=src_ap, func=AF.Identity,
                           scale=sl(f"rstd2{p}")[:, 0:1],
                           bias=sl(f"lnnb{p}")[:, 0:1]),
                       waits=[("l_rc", B.c("l_rc")),
                              ("l_tr", B.h("l_tr", self.n_tr + i - 1))],
                       inc="l_nm")
            else:
                B.emit("vector", lambda e, par=par, src_ap=src_ap, p=p:
                       e.tensor_scalar(
                           xnbf[par].ap(), src_ap,
                           sl(f"lnmv{p}", (128, 2))[:, 0:1],
                           sl(f"rstd2{p}")[:, 0:1], ALU.subtract, ALU.mult),
                       waits=[("l_rc", B.c("l_rc")),
                              ("l_tr", B.h("l_tr", self.n_tr + i - 1))],
                       inc="l_nm")
            if self.track:
                B.emit("vector", lambda e, par=par: e.tensor_reduce(
                    sl("tmax")[:, 0:1], xnbf[par].ap(), AX.X, ALU.max,
                    apply_absolute_value=True),
                    waits=([("l_nm", B.c("l_nm"))] if self.act_norm else ()))
                B.emit("vector", lambda e, t=self.track: e.tensor_max(
                    sl(t)[:, 0:1], sl(t)[:, 0:1], sl("tmax")[:, 0:1]),
                    inc="l_mx")
            for j in range(DC):
                B.emit("tensor", lambda e, par=par, j=j: e.transpose(
                    pts[par][:, j, :], xnbf[par][:, j * 128:(j + 1) * 128],
                    ident.ap()),
                    waits=([("l_nm", B.c("l_nm")),
                            ("l_ev", B.h("l_ev", self.n_ev + i - 1))]
                           if j == 0 else ()),
                    inc=("l_tr" if j == DC - 1 else None))
            B.emit("scalar", lambda e, par=par: e.copy(
                tevac[par].ap(), pts[par].ap()),
                waits=[("l_tr", B.c("l_tr")), B.dtot("l_xn", i)],
                inc="l_ev")
            B.dma2("sync",
                   bass.AP(xnTD, i * 128, [[T, 128], [128 * T, 8], [1, 128]]),
                   tevac[par].ap(), "l_xn", i, waits=[("l_ev", B.c("l_ev"))])

        def tile(self, i, src_ap, gates):
            self.stats(i, src_ap, gates)
            self.norm(i, src_ap)

        def norm_done(self, i):
            return ("l_nm", B.h("l_nm", self.n_nm + i + 1))

    def emit_ln_dram(src, track_slot, entry_waits=()):
        ln = LNPass(track_slot, act_norm=True)
        n_nm0 = ln.n_nm

        def load(i, w0=()):
            par = i % 2
            w = [("l_nm", B.h("l_nm", n_nm0 + i - 1))] + list(w0)
            B.dma2("sync", xt[par].ap(),
                   bass.AP(src, i * 128 * D, [[D, 128], [1, D]]), "l_ld", i,
                   waits=w)
            ln.stats(i, xt[par].ap(), [B.dtot("l_ld", i)])

        load(0, entry_waits)
        for i in range(nt):
            if i + 1 < nt:
                load(i + 1)
            ln.norm(i, xt[i % 2].ap())

    def emit_scales_x(Mslot):
        B.emit("vector", lambda e, M=Mslot: e.tensor_mul(
            sl("w_sx")[:, 0:1], M[:, 0:1], sl("sigc", (128, 2))[:, 0:1]), inc="sc")
        B.emit("vector", lambda e: e.reciprocal(
            sl("w_si")[:, 0:1], sl("w_sx")[:, 0:1]),
            waits=[("sc", B.c("sc"))], inc="sc")
        B.emit("vector", lambda e: e.tensor_scalar(
            sl("inv_sx")[:, 0:1], sl("w_si")[:, 0:1], 127.0, None, ALU.mult),
            waits=[("sc", B.c("sc"))], inc="sc")
        B.emit("vector", lambda e: e.tensor_scalar(
            sl("w_ss")[:, 0:1], sl("w_sx")[:, 0:1], 1.0 / 127.0, None, ALU.mult),
            waits=[("sc", B.c("sc"))], inc="sc")
        B.emit("vector", lambda e: e.tensor_scalar(
            csg.ap(), rs_g.ap(), sl("w_ss")[:, 0:1], None, ALU.mult),
            waits=[("sc", B.c("sc"))], inc="sc")
        B.emit("vector", lambda e: e.tensor_scalar(
            csu.ap(), rs_u.ap(), sl("w_ss")[:, 0:1], None, ALU.mult),
            waits=[("sc", B.c("sc"))], inc="sc")

    def emit_scales_h(Mslot):
        B.emit("vector", lambda e, M=Mslot: e.tensor_mul(
            sl("w_sd")[:, 0:1], M[:, 0:1], sl("sigc", (128, 2))[:, 1:2]), inc="sc")
        B.emit("vector", lambda e: e.reciprocal(
            sl("w_di")[:, 0:1], sl("w_sd")[:, 0:1]),
            waits=[("sc", B.c("sc"))], inc="sc")
        B.emit("vector", lambda e: e.tensor_scalar(
            sl("inv_sd")[:, 0:1], sl("w_di")[:, 0:1], 127.0, None, ALU.mult),
            waits=[("sc", B.c("sc"))], inc="sc")
        B.emit("vector", lambda e: e.tensor_scalar(
            sl("half_sd")[:, 0:1], sl("w_sd")[:, 0:1], 0.5 / 127.0, None, ALU.mult),
            waits=[("sc", B.c("sc"))], inc="sc")

    def emit_ffn_b(hamax_slot):
        n_q3 = B.n("b_q3"); n_mg = B.n("b_mg")
        n_mu = B.n("b_mu"); n_si = B.n("b_si"); n_h = B.n("b_h")
        n_hv = B.n("b_hv")
        xn_waits = B.dboth("l_xn")
        sc_val = B.c("sc")
        def quant_chunk(g, j):
            par = g % 2
            m = g * DC + j
            qp = m % 2
            B.emit("scalar", lambda e, par=par, j=j, qp=qp: e.activation(
                out=qscs[qp].ap(), in_=xq[par][:, j, :], func=AF.Identity,
                scale=sl("inv_sx")[:, 0:1], bias=sl("zb")[:, 0:1]),
                waits=[B.dtot("b_ld", g), ("sc", sc_val),
                       ("b_q3", B.h("b_q3", n_q3 + m - 1))],
                inc="b_q1")
            B.emit("vector", lambda e, qp=qp: e.tensor_scalar(
                qscs[qp].ap(), qscs[qp].ap(), RND, RND, ALU.add, ALU.subtract),
                waits=[("b_q1", B.c("b_q1"))], inc="b_q2")
            B.emit("vector", lambda e, par=par, j=j, qp=qp: e.tensor_scalar(
                xq[par][:, j, :], qscs[qp].ap(), -128.0, 127.0, ALU.max,
                ALU.min), waits=[("b_q2", B.c("b_q2"))], inc="b_q3")

        qsc_done = B.c("q_sc")
        def load_group(g):
            par = g % 2
            w = [("b_mu", B.h("b_mu", n_mu + (g - 1) * IC))] + xn_waits
            if g == 0:
                w = w + [("q_sc", qsc_done)]
            for hc in range(DC):
                B.dma2("sync", xq[par][:, hc, :],
                       bass.AP(xnTD, hc * 128 * T + g * 512,
                               [[T, 128], [1, 512]]),
                       "b_ld", g, waits=(w if hc == 0 else ()))

        load_group(0)
        for j in range(DC):
            quant_chunk(0, j)
        for g in range(ng):
            par = g % 2
            q3_need = B.c("b_q3")
            if g + 1 < ng:
                load_group(g + 1)
            for ic in range(IC):
                k = g * IC + ic
                pi = ic % 2
                pg = p0.ap() if pi == 0 else pacc0[:, 0:512]
                pu = p1.ap() if pi == 0 else pacc0[:, 512:]
                wgq = [("b_q3", q3_need), ("b_si", B.h("b_si", n_si + k - 1))]
                wuq = [("b_h", B.h("b_h", n_h + k - 1))]
                for hc in range(DC):
                    B.emit("tensor", lambda e, par=par, hc=hc, ic=ic, pg=pg:
                           e.matmul(pg, wgT[:, hc, ic * 128:(ic + 1) * 128],
                                    xq[par][:, hc, :], start=(hc == 0),
                                    stop=(hc == DC - 1)),
                           waits=(wgq if hc == 0 else ()),
                           inc=("b_mg" if hc == DC - 1 else None))
                for hc in range(DC):
                    B.emit("tensor", lambda e, par=par, hc=hc, ic=ic, pu=pu:
                           e.matmul(pu, wuT[:, hc, ic * 128:(ic + 1) * 128],
                                    xq[par][:, hc, :], start=(hc == 0),
                                    stop=(hc == DC - 1)),
                           waits=(wuq if hc == 0 else ()),
                           inc=("b_mu" if hc == DC - 1 else None))
                B.emit("scalar", lambda e, pg=pg, pi=pi, ic=ic: e.activation(
                    out=silu[pi].ap(), in_=pg, func=AF.Silu,
                    scale=csg[:, ic:ic + 1], bias=sl("zb")[:, 0:1]),
                    waits=[("b_mg", B.c("b_mg")), ("b_h", B.h("b_h", n_h + k - 1))],
                    inc="b_si")
                B.emit("vector", lambda e, pu=pu, pi=pi, ic=ic:
                       e.scalar_tensor_tensor(
                           hev[pi].ap(), pu, csu[:, ic:ic + 1], silu[pi].ap(),
                           ALU.mult, ALU.mult),
                       waits=[("b_mu", B.c("b_mu")), ("b_si", B.c("b_si")),
                              B.dtot("b_hd", k),
                              ("b_hv", B.h("b_hv", n_hv + k - 1))], inc="b_h")
                B.emit("vector", lambda e, pi=pi, k=k: e.tensor_reduce(
                    sl("hacc", (128, 176))[:, k:k + 1], hev[pi].ap(), AX.X,
                    ALU.max, apply_absolute_value=True), inc="b_hv")
                B.dma2("sync", bass.AP(hDD, ic * 128 * T + g * 512,
                                       [[T, 128], [1, 512]]),
                       hev[pi].ap(), "b_hd", k, waits=[("b_h", B.c("b_h"))])
                if g + 1 < ng and 6 <= ic < 6 + DC:
                    quant_chunk(g + 1, ic - 6)
        B.emit("vector", lambda e, t=hamax_slot, n=ng * IC: e.tensor_reduce(
            sl(t)[:, 0:1], sl("hacc", (128, 176))[:, 0:n], AX.X, ALU.max),
            inc="b_hm")

    def emit_ffn_c(res_src, xdst, first_mm_waits=(), tile_hook=None,
                   outt_guard=None):
        n_q3 = B.n("c_q3"); n_mm = B.n("c_mm")
        n_ev = B.n("c_ev")
        sc_val = B.c("sc")
        bhd_w = B.dboth("b_hd")

        def tile_front(tt):
            par = tt % 2
            w = [("c_mm", B.h("c_mm", n_mm + tt - 1))]
            if tt == 0:
                w = w + bhd_w
            for half in range(2):
                ic0 = 11 * half
                B.dma2("sync", hq[par][:, ic0:ic0 + 11, :],
                       bass.AP(hDD, ic0 * 128 * T + tt * 128,
                               [[T, 128], [128 * T, 11], [1, 128]]),
                       "c_ld", tt, waits=(w if half == 0 else ()))
            for half in range(2):
                ic0 = 11 * half
                m = 2 * tt + half
                qp = m % 2
                B.emit("scalar", lambda e, par=par, ic0=ic0, qp=qp: e.activation(
                    out=hqfs[qp].ap(), in_=hq[par][:, ic0:ic0 + 11, :],
                    func=AF.Identity, scale=sl("inv_sd")[:, 0:1],
                    bias=sl("zb")[:, 0:1]),
                    waits=[B.dtot("c_ld", tt), ("sc", sc_val),
                           ("c_q3", B.h("c_q3", n_q3 + m - 1))],
                    inc="c_q1")
                B.emit("vector", lambda e, qp=qp: e.tensor_scalar(
                    hqfs[qp].ap(), hqfs[qp].ap(), RND, RND, ALU.add,
                    ALU.subtract),
                    waits=[("c_q1", B.c("c_q1"))], inc="c_q2")
                B.emit("vector", lambda e, par=par, ic0=ic0, qp=qp: e.tensor_scalar(
                    hq[par][:, ic0:ic0 + 11, :], hqfs[qp].ap(), -128.0, 127.0,
                    ALU.max, ALU.min),
                    waits=[("c_q2", B.c("c_q2"))], inc="c_q3")

        tile_front(0)
        for tt in range(nt):
            par = tt % 2
            acc = paccs[par]
            wmm = [("c_q3", B.c("c_q3")), ("c_ev", B.h("c_ev", n_ev + tt - 1))]
            if tt == 0:
                wmm = wmm + list(first_mm_waits)
            for ic in range(IC):
                B.emit("tensor", lambda e, par=par, ic=ic, acc=acc: e.matmul(
                    acc[:, 0:512], hq[par][:, ic, :], wdT[:, ic, 0:512],
                    start=(ic == 0), stop=(ic == IC - 1)),
                    waits=(wmm if ic == 0 else ()))
                B.emit("tensor", lambda e, par=par, ic=ic, acc=acc: e.matmul(
                    acc[:, 512:], hq[par][:, ic, :], wdT[:, ic, 512:],
                    start=(ic == 0), stop=(ic == IC - 1)),
                    inc=("c_mm" if ic == IC - 1 else None))
            B.dma2("sync", xt[par].ap(),
                   bass.AP(res_src, tt * 128 * D, [[D, 128], [1, D]]), "c_xl",
                   tt, waits=[("c_ev", B.h("c_ev", n_ev + tt - 1))])
            if tt + 1 < nt:
                tile_front(tt + 1)
            wev = [("c_mm", B.c("c_mm")), B.dtot("c_xl", tt),
                   B.dtot("c_xo", tt)]
            if outt_guard is not None:
                wev.append(outt_guard(tt))
            B.emit("vector", lambda e, par=par, acc=acc: e.scalar_tensor_tensor(
                outt[par].ap(), acc.ap(), sl("half_sd")[:, 0:1], xt[par].ap(),
                ALU.mult, ALU.add),
                waits=wev, inc="c_ev")
            B.dma2("sync", bass.AP(xdst, tt * 128 * D, [[D, 128], [1, D]]),
                   outt[par].ap(), "c_xo", tt, waits=[("c_ev", B.c("c_ev"))])
            if tile_hook is not None:
                tile_hook(tt)

    def emit_attn_wload(entry_waits):
        first = True
        for wsrc, dst in ((wqTD, awq), (wkTD, awk), (wvTD, awv)):
            for j in range(DC):
                idx = B.n("aw_c")
                par = idx % 2
                w = [("aw_c", B.h("aw_c", idx - 1))]
                if first:
                    w = w + list(entry_waits)
                    first = False
                B.dma2("sync", wst[par].ap(),
                       bass.AP(wsrc, j * 128 * D, [[D, 128], [1, D]]), "aw_ld",
                       idx, waits=w)
                B.emit("vector", lambda e, dst=dst, j=j, par=par: e.tensor_copy(
                    dst[:, j, :], wst[par].ap()),
                    waits=[B.dtot("aw_ld", idx)], inc="aw_c")

    def emit_proj():
        xn_waits = B.dboth("l_xn")
        aw_val = B.c("aw_c")

        def one_pass(pre, wsb, bias_pp, dstD, s_mm, s_ev, s_dma, prev_mm=None,
                     prev_dma=None):
            n_mm = B.n(s_mm); n_ev = B.n(s_ev)

            def g_load(g):
                par = g % 2
                w = [(s_mm, B.h(s_mm, n_mm + (g - 1) * DC)),
                     ("aw_c", aw_val)] + xn_waits
                if g < 2 and prev_mm is not None:
                    w.append(prev_mm)
                for hc in range(DC):
                    B.dma2("sync", xn2[par][:, hc, :],
                           bass.AP(xnTD, hc * 128 * T + g * 512,
                                   [[T, 128], [1, 512]]),
                           "pj_ld", g, waits=(w if hc == 0 else ()))

            g_load(0)
            for g in range(ng):
                par = g % 2
                for m in range(DC):
                    gm = g * DC + m
                    pp = p0 if m % 2 == 0 else p1
                    wmm = [B.dtot("pj_ld", g),
                           (s_ev, B.h(s_ev, n_ev + gm - 1))]
                    for hc in range(DC):
                        B.emit("tensor",
                               lambda e, par=par, hc=hc, m=m, pp=pp, wsb=wsb:
                               e.matmul(pp.ap(),
                                        wsb[:, hc, m * 128:(m + 1) * 128],
                                        xn2[par][:, hc, :],
                                        start=(hc == 0), stop=(hc == DC - 1)),
                               waits=(wmm if hc == 0 else ()),
                               inc=(s_mm if hc == DC - 1 else None))
                    wev = [(s_mm, B.c(s_mm)), B.dtot(s_dma, gm)]
                    if gm < 2 and prev_dma is not None:
                        wev += prev_dma
                    B.emit("vector",
                           lambda e, m=m, pp=pp, bias_pp=bias_pp: e.tensor_scalar(
                               kev[m % 2].ap(), pp.ap(), bias_pp[:, m:m + 1],
                               None, ALU.add),
                           waits=wev, inc=s_ev)
                    B.dma2("sync", bass.AP(dstD, m * 128 * T + g * 512,
                                           [[T, 128], [1, 512]]),
                           kev[m % 2].ap(), s_dma, gm, waits=[(s_ev, B.c(s_ev))])
                    if m == 0 and g + 1 < ng:
                        g_load(g + 1)

        one_pass("pk", awk, bk_pp, ktoD, "pk_mm", "pk_ev", "pk_dma")
        B.emit("gpsimd", lambda e: e.collective_compute(
            "AllGather", ALU.bypass,
            replica_groups=[[0, 1], [2, 3], [4, 5], [6, 7]],
            ins=[flat(ktoD, D * T)], outs=[flat(ktgD, 2 * D * T)]),
            waits=B.dboth("pk_dma"), inc="ag_kt")

        n_mm = B.n("pv_mm"); n_ev = B.n("pv_ev")
        pk_mm_done = ("pk_mm", B.c("pk_mm"))

        def v_load(g):
            par = g % 2
            w = [("pv_mm", B.h("pv_mm", n_mm + (g - 1) * 4)),
                 ("aw_c", aw_val)] + xn_waits
            if g < 2:
                w.append(pk_mm_done)
            for hc in range(DC):
                B.dma2("sync", xn2[par][:, hc, :],
                       bass.AP(xnTD, hc * 128 * T + g * 512,
                               [[T, 128], [1, 512]]),
                       "pj_ld", g, waits=(w if hc == 0 else ()))

        v_load(0)
        for g in range(ng):
            par = g % 2
            for sub in range(4):
                gs = g * 4 + sub
                acc = paccs[sub % 2]
                wmm = [B.dtot("pj_ld", g),
                       ("pv_ev", B.h("pv_ev", n_ev + gs - 1))]
                for hc in range(DC):
                    B.emit("tensor", lambda e, par=par, hc=hc, sub=sub, acc=acc:
                           e.matmul(acc[:, 0:512],
                                    xn2[par][:, hc, sub * 128:(sub + 1) * 128],
                                    awv[:, hc, 0:512], start=(hc == 0), stop=False),
                           waits=(wmm if hc == 0 else ()))
                    B.emit("tensor", lambda e, par=par, hc=hc, sub=sub, acc=acc:
                           e.matmul(acc[:, 512:],
                                    xn2[par][:, hc, sub * 128:(sub + 1) * 128],
                                    awv[:, hc, 512:], start=(hc == 0), stop=False))
                B.emit("tensor", lambda e, acc=acc: e.matmul(
                    acc[:, 0:512], ones_b[0:1, :], bv_row[0:1, 0:512],
                    start=False, stop=True))
                B.emit("tensor", lambda e, acc=acc: e.matmul(
                    acc[:, 512:], ones_b[0:1, :], bv_row[0:1, 512:],
                    start=False, stop=True), inc="pv_mm")
                B.emit("vector", lambda e, sub=sub, acc=acc: e.tensor_copy(
                    vev[sub % 2].ap(), acc.ap()),
                    waits=[("pv_mm", B.c("pv_mm")), B.dtot("pv_dma", gs)],
                    inc="pv_ev")
                B.dma2("sync", bass.AP(voD, (g * 512 + sub * 128) * D,
                                       [[D, 128], [1, D]]),
                       vev[sub % 2].ap(), "pv_dma", gs,
                       waits=[("pv_ev", B.c("pv_ev"))])
                if sub == 0 and g + 1 < ng:
                    v_load(g + 1)
        B.emit("gpsimd", lambda e: e.collective_compute(
            "AllGather", ALU.bypass,
            replica_groups=[[0, 1], [2, 3], [4, 5], [6, 7]],
            ins=[flat(voD, T * D)], outs=[flat(vgD, 2 * T * D)]),
            waits=B.dboth("pv_dma"), inc="ag_v")

        one_pass("pq", awq, bq_pp, qtD, "pq_mm", "pq_ev", "pq_dma",
                 prev_mm=("pv_mm", B.c("pv_mm")), prev_dma=B.dboth("pk_dma"))

    def emit_flash(res_src, xdst, sb_hook=None, outt_guard=None):
        entry = B.dboth("pq_dma") + [("ag_kt", B.c("ag_kt")),
                                     ("ag_v", B.c("ag_v"))]
        for e_ in ENGINES:
            B.wait_only(e_, entry)
        n_sm = B.n("f_sm"); n_ac = B.n("f_ac"); n_u = B.n("f_u")
        n_ex = B.n("f_ex"); n_ptr = B.n("f_ptr"); n_pev = B.n("f_pev")
        n_pv = B.n("f_pv"); n_ao = B.n("f_ao"); n_aoe = B.n("f_aoe")
        n_aot = B.n("f_aot"); n_oe = B.n("f_oe"); n_om = B.n("f_om")
        qtbs = [qtb[0], qtb[1], qtb3]
        QB = min(4, nkc)
        nseg = nkc // QB
        hk = nkc // 2
        npc = max(1, hk // 2)
        pwk = hk // npc
        bst = {"rr": 0, "lu": {}, "sb_pacc": -1}

        def scores_dma(sb, kc):
            sbpar = sb % 3
            if kc == 0:
                for hc in range(DC):
                    B.dma2("sync", qtbs[sbpar][:, hc, :],
                           bass.AP(qtD, hc * 128 * T + sb * 256,
                                   [[T, 128], [1, 256]]),
                           "f_qt", sb,
                           waits=([("f_sm",
                                    B.h("f_sm", n_sm + (sb - 2) * nkc * 2))]
                                  if hc == 0 else ()))
            kpar = kc % 2
            gi = sb * nkc + kc
            rank = 0 if kc * 512 < T else 1
            base = rank * D * T + (kc * 512 - rank * T)
            B.dma2("sync", ktb[kpar][:, :, :],
                   bass.AP(ktgD, base, [[T, 128], [128 * T, 8], [1, 512]]),
                   "f_kt", gi,
                   waits=[("f_sm", B.h("f_sm", n_sm + (gi - 1) * 2))])

        def scores_kc(sb, kc, dma_done=False):
            sbpar = sb % 3
            if not dma_done:
                scores_dma(sb, kc)
            kpar = kc % 2
            gi = sb * nkc + kc
            for qt2 in (0, 1):
                ci = gi * 2 + qt2
                use4 = (sb > 0 and kc >= 5)
                if use4:
                    if kc == 5 and qt2 == 0:
                        bst["rr"] = 0
                    bi = bst["rr"] % 4
                    bst["rr"] += 1
                else:
                    bi = ci % 2
                psap = (p0.ap(), p1.ap(), pacc0[:, 0:512],
                        pacc0[:, 512:])[bi]
                tpar = ci % 3
                wmm = [B.dtot("f_kt", gi), B.dtot("f_qt", sb)]
                lu = bst["lu"].get(bi)
                if lu is not None:
                    wmm.append(("f_ac", B.h("f_ac", n_ac + lu + 1)))
                if bi >= 2 and bst["sb_pacc"] != sb:
                    bst["sb_pacc"] = sb
                    wmm.append(("f_oe",
                                B.h("f_oe", n_oe + 2 * (sb - 1) + 2)))
                bst["lu"][bi] = ci
                for hc in range(DC):
                    B.emit("tensor", lambda e, sbpar=sbpar, kpar=kpar, hc=hc,
                           qt2=qt2, psap=psap: e.matmul(
                               psap,
                               qtbs[sbpar][:, hc, qt2 * 128:(qt2 + 1) * 128],
                               ktb[kpar][:, hc, :],
                               start=(hc == 0), stop=(hc == DC - 1)),
                           waits=(wmm if hc == 0 else ()),
                           inc=("f_sm" if hc == DC - 1 else None))
                B.emit("scalar", lambda e, psap=psap, tpar=tpar: e.activation(
                    out=t1s[tpar].ap(), in_=psap, func=AF.Tanh, scale=0.5,
                    bias=sl("zb")[:, 0:1]),
                    waits=[("f_sm", B.c("f_sm")),
                           ("f_u", B.h("f_u", n_u + ci - 2))])
                B.emit("scalar", lambda e, psap=psap, tpar=tpar: e.activation(
                    out=t2s[tpar].ap(), in_=psap, func=AF.Tanh,
                    bias=sl("zb")[:, 0:1]))
                B.emit("scalar", lambda e, psap=psap, tpar=tpar: e.activation(
                    out=rsc[tpar].ap(), in_=psap, func=AF.Relu,
                    bias=sl("zb")[:, 0:1]), inc="f_ac")
                B.emit("vector", lambda e, tpar=tpar: e.scalar_tensor_tensor(
                    t1s[tpar].ap(), t2s[tpar].ap(), 1.5, t1s[tpar].ap(),
                    ALU.mult, ALU.add),
                    waits=[("f_ac", B.c("f_ac"))])
                B.emit("vector", lambda e, qt2=qt2, kc=kc, tpar=tpar:
                       e.tensor_add(
                           Ubuf[qt2][:, kc * 512:(kc + 1) * 512],
                           t1s[tpar].ap(), rsc[tpar].ap()),
                       waits=[("f_ex",
                               B.h("f_ex", n_ex + 4 * npc * (sb - 1)
                                   + (kc // hk) * 2 * npc
                                   + 2 * ((kc % hk) // pwk) + qt2 + 1))],
                       inc="f_u")
                if (kc + 1) % QB == 0:
                    seg = (kc + 1) // QB - 1
                    B.emit("vector", lambda e, qt2=qt2, kc=kc, seg=seg:
                           e.tensor_reduce(
                               sl(f"mp{qt2}", (128, 16))[:, seg:seg + 1],
                               Ubuf[qt2][:, (kc + 1 - QB) * 512:
                                         (kc + 1) * 512],
                               AX.X, ALU.max))

        def exp_part(sb):
            for qt2 in (0, 1):
                B.emit("vector", lambda e, qt2=qt2: e.tensor_reduce(
                    sl(f"mrow{qt2}")[:, 0:1],
                    sl(f"mp{qt2}", (128, 16))[:, 0:nseg], AX.X, ALU.max),
                    waits=[("f_u",
                            B.h("f_u", n_u + 2 * nkc * sb + 2 * nkc - 1 + qt2))])
                B.emit("vector", lambda e, qt2=qt2: e.tensor_scalar(
                    sl(f"negm{qt2}")[:, 0:1], sl(f"mrow{qt2}")[:, 0:1], -0.1,
                    None, ALU.mult), inc="f_mxr")
            first = True
            for half in (0, 1):
                for p in range(npc):
                    for qt2 in (0, 1):
                        c0 = half * (S // 2) + p * pwk * 512
                        c1 = c0 + pwk * 512
                        idx = half * npc + p
                        vlast = half * hk + (p + 1) * pwk - 1
                        w = [("f_ptr", B.h("f_ptr", n_ptr
                                           + 2 * nkc * (sb - 1)
                                           + 2 * vlast + qt2 + 1))]
                        if first:
                            w.append(("f_mxr", B.c("f_mxr")))
                            first = False
                        B.emit("scalar", lambda e, qt2=qt2, c0=c0, c1=c1,
                               idx=idx: e.activation(
                                   out=Pbuf[qt2][:, c0:c1],
                                   in_=Ubuf[qt2][:, c0:c1],
                                   func=AF.Exp, bias=sl(f"negm{qt2}")[:, 0:1],
                                   scale=0.1,
                                   accum_out=lrp[qt2][:, idx:idx + 1]),
                               waits=w, inc="f_ex")

        def pv_part(sb):
            for vc in range(nkc):
                vpar = vc % 2
                B.dma2("sync", vtb[vpar][:, :, :],
                       bass.AP(vgD, vc * 512 * D,
                               [[D, 128], [128 * D, 4], [1, D]]),
                       "f_vt", sb * nkc + vc,
                       waits=[("f_pv",
                               B.h("f_pv", n_pv + 2 * nkc * sb + 2 * vc - 2))])
                for qt2 in (0, 1):
                    wtr = [("f_ex",
                            B.h("f_ex", n_ex + 4 * npc * sb
                                + (vc // hk) * 2 * npc
                                + 2 * ((vc % hk) // pwk) + qt2 + 1)),
                           ("f_pev",
                            B.h("f_pev", n_pev + 2 * nkc * sb + 2 * vc + qt2 - 1))]
                    if vc == 0:
                        wtr.append(("f_aoe", B.h("f_aoe", n_aoe + 2 * (sb - 1)
                                                 + qt2 + 1)))
                    for i4 in range(4):
                        B.emit("tensor", lambda e, qt2=qt2, vc=vc, i4=i4:
                               e.transpose(
                                   pts[qt2][:, i4, :],
                                   Pbuf[qt2][:, (vc * 4 + i4) * 128:
                                             (vc * 4 + i4 + 1) * 128],
                                   ident.ap()),
                               waits=(wtr if i4 == 0 else ()),
                               inc=("f_ptr" if i4 == 3 else None))
                for qt2 in (0, 1):
                    B.emit("vector", lambda e, qt2=qt2: e.tensor_copy(
                        ptsb[qt2][:, 0:4, :], pts[qt2][:, 0:4, :]),
                        waits=[("f_ptr", B.h("f_ptr",
                                             n_ptr + 2 * nkc * sb + 2 * vc
                                             + qt2 + 1)),
                               ("f_pv",
                                B.h("f_pv", n_pv + 2 * nkc * sb + 2 * (vc - 1)
                                    + qt2 + 1))], inc="f_pev")
                for qt2 in (0, 1):
                    acc = paccs[qt2]
                    wpv = [("f_pev", B.h("f_pev",
                                         n_pev + 2 * nkc * sb + 2 * vc
                                         + qt2 + 1)),
                           B.dtot("f_vt", sb * nkc + vc)]
                    if vc == 0:
                        wpv.append(("f_oe", B.h("f_oe", n_oe + 2 * sb + qt2 - 1)))
                    for half in (0, 1):
                        for i4 in range(4):
                            B.emit("tensor", lambda e, qt2=qt2, vc=vc, i4=i4,
                                   vpar=vpar, acc=acc, half=half: e.matmul(
                                       acc[:, half * 512:(half + 1) * 512],
                                       ptsb[qt2][:, i4, :],
                                       vtb[vpar][:, i4,
                                                 half * 512:(half + 1) * 512],
                                       start=(vc == 0 and i4 == 0),
                                       stop=(vc == nkc - 1 and i4 == 3)),
                                   waits=(wpv if (half == 0 and i4 == 0)
                                          else ()),
                                   inc=("f_pv" if (half == 1 and i4 == 3)
                                        else None))

        def epi_e0(sb):
            for qt2 in (0, 1):
                B.emit("vector", lambda e, qt2=qt2: e.tensor_reduce(
                    sl(f"lrow{qt2}")[:, 0:1], lrp[qt2][:, 0:2 * npc],
                    AX.X, ALU.add), waits=[("f_ex", B.c("f_ex"))])
                B.emit("vector", lambda e, qt2=qt2: e.reciprocal(
                    sl(f"invl{qt2}")[:, 0:1], sl(f"lrow{qt2}")[:, 0:1]),
                    inc="f_iv")
            for qt2 in (0, 1):
                acc = paccs[qt2]
                B.emit("vector", lambda e, qt2=qt2, acc=acc: e.tensor_scalar(
                    ao_sb2[qt2].ap(), acc.ap(), sl(f"invl{qt2}")[:, 0:1], None,
                    ALU.mult),
                    waits=[("f_pv", B.h("f_pv", n_pv + 2 * nkc * sb
                                        + 2 * (nkc - 1) + qt2 + 1)),
                           ("f_iv", B.c("f_iv")),
                           ("f_aot", B.h("f_aot", n_aot + 2 * sb + qt2 - 1))],
                    inc="f_ao")
            for qt2 in (0, 1):
                row = sb * 256 + qt2 * 128
                B.dma2("sync", xt[qt2].ap(),
                       bass.AP(res_src, row * D, [[D, 128], [1, D]]), "f_xl",
                       qt2,
                       waits=[("f_oe", B.h("f_oe", n_oe + 2 * sb + qt2 - 1))])

        def epi_aot(sb, qt2):
            for j in range(DC):
                B.emit("tensor", lambda e, qt2=qt2, j=j: e.transpose(
                    pts[qt2][:, j, :], ao_sb2[qt2][:, j * 128:(j + 1) * 128],
                    ident.ap()),
                    waits=([("f_ao", B.h("f_ao", n_ao + 2 * sb + qt2 + 1))]
                           if j == 0 else ()),
                    inc=("f_aot" if j == DC - 1 else None))

        def epi_aoe(sb, qt2):
            B.emit("vector", lambda e, qt2=qt2: e.tensor_copy(
                aoT_sb2[qt2].ap(), pts[qt2].ap()),
                waits=[("f_aot", B.h("f_aot", n_aot + 2 * sb + qt2 + 1)),
                       ("f_om", B.h("f_om", n_om + 2 * (sb - 1) + qt2 + 1))],
                inc="f_aoe")

        def epi_omm(sb, qt2):
            acc = paccs[qt2]
            womm = [("f_aoe", B.h("f_aoe", n_aoe + 2 * sb + qt2 + 1)),
                    ("f_ao", B.h("f_ao", n_ao + 2 * sb + 2))]
            for hc in range(DC):
                B.emit("tensor", lambda e, hc=hc, acc=acc, qt2=qt2: e.matmul(
                    acc[:, 0:512], aoT_sb2[qt2][:, hc, :],
                    wo_sb[:, hc, 0:512],
                    start=(hc == 0), stop=False),
                    waits=(womm if hc == 0 else ()))
                B.emit("tensor", lambda e, hc=hc, acc=acc, qt2=qt2: e.matmul(
                    acc[:, 512:], aoT_sb2[qt2][:, hc, :],
                    wo_sb[:, hc, 512:],
                    start=(hc == 0), stop=False))
            B.emit("tensor", lambda e, acc=acc: e.matmul(
                acc[:, 0:512], ones_b[0:1, :], bo_row[0:1, 0:512],
                start=False, stop=True))
            B.emit("tensor", lambda e, acc=acc: e.matmul(
                acc[:, 512:], ones_b[0:1, :], bo_row[0:1, 512:],
                start=False, stop=True), inc="f_om")

        def epi_out(sb, qt2):
            acc = paccs[qt2]
            row = sb * 256 + qt2 * 128
            wev = [("f_om", B.h("f_om", n_om + 2 * sb + qt2 + 1)),
                   B.dtot("f_xl", qt2), B.dtot("f_xo", qt2)]
            if outt_guard is not None:
                g_ = outt_guard(sb, qt2)
                if g_ is not None:
                    wev.append(g_)
            B.emit("vector", lambda e, qt2=qt2, acc=acc: e.tensor_add(
                outt[qt2][:, 0:512], acc[:, 0:512], xt[qt2][:, 0:512]),
                waits=wev)
            B.emit("vector", lambda e, qt2=qt2, acc=acc: e.tensor_add(
                outt[qt2][:, 512:], acc[:, 512:], xt[qt2][:, 512:]),
                inc="f_oe")
            B.dma2("sync", bass.AP(xdst, row * D, [[D, 128], [1, D]]),
                   outt[qt2].ap(), "f_xo", qt2,
                   waits=[("f_oe", B.c("f_oe"))])

        for kc in range(nkc):
            scores_kc(0, kc)
        for sb in range(nsb):
            exp_part(sb)
            if sb + 1 < nsb:
                scores_dma(sb + 1, 0)
                scores_dma(sb + 1, 1)
            pv_part(sb)
            stages = [lambda sb=sb: epi_e0(sb),
                      lambda sb=sb: epi_aot(sb, 0),
                      lambda sb=sb: (epi_aoe(sb, 0), epi_aot(sb, 1)),
                      lambda sb=sb: (epi_aoe(sb, 1), epi_omm(sb, 0)),
                      lambda sb=sb: (epi_omm(sb, 1), epi_out(sb, 0)),
                      lambda sb=sb: (epi_out(sb, 1),
                                     sb_hook(sb) if sb_hook else None)]
            if sb + 1 < nsb:
                stages[0]()
                done = 1
                for kc in range(nkc):
                    scores_kc(sb + 1, kc, dma_done=(kc < 2))
                    if done < len(stages):
                        stages[done]()
                        done += 1
                while done < len(stages):
                    stages[done]()
                    done += 1
            else:
                for f_ in stages:
                    f_()

    def emit_wreload(entry_waits):
        first = True
        for j in range(DC):
            w = (list(entry_waits) + [("q_st", B.c("q_st"))]) if first else ()
            first = False
            B.dma("sync", wgT[:, j, :],
                  bass.AP(wgS, j * 128 * I, [[I, 128], [1, I]]), "wr_ld", waits=w)
        for j in range(DC):
            B.dma("sync", wuT[:, j, :],
                  bass.AP(wuS, j * 128 * I, [[I, 128], [1, I]]), "wr_ld")
        for ic in range(IC):
            B.dma("sync", wdT[:, ic, :],
                  bass.AP(wdS, ic * 128 * D, [[D, 128], [1, D]]), "wr_ld")

    gu_ev, quant_w_fn = emit_setup()
    emit_ln_dram(xin, "amax1", entry_waits=[("p0_misc", 64)])
    xpart_request(sl("amax1"), ccB[0], [("l_mx", B.c("l_mx"))], [], "c1")
    quant_w_fn(wdD, None, wdT, wdS, DC, True, I)
    xpart_complete(sl("M1"), "c1")
    emit_scales_x(sl("M1"))
    for f_ in deferred_stores:
        f_()
    B.wait_only("tensor", [("q_ev", gu_ev), ("sc", B.c("sc"))])
    emit_ffn_b("hamax1")
    xpart_max_allreduce(sl("hamax1"), ccB[1], sl("M2"),
                        [("b_hm", B.c("b_hm"))] + B.dboth("b_hd"), [], "c2")
    emit_scales_h(sl("M2"))
    b1_h = B.c("b_h")
    ln2 = LNPass(None)
    emit_ffn_c(xin, x1D,
               first_mm_waits=[("b_h", b1_h), ("q_ev", B.c("q_ev"))],
               tile_hook=lambda tt: ln2.tile(
                   tt, outt[tt % 2].ap(), [("c_ev", B.c("c_ev"))]),
               outt_guard=lambda tt: ln2.norm_done(tt - 2) if tt >= 2 else None)
    emit_attn_wload([("c_mm", B.c("c_mm")), ("b_mu", B.c("b_mu")),
                     ("q_st", B.c("q_st"))] + B.dboth("c_xo"))
    emit_proj()
    ln3 = LNPass("amax3")

    def ln3_hook(sb):
        g = [("f_oe", B.c("f_oe"))]
        ln3.stats(2 * sb, outt[0].ap(), g)
        ln3.stats(2 * sb + 1, outt[1].ap(), g)
        ln3.norm(2 * sb, outt[0].ap())
        ln3.norm(2 * sb + 1, outt[1].ap())

    emit_flash(x1D, x2D, sb_hook=ln3_hook,
               outt_guard=lambda sb, qt2: (ln3.norm_done(2 * (sb - 1) + qt2)
                                           if sb >= 1 else None))
    emit_wreload(B.dboth("f_xo"))
    xpart_max_allreduce(sl("amax3"), ccB[2], sl("M1"),
                        [("l_mx", B.c("l_mx"))], [("f_oe", B.c("f_oe"))], "c3")
    emit_scales_x(sl("M1"))
    B.wait_only("tensor", [("wr_ld", B.c("wr_ld")), ("sc", B.c("sc"))])
    emit_ffn_b("hamax3")
    xpart_max_allreduce(sl("hamax3"), ccB[3], sl("M2"),
                        [("b_hm", B.c("b_hm"))] + B.dboth("b_hd"), [], "c4")
    emit_scales_h(sl("M2"))
    b3_h = B.c("b_h")
    emit_ffn_c(x2D, xout, first_mm_waits=[("b_h", b3_h)])
    B.wait_only("sync", B.dboth("c_xo"))
    B.wait_only("gpsimd", B.dboth("c_xo"))

    with nc.Block() as block:
        @block.sync
        def _(eng):
            for f in B.q["sync"]:
                f(eng)

        @block.scalar
        def _(eng):
            for f in B.q["scalar"]:
                f(eng)

        @block.vector
        def _(eng):
            for f in B.q["vector"]:
                f(eng)

        @block.tensor
        def _(eng):
            for f in B.q["tensor"]:
                f(eng)

        @block.gpsimd
        def _(eng):
            for f in B.q["gpsimd"]:
                f(eng)

    return nc


def make_in_maps(x_full, Wq, bq, Wk, bk, Wv, bv, Wo, bo, clip_x, clip_down,
                 w_gate, w_up, w_down, T=4096):
    f32 = np.float32
    ident = np.eye(128, dtype=ml_dtypes.bfloat16)
    com = {
        "wqT": np.ascontiguousarray(np.asarray(Wq, f32).T),
        "wkT": np.ascontiguousarray(np.asarray(Wk, f32).T),
        "wvT": np.ascontiguousarray(np.asarray(Wv, f32).T),
        "woT": np.ascontiguousarray(np.asarray(Wo, f32).T),
        "bqp": np.ascontiguousarray(np.asarray(bq, f32).reshape(8, 128).T),
        "bkp": np.ascontiguousarray(np.asarray(bk, f32).reshape(8, 128).T),
        "bvr": np.asarray(bv, f32).reshape(1, D),
        "bor": np.asarray(bo, f32).reshape(1, D),
        "wg": np.asarray(w_gate, f32),
        "wu": np.asarray(w_up, f32),
        "wd": np.asarray(w_down, f32),
        "clip": np.array([[np.asarray(clip_x).reshape(-1)[0],
                           np.asarray(clip_down).reshape(-1)[0]]], f32),
        "ident": ident,
    }
    in_maps = []
    for c in range(8):
        b, hh = c // 2, c % 2
        shard = np.ascontiguousarray(x_full[b, hh * T:(hh + 1) * T, :], dtype=f32)
        m = dict(com)
        m["xin"] = shard
        in_maps.append(m)
    return in_maps


def kernel(L0_attn_x, L0_attn_Wq, L0_attn_bq, L0_attn_Wk, L0_attn_bk,
           L0_attn_Wv, L0_attn_bv, L0_attn_Wo, L0_attn_bo,
           L0_ffn_clip_x, L0_ffn_clip_down, L0_ffn_w_gate, L0_ffn_w_up,
           L0_ffn_w_down):
    T = 4096
    x = np.asarray(L0_attn_x, np.float32)
    in_maps = make_in_maps(x, L0_attn_Wq, L0_attn_bq, L0_attn_Wk, L0_attn_bk,
                           L0_attn_Wv, L0_attn_bv, L0_attn_Wo, L0_attn_bo,
                           L0_ffn_clip_x, L0_ffn_clip_down, L0_ffn_w_gate,
                           L0_ffn_w_up, L0_ffn_w_down, T=T)
    nc = build_program(T)
    res = run_bass_kernel_spmd(nc, in_maps, core_ids=list(range(8)))
    out = np.empty((4, 2 * T, D), np.float32)
    for c in range(8):
        b, hh = c // 2, c % 2
        out[b, hh * T:(hh + 1) * T, :] = res.results[c]["xout"]
    return out

